# version 04
# baseline (speedup 1.0000x reference)
import collections
import contextlib

import numpy as np

import concourse.bass as bass
import concourse.mybir as mybir
from concourse.bass_utils import run_bass_kernel_spmd

F32 = mybir.dt.float32
BF16 = mybir.dt.bfloat16
AF = mybir.ActivationFunctionType
ALU = mybir.AluOpType
AX = mybir.AxisListType

E = 2048
H = 16
DH = 128
QKVF = 6144
NI = 8192
T = 2048
NTT = 16
NTC = 4
SEQ = 8192
NKB = 64
NCH = 16
NRANK = 4

MAGIC = float(np.float32(1.5 * 2 ** 23))
_SC = 1.0 / float(np.float32(np.sqrt(128.0)))
SC_HI = float(np.float32(_SC))
SC_LO = float(np.float32(_SC - SC_HI))
FWC = float(np.float32(1.0 / np.sqrt(8192.0)))
EPS = 1e-5
INV2048 = 1.0 / 2048.0
P2M7 = 2.0 ** -7
P2M24 = 2.0 ** -24

ENGS = ("sync", "gpsimd", "vector", "scalar", "tensor")


class Prog:
    def __init__(self, nc, es):
        self.nc = nc
        self.es = es
        self.ops = {e: [] for e in ENGS}
        self.cnt = collections.Counter()
        self.sems = {}
        self.bar_k = 0

    def sem(self, name):
        if name not in self.sems:
            self.sems[name] = self.es.enter_context(self.nc.semaphore(name))
        return self.sems[name]

    def val(self, name):
        return self.cnt[name]

    def emit(self, eng, fn):
        self.ops[eng].append(fn)

    def wait(self, eng, name, v):
        if v is None or v <= 0:
            return
        s = self.sem(name)
        self.emit(eng, lambda e, s=s, v=v: e.wait_ge(s, v))

    def ins(self, eng, meth, *args, inc=None, **kw):
        if inc is None:
            self.emit(eng, lambda e, m=meth, a=args, k=kw: getattr(e, m)(*a, **k))
            return None
        name, n = inc
        s = self.sem(name)
        self.cnt[name] += n
        v = self.cnt[name]
        self.emit(eng, lambda e, m=meth, a=args, k=kw, s=s, n=n: getattr(e, m)(*a, **k).then_inc(s, n))
        return v

    def bump(self, eng, name, n=1):
        s = self.sem(name)
        self.cnt[name] += n
        v = self.cnt[name]
        self.emit(eng, lambda e, s=s, n=n: e.sem_inc(s, n))
        return v

    def barrier(self):
        self.bar_k += 1
        for name, v in list(self.cnt.items()):
            if name.startswith("s") and name not in ("sc", "so0", "so1"):
                self.wait("sync", name, v)
            elif name.startswith("g") and name != "gp":
                self.wait("gpsimd", name, v)
            elif name in ("so0", "so1"):
                self.wait("gpsimd", name, v)
        for eng in ENGS:
            self.bump(eng, "bar", 1)
        tgt = 5 * self.bar_k
        for eng in ENGS:
            self.wait(eng, "bar", tgt)


def build(debug=False):
    nc = bass.Bass("TRN2", target_bir_lowering=False)
    sck = "ExternalOutput" if debug else "Internal"

    x_d = nc.dram_tensor("x", [T, E], F32, kind="ExternalInput").ap()
    wqkv_d = nc.dram_tensor("wqkv", [QKVF, E], F32, kind="ExternalInput").ap()
    wout_d = nc.dram_tensor("wout", [E, E], F32, kind="ExternalInput").ap()
    bout_d = nc.dram_tensor("bout", [E], F32, kind="ExternalInput").ap()
    wg_d = nc.dram_tensor("wg", [NI, E], F32, kind="ExternalInput").ap()
    wu_d = nc.dram_tensor("wu", [NI, E], F32, kind="ExternalInput").ap()
    wd_d = nc.dram_tensor("wd", [E, NI], F32, kind="ExternalInput").ap()
    y_d = nc.dram_tensor("y", [T, E], F32, kind="ExternalOutput").ap()

    q_sc = nc.dram_tensor("q_sc", [H, DH, T], F32, kind=sck).ap()
    agin = nc.dram_tensor("agin", [2, H, DH, T], F32).ap()
    agout = nc.dram_tensor("agout", [NRANK, 2, H, DH, T], F32).ap()
    ao_sc = nc.dram_tensor("ao_sc", [NTT, H, DH, 128], F32, kind=sck).ap()
    x1_sc = nc.dram_tensor("x1_sc", [NTT, 128, E], F32, kind=sck).ap()
    z_sc = nc.dram_tensor("z_sc", [NTT, 128, NI], F32, kind=sck).ap()
    wgt_sc = nc.dram_tensor("wgt_sc", [16, 4, 16, 128, 128], BF16, kind=sck).ap()
    wut_sc = nc.dram_tensor("wut_sc", [16, 4, 16, 128, 128], BF16, kind=sck).ap()
    wdt_sc = nc.dram_tensor("wdt_sc", [16, 64, 128, 128], BF16, kind=sck).ap()
    if debug:
        dbg_p = nc.dram_tensor("dbg_p", [128, SEQ], F32, kind="ExternalOutput").ap()
        dbg_tab = nc.dram_tensor("dbg_tab", [128, 4 * 64 + 2 * 16 + 4], F32, kind="ExternalOutput").ap()
        dbg_kq = nc.dram_tensor("dbg_kq", [128, E], BF16, kind="ExternalOutput").ap()
        dbg_ffn = nc.dram_tensor("dbg_ffn", [128, 64], F32, kind="ExternalOutput").ap()
        dbg_acc = nc.dram_tensor("dbg_acc", [128, 64], F32, kind="ExternalOutput").ap()
        dbg_g = nc.dram_tensor("dbg_g", [128, 1536], F32, kind="ExternalOutput").ap()
        dbg_fw = nc.dram_tensor("dbg_fw", [128, NI], F32, kind="ExternalOutput").ap()
        dbg_k2 = nc.dram_tensor("dbg_k2", [128, NI], BF16, kind="ExternalOutput").ap()
        dbg_dt = nc.dram_tensor("dbg_dt", [128, E], F32, kind="ExternalOutput").ap()
    dbg_gate = [0]

    es = contextlib.ExitStack()
    P = Prog(nc, es)
    for s in ("sd", "gd", "so", "cc", "bar", "tq", "tp", "tv", "sc", "ve", "gp"):
        P.sem(s)

    ident32 = es.enter_context(nc.sbuf_tensor("ident32", [128, 128], F32))
    ident16 = es.enter_context(nc.sbuf_tensor("ident16", [128, 128], BF16))
    ones_col = es.enter_context(nc.sbuf_tensor("ones_col", [128, 1], F32))
    ones_row = es.enter_context(nc.sbuf_tensor("ones_row", [1, 128], F32))
    mu_bc = es.enter_context(nc.sbuf_tensor("mu_bc", [128, 4], F32))
    agtab = es.enter_context(nc.sbuf_tensor("agtab", [128, NTT], F32))
    eps_col = es.enter_context(nc.sbuf_tensor("eps_col", [128, 1], F32))
    autab = es.enter_context(nc.sbuf_tensor("autab", [128, NTT], F32))

    P.ins("gpsimd", "memset", ident32[:], 0.0)
    P.ins("gpsimd", "affine_select", out=ident32[:], in_=ident32[:],
          compare_op=ALU.not_equal, fill=1.0, base=0, pattern=[[-1, 128]],
          channel_multiplier=1)
    P.ins("gpsimd", "memset", ident16[:], 0.0)
    P.ins("gpsimd", "affine_select", out=ident16[:], in_=ident16[:],
          compare_op=ALU.not_equal, fill=1.0, base=0, pattern=[[-1, 128]],
          channel_multiplier=1)
    P.ins("gpsimd", "memset", ones_col[:], 1.0)
    P.ins("gpsimd", "memset", eps_col[:], EPS)
    gp_const = P.ins("gpsimd", "memset", ones_row[:], 1.0, inc=("gp", 1))

    ph0 = contextlib.ExitStack()
    xraw = [ph0.enter_context(nc.sbuf_tensor(f"xraw{i}", [128, E], F32)) for i in range(2)]
    wqraw = [ph0.enter_context(nc.sbuf_tensor(f"wqraw{i}", [128, E], F32)) for i in range(2)]
    wqT = [ph0.enter_context(nc.sbuf_tensor(f"wqT{i}", [128, 16, 128], F32)) for i in range(2)]
    qstage = [ph0.enter_context(nc.sbuf_tensor(f"qstage{i}", [128, 512], F32)) for i in range(2)]
    qs1 = ph0.enter_context(nc.sbuf_tensor("qs1", [128, 512], F32))
    qs2 = [ph0.enter_context(nc.sbuf_tensor(f"qs2{i}", [128, 512], F32)) for i in range(2)]
    ps_tp = [ph0.enter_context(nc.psum_tensor(f"ps_tp{i}", [128, 512], F32)) for i in range(2)]
    ps_mm = [ph0.enter_context(nc.psum_tensor(f"ps_mm{i}", [128, 512], F32)) for i in range(2)]
    ps_sm = ph0.enter_context(nc.psum_tensor("ps_sm", [128, 2], F32))
    ps_tpb = [ph0.enter_context(nc.psum_tensor(f"ps_tpb{i}", [128, 512], BF16)) for i in range(2)]
    wqvT = ph0.enter_context(nc.sbuf_tensor("wqvT", [128, 16, 256], F32))

    phx = contextlib.ExitStack()
    xT = phx.enter_context(nc.sbuf_tensor("xT", [128, 16, T], F32))

    xraw_free = [0, 0]
    pstp_free = [0, 0]
    P.wait("tensor", "gp", gp_const)
    for tt in range(NTT):
        sl = tt % 2
        P.wait("sync", "tq", xraw_free[sl])
        sdv = P.ins("sync", "dma_start", out=xraw[sl][:], in_=x_d[tt * 128:(tt + 1) * 128, :],
                    inc=(f"sx{sl}", 16))
        P.wait("tensor", f"sx{sl}", sdv)
        for g in range(4):
            gs = (tt * 4 + g) % 2
            P.wait("tensor", "sc", pstp_free[gs])
            for j in range(4):
                ech = 4 * g + j
                v = P.ins("tensor", "transpose", ps_tp[gs][:, j * 128:(j + 1) * 128],
                          xraw[sl][:, ech * 128:(ech + 1) * 128], ident32[:],
                          inc=("tq", 1) if j == 3 else None)
            P.wait("scalar", "tq", v)
            pstp_free[gs] = P.ins("scalar", "activation",
                                  out=xT[:, 4 * g:4 * g + 4, tt * 128:(tt + 1) * 128],
                                  in_=ps_tp[gs][:].rearrange("p (a b) -> p a b", a=4),
                                  func=AF.Copy, bias=0.0, scale=1.0, inc=("sc", 1))
        xraw_free[sl] = P.val("tq")
    sc_xT_done = P.val("sc")

    wqraw_free = [0, 0]
    qstage_free = [0, 0]
    qstage_free_sem = ["gd", "gd"]
    qs2_free = [0, 0]
    psmm_free = [0, 0]
    idx_c = [0]
    ms_c = [0]

    P.wait("tensor", "sc", sc_xT_done)

    def emit_w_transpose(ft, dst_ap_fn):
        sl = idx_c[0] % 2
        idx_c[0] += 1
        P.wait("sync", "tq", wqraw_free[sl])
        sdv = P.ins("sync", "dma_start", out=wqraw[sl][:],
                    in_=wqkv_d[ft * 128:(ft + 1) * 128, :], inc=(f"sw{sl}", 16))
        P.wait("tensor", f"sw{sl}", sdv)
        for g in range(4):
            gs = (idx_c[0] * 4 + g) % 2
            P.wait("tensor", "sc", pstp_free[gs])
            for j in range(4):
                ech = 4 * g + j
                v = P.ins("tensor", "transpose", ps_tp[gs][:, j * 128:(j + 1) * 128],
                          wqraw[sl][:, ech * 128:(ech + 1) * 128], ident32[:],
                          inc=("tq", 1) if j == 3 else None)
            P.wait("scalar", "tq", v)
            pstp_free[gs] = P.ins("scalar", "activation",
                                  out=dst_ap_fn(g),
                                  in_=ps_tp[gs][:].rearrange("p (a b) -> p a b", a=4),
                                  func=AF.Copy, bias=0.0, scale=1.0, inc=("sc", 1))
        wqraw_free[sl] = P.val("tq")
        return P.val("sc")

    def emit_kq_tile(ft):
        sl2 = idx_c[0] % 2
        wqT_ready = emit_w_transpose(ft, lambda g, sl2=sl2: wqT[sl2][:, 4 * g:4 * g + 4, :])
        P.wait("tensor", "sc", wqT_ready)
        for tc in range(NTC):
            ms = ms_c[0] % 2
            ms_c[0] += 1
            P.wait("tensor", "sc", psmm_free[ms])
            for ech in range(16):
                v = P.ins("tensor", "matmul", ps_mm[ms][:],
                          wqT[sl2][:, ech, :], xT[:, ech, tc * 512:(tc + 1) * 512],
                          start=(ech == 0), stop=(ech == 15),
                          inc=("tq", 1) if ech == 15 else None)
            P.wait("scalar", "tq", v)
            P.wait("scalar", qstage_free_sem[ms], qstage_free[ms])
            scv = P.ins("scalar", "activation", out=qstage[ms][:], in_=ps_mm[ms][:],
                        func=AF.Copy, bias=0.0, scale=1.0, inc=("sc", 1))
            psmm_free[ms] = scv
            if ft < 16:
                P.wait("vector", "sc", scv)
                P.ins("vector", "tensor_scalar", out=qs1[:], in0=qstage[ms][:],
                      scalar1=SC_HI, scalar2=None, op0=ALU.mult)
                P.wait("vector", f"gq{ms}", qs2_free[ms])
                vev = P.ins("vector", "scalar_tensor_tensor", out=qs2[ms][:],
                            in0=qstage[ms][:], scalar=SC_LO, in1=qs1[:],
                            op0=ALU.mult, op1=ALU.add, inc=("ve", 1))
                qstage_free[ms] = vev
                qstage_free_sem[ms] = "ve"
                P.wait("gpsimd", "ve", vev)
                qs2_free[ms] = P.ins("gpsimd", "dma_start",
                                     out=q_sc[ft, :, tc * 512:(tc + 1) * 512],
                                     in_=qs2[ms][:], inc=(f"gq{ms}", 16))
            else:
                P.wait("gpsimd", "sc", scv)
                dst = agin[0, ft - 16, :, tc * 512:(tc + 1) * 512]
                qstage_free[ms] = P.ins("gpsimd", "dma_start", out=dst, in_=qstage[ms][:],
                                        inc=(f"gk{ms}", 16))
                qstage_free_sem[ms] = f"gk{ms}"

    for ft in range(16, 32):
        emit_kq_tile(ft)
    agin_v = agin[1].rearrange("h d t -> (h d) t")
    for vg in range(8):
        wqv_ready = 0
        for kk in range(2):
            ft = 32 + 2 * vg + kk
            wqv_ready = emit_w_transpose(
                ft, lambda g, kk=kk: wqvT[:, 4 * g:4 * g + 4, kk * 128:(kk + 1) * 128])
        P.wait("tensor", "sc", wqv_ready)
        for tt in range(NTT):
            ms = ms_c[0] % 2
            ms_c[0] += 1
            P.wait("tensor", "sc", psmm_free[ms])
            for ech in range(16):
                v = P.ins("tensor", "matmul", ps_mm[ms][:, 0:256],
                          xT[:, ech, tt * 128:(tt + 1) * 128],
                          wqvT[:, ech, :].rearrange("p a -> p a"),
                          start=(ech == 0), stop=(ech == 15),
                          inc=("tq", 1) if ech == 15 else None)
            P.wait("scalar", "tq", v)
            P.wait("scalar", qstage_free_sem[ms], qstage_free[ms])
            scv = P.ins("scalar", "activation", out=qstage[ms][:, 0:256],
                        in_=ps_mm[ms][:, 0:256],
                        func=AF.Copy, bias=0.0, scale=1.0, inc=("sc", 1))
            psmm_free[ms] = scv
            P.wait("gpsimd", "sc", scv)
            qstage_free[ms] = P.ins("gpsimd", "dma_start",
                                    out=agin_v[tt * 128:(tt + 1) * 128,
                                               vg * 256:(vg + 1) * 256],
                                    in_=qstage[ms][:, 0:256], inc=(f"gk{ms}", 16))
            qstage_free_sem[ms] = f"gk{ms}"
    P.wait("gpsimd", "gk0", P.val("gk0"))
    P.wait("gpsimd", "gk1", P.val("gk1"))
    P.ins("gpsimd", "collective_compute",
          "AllGather", ALU.bypass,
          replica_groups=[[0, 1, 2, 3], [4, 5, 6, 7]],
          ins=[agin.opt()], outs=[agout.opt()], inc=("cc", 1))
    for ft in range(16):
        emit_kq_tile(ft)
    q_sc_done = (P.val("gq0"), P.val("gq1"))
    tq_c_done = P.val("tq")
    phx.close()
    P.barrier()

    wtile = [ph0.enter_context(nc.sbuf_tensor(f"wtile{i}", [128, E], F32)) for i in range(2)]
    acc = [ph0.enter_context(nc.sbuf_tensor(f"acc{i}", [128, 1], F32)) for i in range(2)]
    tmp1 = ph0.enter_context(nc.sbuf_tensor("tmp1", [128, 1], F32))
    tot = ph0.enter_context(nc.sbuf_tensor("tot", [1, 2], F32))
    pairq = ph0.enter_context(nc.sbuf_tensor("pairq", [1, 2], F32))
    uq2 = [ph0.enter_context(nc.sbuf_tensor(f"uq{i}", [128, E], F32)) for i in range(2)]
    v1q = ph0.enter_context(nc.sbuf_tensor("v1q", [128, E], F32))
    tq_ = [ph0.enter_context(nc.sbuf_tensor(f"tq{i}", [128, E], BF16)) for i in range(2)]
    tstage = [ph0.enter_context(nc.sbuf_tensor(f"tstage{i}", [128, 512], BF16)) for i in range(2)]
    swmu = ph0.enter_context(nc.sbuf_tensor("swmu", [128, 2], F32))

    first_c2 = [True]
    pstpb_free = [0, 0]
    wtile_free = [0, 0]
    wtile_free_sem = ["ve", "ve"]
    tstage_free = [0, 0]
    tqbuf_free = [0, 0]

    def quant_matrix(mi, src_d, dst_sc, kind):
        ntile = 64

        def tile_src(t):
            if kind == "gu":
                return src_d[t * 128:(t + 1) * 128, :]
            ftl, ip = t // 4, t % 4
            return src_d[ftl * 128:(ftl + 1) * 128, ip * E:(ip + 1) * E]

        for t in range(ntile):
            sl = t % 2
            q = "sync" if sl == 0 else "gpsimd"
            qs = f"sw{sl}" if sl == 0 else "gw1"
            if first_c2[0]:
                P.wait("sync", "tq", tq_c_done)
                P.wait("gpsimd", "tq", tq_c_done)
                first_c2[0] = False
            P.wait(q, wtile_free_sem[sl], wtile_free[sl])
            sdv = P.ins(q, "dma_start", out=wtile[sl][:], in_=tile_src(t),
                        inc=(qs, 16))
            P.wait("vector", qs, sdv)
            vv = P.ins("vector", "tensor_reduce", out=tmp1[:], in_=wtile[sl][:],
                       axis=AX.X, op=ALU.add, apply_absolute_value=True, inc=("ve", 1))
            wtile_free[sl] = vv
            wtile_free_sem[sl] = "ve"
            P.ins("vector", "drain")
            if debug and mi == 0:
                P.wait("gpsimd", "ve", vv)
                gdbg = P.ins("gpsimd", "dma_start", out=dbg_acc[:, t:t + 1], in_=tmp1[:],
                             inc=("gd", 16), allow_slow_non_contiguous=True)
                P.wait("vector", "gd", gdbg)
            if t == 0:
                P.ins("vector", "tensor_copy", acc[0][:], tmp1[:])
            else:
                P.ins("vector", "tensor_tensor", out=acc[t % 2][:], in0=acc[(t - 1) % 2][:],
                      in1=tmp1[:], op=ALU.add)
            P.ins("vector", "drain")
        vv = P.bump("vector", "ve", 1)
        if debug and mi == 0:
            P.wait("gpsimd", "ve", vv)
            P.ins("gpsimd", "dma_start", out=dbg_ffn[:, 36:37], in_=acc[(ntile - 1) % 2][:],
                  inc=("gd", 16), allow_slow_non_contiguous=True)
        P.wait("tensor", "ve", vv)
        txv = P.ins("tensor", "matmul", ps_sm[0:1, 0:1], acc[(ntile - 1) % 2][:], ones_col[:],
                    start=True, stop=True, inc=("tq", 1))
        P.wait("scalar", "tq", txv)
        scv = P.ins("scalar", "activation", out=tot[:, 0:1], in_=ps_sm[0:1, 0:1],
                    func=AF.Copy, bias=0.0, scale=1.0, inc=("sc", 1))
        if debug and mi == 0:
            P.wait("gpsimd", "sc", scv)
            P.ins("gpsimd", "dma_start", out=dbg_ffn[0:1, 38:39], in_=tot[:, 0:1],
                  inc=("gd", 16), allow_slow_non_contiguous=True)
        P.wait("vector", "sc", scv)
        P.ins("vector", "tensor_scalar", out=tot[:, 1:2], in0=tot[:, 0:1],
              scalar1=P2M24, scalar2=EPS, op0=ALU.mult, op1=ALU.max)
        P.ins("vector", "drain")
        P.ins("vector", "reciprocal", out=pairq[:, 0:1], in_=tot[:, 1:2])
        P.ins("vector", "drain")
        vv = P.ins("vector", "tensor_copy", pairq[:, 1:2], tot[:, 1:2], inc=("ve", 1))
        P.wait("tensor", "ve", vv)
        txv = P.ins("tensor", "matmul", ps_sm[:, 0:2], ones_row[:], pairq[:],
                    start=True, stop=True, inc=("tq", 1))
        P.wait("scalar", "tq", txv)
        swmu_ready = P.ins("scalar", "activation", out=swmu[:], in_=ps_sm[:, 0:2],
                           func=AF.Copy, bias=0.0, scale=1.0, inc=("sc", 1))
        P.ins("scalar", "drain")
        P.wait("vector", "sc", swmu_ready)
        P.ins("vector", "tensor_copy", mu_bc[:, mi:mi + 1], swmu[:, 1:2])

        for t in range(ntile):
            sl = t % 2
            q = "sync" if sl == 0 else "gpsimd"
            qs = f"sw{sl}" if sl == 0 else "gw1"
            P.wait(q, wtile_free_sem[sl], wtile_free[sl])
            sdv = P.ins(q, "dma_start", out=wtile[sl][:], in_=tile_src(t),
                        inc=(qs, 16))
            P.wait("scalar", qs, sdv)
            scv = P.ins("scalar", "activation", out=uq2[sl][:], in_=wtile[sl][:],
                        func=AF.Copy, bias=MAGIC, scale=swmu[:, 0:1], inc=("sc", 1))
            wtile_free[sl] = scv
            wtile_free_sem[sl] = "sc"
            P.wait("vector", "sc", scv)
            P.ins("vector", "tensor_scalar", out=v1q[:], in0=uq2[sl][:],
                  scalar1=-MAGIC, scalar2=-1.0, op0=ALU.add, op1=ALU.max)
            P.wait("vector", "tq", tqbuf_free[sl])
            vv = P.ins("vector", "tensor_scalar", out=tq_[sl][:], in0=v1q[:],
                       scalar1=1.0, scalar2=None, op0=ALU.min, inc=("ve", 1))
            P.wait("tensor", "ve", vv)
            for g in range(4):
                gs = (t * 4 + g) % 2
                P.wait("tensor", "sc", pstpb_free[gs])
                for j in range(4):
                    ech = 4 * g + j
                    v = P.ins("tensor", "transpose", ps_tpb[gs][:, j * 128:(j + 1) * 128],
                              tq_[sl][:, ech * 128:(ech + 1) * 128], ident16[:],
                              inc=("tq", 1) if j == 3 else None)
                P.wait("scalar", "tq", v)
                P.wait("scalar", f"gt{gs}", tstage_free[gs])
                scv = P.ins("scalar", "activation", out=tstage[gs][:],
                            in_=ps_tpb[gs][:], func=AF.Copy, bias=0.0, scale=1.0,
                            inc=("sc", 1))
                pstpb_free[gs] = scv
                P.wait("gpsimd", "sc", scv)
                if kind == "gu":
                    dst = dst_sc[t // 4, t % 4, 4 * g:4 * g + 4].rearrange("a p t -> p a t")
                else:
                    ftl, ip = t // 4, t % 4
                    dst = dst_sc[ftl, ip * 16 + 4 * g: ip * 16 + 4 * g + 4].rearrange("a p t -> p a t")
                tstage_free[gs] = P.ins("gpsimd", "dma_start",
                                        out=dst,
                                        in_=tstage[gs][:].rearrange("p (a b) -> p a b", a=4),
                                        inc=(f"gt{gs}", 16))
            tqbuf_free[sl] = P.val("tq")

    quant_matrix(0, wg_d, wgt_sc, "gu")
    quant_matrix(1, wu_d, wut_sc, "gu")
    quant_matrix(2, wd_d, wdt_sc, "d")

    ph0.close()
    P.barrier()

    ph = contextlib.ExitStack()
    Kh = [ph.enter_context(nc.sbuf_tensor(f"Kh{i}", [128, SEQ], F32)) for i in range(2)]
    Vtok = [ph.enter_context(nc.sbuf_tensor(f"Vtok{i}", [128, NKB, 128], F32)) for i in range(2)]
    Qh = ph.enter_context(nc.sbuf_tensor("Qh", [128, T], F32))
    Pbuf = ph.enter_context(nc.sbuf_tensor("Pbuf", [128, SEQ], F32))
    P2 = [ph.enter_context(nc.sbuf_tensor(f"P2{i}", [128, T], F32)) for i in range(4)]
    PT = [ph.enter_context(nc.sbuf_tensor(f"PT{i}", [128, 512], F32)) for i in range(2)]
    attst0 = ph.enter_context(nc.sbuf_tensor("attst0", [128, 128], F32))
    attst = [attst0, attst0]
    mtab = ph.enter_context(nc.sbuf_tensor("mtab", [128, 64], F32))
    btab = ph.enter_context(nc.sbuf_tensor("btab", [128, 16], F32))
    ttab = ph.enter_context(nc.sbuf_tensor("ttab", [128, 64], F32))
    arg1 = ph.enter_context(nc.sbuf_tensor("arg1", [128, 16], F32))
    e1t = ph.enter_context(nc.sbuf_tensor("e1t", [128, 16], F32))
    wtab = ph.enter_context(nc.sbuf_tensor("wtab", [128, 64], F32))
    d2t = ph.enter_context(nc.sbuf_tensor("d2t", [128, 64], F32))
    e2t = ph.enter_context(nc.sbuf_tensor("e2t", [128, 64], F32))
    d1t = wtab
    g1t = d2t
    g2t = wtab
    Mv = ph.enter_context(nc.sbuf_tensor("Mv", [128, 1], F32))
    Dv = ph.enter_context(nc.sbuf_tensor("Dv", [128, 1], F32))
    Dinv = ph.enter_context(nc.sbuf_tensor("Dinv", [128, 1], F32))
    ps_s = [ph.enter_context(nc.psum_tensor(f"ps_s{i}", [128, 512], F32)) for i in range(4)]
    ps_pt = [ph.enter_context(nc.psum_tensor(f"ps_pt{i}", [128, 512], F32)) for i in range(2)]
    ps_pv = [ph.enter_context(nc.psum_tensor(f"ps_pv{i}", [128, 128], F32)) for i in range(2)]

    kh_free = [0, 0]
    kh_ready = [0, 0]
    vt_free = [0, 0]
    vt_ready = [0, 0]
    qh_ready = [0]
    pss_free = [0, 0, 0, 0]
    pspt_free = [0, 0]
    ptbuf_free = [0, 0]
    pspv_free = [0, 0]
    attst_free = [0, 0]

    agout_v = [agout[r, 1].rearrange("h d t -> (h d) t") for r in range(NRANK)]

    def kv_loads(h):
        sl = h % 2
        P.wait("sync", "cc", 1)
        P.wait("sync", "tq", kh_free[sl])
        for r in range(NRANK):
            v = P.ins("sync", "dma_start", out=Kh[sl][:, r * T:(r + 1) * T],
                      in_=agout[r, 0, h], inc=(f"sk{sl}", 16))
        kh_ready[sl] = v
        P.wait("sync", "tv", vt_free[sl])
        for r in range(NRANK):
            for b4 in range(4):
                v = P.ins("sync", "dma_start",
                          out=Vtok[sl][:, r * 16 + b4 * 4: r * 16 + b4 * 4 + 4, :],
                          in_=agout_v[r][b4 * 512:(b4 + 1) * 512,
                                         h * 128:(h + 1) * 128]
                          .rearrange("(kb p) d -> p kb d", p=128),
                          inc=(f"sv{sl}", 16))
        vt_ready[sl] = v

    def q_load(h, after_tq):
        P.wait("sync", "gq0", q_sc_done[0])
        P.wait("sync", "gq1", q_sc_done[1])
        P.wait("sync", "tq", after_tq)
        qh_ready[0] = P.ins("sync", "dma_start", out=Qh[:], in_=q_sc[h], inc=("sq0", 16))

    def emit_qkt(h, qt, gq):
        sl = h % 2
        P.wait("tensor", f"sk{sl}", kh_ready[sl])
        P.wait("tensor", "sq0", qh_ready[0])
        for ch in range(NCH):
            cs = (gq * 16 + ch) % 4
            P.wait("tensor", "sc", pss_free[cs])
            tqv = P.ins("tensor", "matmul", ps_s[cs][:],
                        Qh[:, qt * 128:(qt + 1) * 128],
                        Kh[sl][:, ch * 512:(ch + 1) * 512],
                        start=True, stop=True, inc=("tq", 1))
            P.wait("vector", "tq", tqv)
            P.ins("vector", "tensor_reduce", out=mtab[:, ch * 4:ch * 4 + 4],
                  in_=ps_s[cs][:].rearrange("p (a b) -> p a b", a=4),
                  axis=AX.X, op=ALU.max)
            P.ins("vector", "drain")
            vev = P.ins("vector", "tensor_reduce", out=btab[:, ch:ch + 1],
                        in_=mtab[:, ch * 4:ch * 4 + 4], axis=AX.X, op=ALU.max,
                        negate=True, inc=("ve", 1))
            P.ins("vector", "drain")
            P.wait("scalar", "ve", vev)
            if gq > 0 and ch % 4 == 0:
                P.wait("scalar", "vsl", 4 * (gq - 1) + ch // 4 + 1)
            if ch < 7:
                for j in range(4):
                    scv = P.ins("scalar", "activation",
                                out=Pbuf[:, ch * 512 + j * 128: ch * 512 + (j + 1) * 128],
                                in_=ps_s[cs][:, j * 128:(j + 1) * 128],
                                func=AF.Exp, bias=btab[:, ch:ch + 1], scale=1.0,
                                accum_out=ttab[:, ch * 4 + j: ch * 4 + j + 1],
                                inc=("sc", 1) if j == 3 else None)
            else:
                scv = P.ins("scalar", "activation",
                            out=Pbuf[:, ch * 512:(ch + 1) * 512],
                            in_=ps_s[cs][:],
                            func=AF.Exp, bias=btab[:, ch:ch + 1], scale=1.0,
                            inc=("sc", 1))
                P.wait("vector", "sc", scv)
                P.ins("vector", "tensor_reduce", out=ttab[:, ch * 4:ch * 4 + 4],
                      in_=Pbuf[:, ch * 512:(ch + 1) * 512]
                      .rearrange("p (a b) -> p a b", a=4),
                      axis=AX.X, op=ALU.add)
                P.ins("vector", "drain")
            pss_free[cs] = scv
        return P.val("sc")

    def emit_tables(h, qt):
        P.ins("vector", "tensor_reduce", out=Mv[:], in_=mtab[:], axis=AX.X, op=ALU.max)
        P.ins("vector", "drain")
        P.ins("vector", "tensor_scalar", out=arg1[:], in0=btab[:],
              scalar1=Mv[:], scalar2=-1.0, op0=ALU.add, op1=ALU.mult)
        vev = P.bump("vector", "ve", 1)
        P.ins("vector", "drain")
        P.wait("scalar", "ve", vev)
        scv = P.ins("scalar", "activation", out=e1t[:], in_=arg1[:], func=AF.Exp,
                    bias=0.0, scale=1.0, inc=("sc", 1))
        P.wait("vector", "sc", scv)
        P.ins("vector", "tensor_tensor",
              out=wtab[:].rearrange("p (c j) -> p c j", c=16),
              in0=ttab[:].rearrange("p (c j) -> p c j", c=16),
              in1=e1t[:].unsqueeze(2).broadcast_to([128, 16, 4]), op=ALU.mult)
        P.ins("vector", "drain")
        P.ins("vector", "tensor_reduce", out=Dv[:], in_=wtab[:], axis=AX.X, op=ALU.add)
        P.ins("vector", "drain")
        P.ins("vector", "reciprocal", out=Dinv[:], in_=Dv[:])
        P.ins("vector", "drain")
        P.ins("vector", "tensor_tensor",
              out=d1t[:].rearrange("p (c j) -> p c j", c=16),
              in0=btab[:].unsqueeze(2).broadcast_to([128, 16, 4]),
              in1=mtab[:].rearrange("p (c j) -> p c j", c=16), op=ALU.add)
        P.ins("vector", "drain")
        P.ins("vector", "tensor_scalar", out=e2t[:], in0=d1t[:], scalar1=-1.0,
              scalar2=80.0, op0=ALU.mult, op1=ALU.min)
        P.ins("vector", "drain")
        vev = P.ins("vector", "tensor_tensor",
                    out=d2t[:].rearrange("p (c j) -> p c j", c=16),
                    in0=e2t[:].rearrange("p (c j) -> p c j", c=16),
                    in1=arg1[:].unsqueeze(2).broadcast_to([128, 16, 4]),
                    op=ALU.add, inc=("ve", 1))
        P.wait("scalar", "ve", vev)
        scv = P.ins("scalar", "activation", out=e2t[:], in_=d2t[:], func=AF.Exp,
                    bias=0.0, scale=1.0, inc=("sc", 1))
        P.wait("vector", "sc", scv)
        P.ins("vector", "tensor_tensor", out=g1t[:], in0=ttab[:], in1=e2t[:], op=ALU.mult)
        P.ins("vector", "drain")
        P.ins("vector", "tensor_scalar", out=g2t[:], in0=g1t[:], scalar1=Dinv[:],
              scalar2=None, op0=ALU.mult)
        P.ins("vector", "drain")

    def emit_scales(h, qt, gq, exps_done):
        for r in range(4):
            kq = 4 * gq + r
            slot = kq % 4
            P.wait("vector", "sc", exps_done)
            if kq >= 4:
                P.wait("vector", "tpq", kq - 3)
            P.ins("vector", "tensor_tensor",
                  out=P2[slot][:].rearrange("p (c j) -> p c j", c=16),
                  in0=Pbuf[:, r * T:(r + 1) * T].rearrange("p (c j) -> p c j", c=16),
                  in1=g2t[:, r * 16:(r + 1) * 16].unsqueeze(2).broadcast_to([128, 16, 128]),
                  op=ALU.mult, inc=("vsl", 1))

    def emit_ptpv_te(h, qt, gq, exps_done):
        sl = h % 2
        for r in range(4):
            kq = 4 * gq + r
            slot = kq % 4

            def emit_tp(g, kq=kq, slot=slot, gq=gq, r=r):
                gg = (gq * 16 + r * 4 + g) % 2
                P.wait("tensor", "vsl", kq + 1)
                P.wait("tensor", "sc", pspt_free[gg])
                for j in range(4):
                    col = 4 * g + j
                    v = P.ins("tensor", "transpose", ps_pt[gg][:, j * 128:(j + 1) * 128],
                              P2[slot][:, col * 128:(col + 1) * 128], ident32[:],
                              inc=("tp", 1) if j == 3 else None)
                if g == 3:
                    P.bump("tensor", "tpq", 1)
                P.wait("scalar", "tp", v)
                P.wait("scalar", "tv", ptbuf_free[gg])
                pspt_free[gg] = P.ins("scalar", "activation", out=PT[gg][:],
                                      in_=ps_pt[gg][:], func=AF.Copy, bias=0.0,
                                      scale=1.0, inc=("sc", 1))

            def emit_pv(g, gq=gq, r=r, sl=sl):
                gg = (gq * 16 + r * 4 + g) % 2
                jbase = r * 16 + 4 * g
                P.wait("tensor", "sc", pspt_free[gg])
                if jbase == 0:
                    P.wait("tensor", "sc", pspv_free[gq % 2])
                    P.wait("tensor", f"sv{sl}", vt_ready[sl])
                for j in range(4):
                    kb = jbase + j
                    v = P.ins("tensor", "matmul", ps_pv[gq % 2][:],
                              Vtok[sl][:, kb, :], PT[gg][:, j * 128:(j + 1) * 128],
                              start=(kb == 0), stop=(kb == 63),
                              inc=("tv", 1) if j == 3 else None)
                ptbuf_free[gg] = v

            emit_tp(0)
            emit_tp(1)
            emit_pv(0)
            emit_tp(2)
            emit_pv(1)
            emit_tp(3)
            emit_pv(2)
            emit_pv(3)
        pv_done = P.val("tv")
        if qt == NTT - 1:
            vt_free[sl] = pv_done
        P.wait("scalar", "tv", pv_done)
        P.wait("scalar", "ga0", attst_free[0])
        scv = P.ins("scalar", "activation", out=attst[0][:], in_=ps_pv[gq % 2][:],
                    func=AF.Copy, bias=0.0, scale=1.0, inc=("sc", 1))
        pspv_free[gq % 2] = scv
        P.wait("gpsimd", "sc", scv)
        attst_free[0] = P.ins("gpsimd", "dma_start", out=ao_sc[qt, h],
                              in_=attst[0][:], inc=("ga0", 16))

    kv_loads(0)
    kv_loads(1)
    q_load(0, 0)
    prev = None
    gq = 0
    for h in range(H):
        for qt in range(NTT):
            if prev is not None:
                emit_scales(*prev)
            ed = emit_qkt(h, qt, gq)
            if prev is not None:
                emit_ptpv_te(*prev)
            if qt == 0 and h >= 1 and h + 1 < H:
                kv_loads(h + 1)
            if qt == NTT - 1:
                kh_free[h % 2] = P.val("tq")
                if h + 1 < H:
                    q_load(h + 1, P.val("tq"))
            emit_tables(h, qt)
            prev = (h, qt, gq, ed)
            gq += 1
    emit_scales(*prev)
    emit_ptpv_te(*prev)
    ao_done = (P.val("ga0"), P.val("ga1"))

    ph.close()
    P.barrier()

    ph = contextlib.ExitStack()
    woutT = ph.enter_context(nc.sbuf_tensor("woutT", [128, 16, E], F32))
    woraw = [ph.enter_context(nc.sbuf_tensor(f"woraw{i}", [128, E], F32)) for i in range(2)]
    aoblk = ph.enter_context(nc.sbuf_tensor("aoblk", [128, 16, 128], F32))
    xtt = ph.enter_context(nc.sbuf_tensor("xtt", [128, E], F32))
    borow = ph.enter_context(nc.sbuf_tensor("borow", [1, E], F32))
    Btile = ph.enter_context(nc.sbuf_tensor("Btile", [128, E], F32))
    t_a = ph.enter_context(nc.sbuf_tensor("t_a", [128, E], F32))
    s1 = ph.enter_context(nc.sbuf_tensor("s1", [128, 1], F32))
    meanv = ph.enter_context(nc.sbuf_tensor("meanv", [128, 1], F32))
    ssq = ph.enter_context(nc.sbuf_tensor("ssq", [128, 1], F32))
    rstd = ph.enter_context(nc.sbuf_tensor("rstd", [128, 1], F32))
    x1st = [ph.enter_context(nc.sbuf_tensor(f"x1st{i}", [128, E], F32)) for i in range(2)]

    php = contextlib.ExitStack()
    ps_tp4 = [php.enter_context(nc.psum_tensor(f"ps_tp4{i}", [128, 512], F32)) for i in range(2)]

    sdv = P.ins("sync", "dma_start", out=borow[:], in_=bout_d.rearrange("(a b) -> a b", a=1),
                inc=("sb", 16))
    P.wait("tensor", "sb", sdv)
    pstp_free4 = [0, 0]
    for fc in range(4):
        gs = fc % 2
        P.wait("tensor", "sc", pstp_free4[gs])
        txv = P.ins("tensor", "matmul", ps_tp4[gs][:], ones_row[:],
                    borow[:, fc * 512:(fc + 1) * 512], start=True, stop=True, inc=("tq", 1))
        P.wait("scalar", "tq", txv)
        pstp_free4[gs] = P.ins("scalar", "activation", out=Btile[:, fc * 512:(fc + 1) * 512],
                               in_=ps_tp4[gs][:], func=AF.Copy, bias=0.0, scale=1.0,
                               inc=("sc", 1))
    woraw_free = [0, 0]
    for ft in range(16):
        sl = ft % 2
        P.wait("sync", "tq", woraw_free[sl])
        sdv = P.ins("sync", "dma_start", out=woraw[sl][:],
                    in_=wout_d[ft * 128:(ft + 1) * 128, :], inc=(f"sw{sl}", 16))
        P.wait("tensor", f"sw{sl}", sdv)
        for g in range(4):
            gs = g % 2
            P.wait("tensor", "sc", pstp_free4[gs])
            for j in range(4):
                ech = 4 * g + j
                v = P.ins("tensor", "transpose", ps_tp4[gs][:, j * 128:(j + 1) * 128],
                          woraw[sl][:, ech * 128:(ech + 1) * 128], ident32[:],
                          inc=("tq", 1) if j == 3 else None)
            P.wait("scalar", "tq", v)
            pstp_free4[gs] = P.ins("scalar", "activation",
                                   out=woutT[:, 4 * g:4 * g + 4, ft * 128:(ft + 1) * 128],
                                   in_=ps_tp4[gs][:].rearrange("p (a b) -> p a b", a=4),
                                   func=AF.Copy, bias=0.0, scale=1.0, inc=("sc", 1))
        woraw_free[sl] = P.val("tq")
    woutT_ready = P.val("sc")
    tq_prep_done = P.val("tq")
    php.close()

    ps_o = [ph.enter_context(nc.psum_tensor(f"ps_o{i}", [128, E], F32)) for i in range(2)]
    aoblk2 = [aoblk[:], woraw[0][:].rearrange("p (h t) -> p h t", h=16)]
    xtt2 = [xtt, woraw[1]]
    aoblk_free = [0, 0]
    xtt_free = [0, 0]
    pso_free = [0, 0]
    x1st_free = [0, 0]
    P.wait("tensor", "sc", woutT_ready)
    for tt in range(NTT):
        sl = tt % 2
        P.wait("sync", "ga0", ao_done[0])
        P.wait("sync", "ga1", ao_done[1])
        if tt < 2:
            P.wait("sync", "tq", tq_prep_done)
        P.wait("sync", "tq", aoblk_free[sl])
        aoblk_ready = P.ins("sync", "dma_start", out=aoblk2[sl],
                            in_=ao_sc[tt].rearrange("h p t -> p h t"), inc=(f"sa{sl}", 16))
        P.wait("sync", "ve", xtt_free[sl])
        xtt_ready = P.ins("sync", "dma_start", out=xtt2[sl][:],
                          in_=x_d[tt * 128:(tt + 1) * 128, :], inc=(f"sxt{sl}", 16))
        P.wait("tensor", f"sa{sl}", aoblk_ready)
        P.wait("tensor", "sc", pso_free[sl])
        if tt < 2:
            P.wait("tensor", "tq", tq_prep_done)
        for ech in range(16):
            for fc in range(4):
                v = P.ins("tensor", "matmul", ps_o[sl][:, fc * 512:(fc + 1) * 512],
                          aoblk2[sl][:, ech, :], woutT[:, ech, fc * 512:(fc + 1) * 512],
                          start=(ech == 0), stop=(ech == 15),
                          inc=("tq", 1) if (ech == 15 and fc == 3) else None)
        aoblk_free[sl] = v
        P.wait("vector", "tq", v)
        P.wait("vector", f"sxt{sl}", xtt_ready)
        P.wait("vector", "sc", woutT_ready)
        P.wait("vector", f"gx{sl}", x1st_free[sl])
        P.ins("vector", "tensor_tensor", out=t_a[:], in0=ps_o[sl][:], in1=Btile[:], op=ALU.add)
        vv = P.ins("vector", "tensor_tensor", out=x1st[sl][:], in0=t_a[:], in1=xtt2[sl][:],
                   op=ALU.add, inc=("ve", 1))
        xtt_free[sl] = vv
        P.ins("vector", "tensor_reduce", out=s1[:], in_=x1st[sl][:], axis=AX.X, op=ALU.add)
        P.ins("vector", "drain")
        P.ins("vector", "tensor_scalar", out=meanv[:], in0=s1[:], scalar1=INV2048,
              scalar2=None, op0=ALU.mult)
        P.ins("vector", "drain")
        vv = P.ins("vector", "tensor_scalar", out=t_a[:], in0=x1st[sl][:], scalar1=meanv[:],
                   scalar2=None, op0=ALU.subtract, inc=("ve", 1))
        P.wait("scalar", "ve", vv)
        pso_free[sl] = P.ins("scalar", "activation", out=ps_o[sl][:], in_=t_a[:],
                             func=AF.Square, bias=0.0, scale=1.0, accum_out=ssq[:],
                             inc=("sc", 1))
        P.ins("scalar", "drain")
        scv = P.ins("scalar", "activation", out=rstd[:], in_=ssq[:], func=AF.Sqrt,
                    bias=eps_col[:], scale=INV2048, inc=("sc", 1))
        P.wait("vector", "sc", scv)
        P.ins("vector", "reciprocal", out=meanv[:], in_=rstd[:])
        P.ins("vector", "drain")
        vv = P.ins("vector", "tensor_scalar", out=x1st[sl][:], in0=t_a[:], scalar1=meanv[:],
                   scalar2=None, op0=ALU.mult, inc=("ve", 1))
        P.wait("gpsimd", "ve", vv)
        x1st_free[sl] = P.ins("gpsimd", "dma_start", out=x1_sc[tt], in_=x1st[sl][:],
                              inc=(f"gx{sl}", 16))
    x1_done = (P.val("gx0"), P.val("gx1"))

    ph.close()
    P.barrier()

    ph = contextlib.ExitStack()
    kqT = ph.enter_context(nc.sbuf_tensor("kqT", [128, 16, NTT, 128], BF16))
    x1row = [ph.enter_context(nc.sbuf_tensor(f"x1row{i}", [128, E], F32)) for i in range(2)]
    uF = ph.enter_context(nc.sbuf_tensor("uF", [128, E], F32))
    v1F = ph.enter_context(nc.sbuf_tensor("v1F", [128, E], F32))
    kqF = ph.enter_context(nc.sbuf_tensor("kqF", [128, E], BF16))
    amax = ph.enter_context(nc.sbuf_tensor("amax", [128, 1], F32))
    clipm = ph.enter_context(nc.sbuf_tensor("clipm", [128, 1], F32))
    recipm = ph.enter_context(nc.sbuf_tensor("recipm", [128, 1], F32))
    sfull = ph.enter_context(nc.sbuf_tensor("sfull", [128, 1], F32))
    invs = ph.enter_context(nc.sbuf_tensor("invs", [128, 1], F32))
    wgch = [ph.enter_context(nc.sbuf_tensor(f"wgch{i}", [128, 4, 16, 128], BF16)) for i in range(2)]
    wuch = [ph.enter_context(nc.sbuf_tensor(f"wuch{i}", [128, 4, 16, 128], BF16)) for i in range(2)]
    zg = [ph.enter_context(nc.sbuf_tensor(f"zg{i}", [128, 512], F32)) for i in range(2)]
    zst = [ph.enter_context(nc.sbuf_tensor(f"zst{i}", [128, 512], F32)) for i in range(2)]
    ps_g = [ph.enter_context(nc.psum_tensor(f"ps_g{i}", [128, 512], F32)) for i in range(2)]
    ps_u = [ph.enter_context(nc.psum_tensor(f"ps_u{i}", [128, 512], F32)) for i in range(2)]
    ps_tpf = [ph.enter_context(nc.psum_tensor(f"ps_tpf{i}", [128, 512], BF16)) for i in range(2)]

    x1row_free = [0, 0]
    pstpf_free = [0, 0]
    for tt in range(NTT):
        sl = tt % 2
        P.wait("sync", "gx0", x1_done[0])
        P.wait("sync", "gx1", x1_done[1])
        P.wait("sync", "sc", x1row_free[sl])
        sdv = P.ins("sync", "dma_start", out=x1row[sl][:], in_=x1_sc[tt],
                    inc=(f"sx{sl}", 16))
        P.wait("vector", f"sx{sl}", sdv)
        P.ins("vector", "tensor_reduce", out=amax[:], in_=x1row[sl][:], axis=AX.X,
              op=ALU.max, apply_absolute_value=True)
        P.ins("vector", "drain")
        P.ins("vector", "tensor_scalar", out=clipm[:], in0=amax[:], scalar1=EPS,
              scalar2=None, op0=ALU.max)
        P.ins("vector", "drain")
        P.ins("vector", "reciprocal", out=recipm[:], in_=clipm[:])
        P.ins("vector", "drain")
        P.ins("vector", "tensor_scalar", out=sfull[:], in0=recipm[:], scalar1=128.0,
              scalar2=None, op0=ALU.mult)
        P.ins("vector", "tensor_scalar", out=invs[:], in0=clipm[:], scalar1=P2M7,
              scalar2=None, op0=ALU.mult)
        P.ins("vector", "drain")
        P.ins("vector", "tensor_scalar", out=agtab[:, tt:tt + 1], in0=invs[:],
              scalar1=mu_bc[:, 0:1], scalar2=None, op0=ALU.mult)
        vev = P.ins("vector", "tensor_scalar", out=autab[:, tt:tt + 1], in0=invs[:],
                    scalar1=mu_bc[:, 1:2], scalar2=None, op0=ALU.mult, inc=("ve", 1))
        P.wait("scalar", "ve", vev)
        scv = P.ins("scalar", "activation", out=uF[:], in_=x1row[sl][:], func=AF.Copy,
                    bias=MAGIC, scale=sfull[:], inc=("sc", 1))
        x1row_free[sl] = scv
        P.wait("vector", "sc", scv)
        P.ins("vector", "tensor_scalar", out=v1F[:], in0=uF[:], scalar1=-MAGIC,
              scalar2=-128.0, op0=ALU.add, op1=ALU.max)
        vev = P.ins("vector", "tensor_scalar", out=kqF[:], in0=v1F[:], scalar1=127.0,
                    scalar2=None, op0=ALU.min, inc=("ve", 1))
        if debug and tt == 0:
            P.wait("gpsimd", "ve", vev)
            P.ins("gpsimd", "dma_start", out=dbg_kq, in_=kqF[:], inc=("gd", 16))
            P.wait("gpsimd", "gd", P.val("gd"))
            dbg_gate[0] = P.bump("gpsimd", "gp", 1)
            P.wait("scalar", "gp", dbg_gate[0])
        P.wait("tensor", "ve", vev)
        for g in range(4):
            gs = (tt * 4 + g) % 2
            P.wait("tensor", "sc", pstpf_free[gs])
            for j in range(4):
                ech = 4 * g + j
                v = P.ins("tensor", "transpose", ps_tpf[gs][:, j * 128:(j + 1) * 128],
                          kqF[:, ech * 128:(ech + 1) * 128], ident16[:],
                          inc=("tq", 1) if j == 3 else None)
            P.wait("scalar", "tq", v)
            pstpf_free[gs] = P.ins("scalar", "activation",
                                   out=kqT[:, 4 * g:4 * g + 4, tt, :],
                                   in_=ps_tpf[gs][:].rearrange("p (a b) -> p a b", a=4),
                                   func=AF.Copy, bias=0.0, scale=1.0, inc=("sc", 1))
    kqT_ready = P.val("sc")
    prep_ve = P.val("ve")
    if debug:
        P.wait("gpsimd", "ve", prep_ve)
        for off, tsr, w in ((0, agtab, 16), (16, autab, 16), (32, mu_bc, 4)):
            P.ins("gpsimd", "dma_start", out=dbg_ffn[:, off:off + w], in_=tsr[:],
                  inc=("gd", 16), allow_slow_non_contiguous=True)

    wgch_free = [0, 0]
    psg_free = [0, 0]
    psu_free = [0, 0]
    zg_free = [0, 0]
    zst_free = [0, 0]
    P.wait("tensor", "sc", kqT_ready)
    P.wait("scalar", "ve", prep_ve)
    for ic in range(16):
        cs = ic % 2
        P.wait("sync", "tq", wgch_free[cs])
        P.ins("sync", "dma_start", out=wgch[cs][:],
              in_=wgt_sc[ic].rearrange("a e p t -> p a e t"), inc=(f"sg{cs}", 16))
        sdv = P.ins("sync", "dma_start", out=wuch[cs][:],
                    in_=wut_sc[ic].rearrange("a e p t -> p a e t"), inc=(f"sg{cs}", 16))
        P.wait("tensor", f"sg{cs}", sdv)
        for tt in range(NTT):
            ts_ = tt % 2
            P.wait("tensor", "sc", psg_free[ts_])
            P.wait("tensor", "ve", psu_free[ts_])
            for ech in range(16):
                P.ins("tensor", "matmul", ps_g[ts_][:], kqT[:, ech, tt, :],
                      wgch[cs][:, :, ech, :], start=(ech == 0), stop=(ech == 15))
                v = P.ins("tensor", "matmul", ps_u[ts_][:], kqT[:, ech, tt, :],
                          wuch[cs][:, :, ech, :], start=(ech == 0), stop=(ech == 15),
                          inc=("tq", 1) if ech == 15 else None)
            P.wait("scalar", "tq", v)
            if debug and ic == 0 and tt == 0:
                P.wait("scalar", "tq", v)
                P.ins("scalar", "activation", out=uF[:, 0:512], in_=ps_g[ts_][:],
                      func=AF.Copy, bias=0.0, scale=1.0)
                dbv = P.ins("scalar", "activation", out=uF[:, 512:1024], in_=ps_u[ts_][:],
                            func=AF.Copy, bias=0.0, scale=1.0, inc=("sc", 1))
                P.wait("gpsimd", "sc", dbv)
                P.ins("gpsimd", "dma_start", out=dbg_g[:, 0:1024], in_=uF[:, 0:1024],
                      inc=("gd", 16), allow_slow_non_contiguous=True)
                P.wait("gpsimd", "gd", P.val("gd"))
                zgate = P.bump("gpsimd", "gp", 1)
                P.wait("scalar", "gp", zgate)
            P.wait("scalar", "ve", zg_free[ts_])
            scv = P.ins("scalar", "activation", out=zg[ts_][:], in_=ps_g[ts_][:],
                        func=AF.Silu, bias=0.0, scale=agtab[:, tt:tt + 1], inc=("sc", 1))
            psg_free[ts_] = scv
            if debug and ic == 0 and tt == 0:
                P.wait("gpsimd", "sc", scv)
                P.ins("gpsimd", "dma_start", out=dbg_g[:, 1024:1536], in_=zg[ts_][:],
                      inc=("gd", 16), allow_slow_non_contiguous=True)
                P.wait("gpsimd", "gd", P.val("gd"))
                zgate = P.bump("gpsimd", "gp", 1)
                P.wait("vector", "gp", zgate)
            P.wait("vector", "sc", scv)
            P.wait("vector", f"gz{ts_}", zst_free[ts_])
            vev = P.ins("vector", "scalar_tensor_tensor", out=zst[ts_][:], in0=ps_u[ts_][:],
                        scalar=autab[:, tt:tt + 1], in1=zg[ts_][:],
                        op0=ALU.mult, op1=ALU.mult, inc=("ve", 1))
            psu_free[ts_] = vev
            zg_free[ts_] = vev
            P.wait("gpsimd", "ve", vev)
            zst_free[ts_] = P.ins("gpsimd", "dma_start",
                                  out=z_sc[tt, :, ic * 512:(ic + 1) * 512],
                                  in_=zst[ts_][:], inc=(f"gz{ts_}", 16))
        wgch_free[cs] = P.val("tq")
    z_done = (P.val("gz0"), P.val("gz1"))

    ph.close()
    P.barrier()

    ph = contextlib.ExitStack()
    Zrow = ph.enter_context(nc.sbuf_tensor("Zrow", [128, NI], F32))
    Zb = ph.enter_context(nc.sbuf_tensor("Zb", [128, NI], F32))
    kq2 = ph.enter_context(nc.sbuf_tensor("kq2", [128, NI], BF16))
    kq2T = ph.enter_context(nc.sbuf_tensor("kq2T", [128, 64, 2, 128], BF16))
    wdch = [ph.enter_context(nc.sbuf_tensor(f"wdch{i}", [128, 64, 128], BF16)) for i in range(2)]
    dstage = [ph.enter_context(nc.sbuf_tensor(f"dst{i}", [128, 256], F32)) for i in range(2)]
    dtok = [ph.enter_context(nc.sbuf_tensor(f"dtok{i}", [128, E], F32)) for i in range(2)]
    x1g = ph.enter_context(nc.sbuf_tensor("x1g", [128, E], F32))
    outst = [ph.enter_context(nc.sbuf_tensor(f"outst{i}", [128, E], F32)) for i in range(2)]
    amax2 = ph.enter_context(nc.sbuf_tensor("amax2", [128, 1], F32))
    clipm2 = ph.enter_context(nc.sbuf_tensor("clipm2", [128, 1], F32))
    recip2 = ph.enter_context(nc.sbuf_tensor("recip2", [128, 1], F32))
    s2v = ph.enter_context(nc.sbuf_tensor("s2v", [128, 1], F32))
    gam = ph.enter_context(nc.sbuf_tensor("gam", [128, 1], F32))
    invs2 = ph.enter_context(nc.sbuf_tensor("invs2", [128, 1], F32))
    bdt = ph.enter_context(nc.sbuf_tensor("bdt", [128, 2], F32))
    s1b = ph.enter_context(nc.sbuf_tensor("s1b", [128, 1], F32))
    meanb = ph.enter_context(nc.sbuf_tensor("meanb", [128, 1], F32))
    ssqb = ph.enter_context(nc.sbuf_tensor("ssqb", [128, 1], F32))
    rstdb = ph.enter_context(nc.sbuf_tensor("rstdb", [128, 1], F32))
    t_a2 = ph.enter_context(nc.sbuf_tensor("t_a2", [128, E], F32))
    t_b2 = ph.enter_context(nc.sbuf_tensor("t_b2", [128, E], F32))
    ps_f = [ph.enter_context(nc.psum_tensor(f"ps_f{i}", [128, 256], F32)) for i in range(2)]
    ps_t2 = [ph.enter_context(nc.psum_tensor(f"ps_t2{i}", [128, 512], F32)) for i in range(2)]
    ps_tq2 = [ph.enter_context(nc.psum_tensor(f"ps_tq2{i}", [128, 512], BF16)) for i in range(2)]

    zrow_free = 0
    zb_free = 0
    kq2_free = 0
    kq2T_free = 0
    pstq2_free = [0, 0]
    wdch_free = [0, 0]
    psf_free = [0, 0]
    dstage_free = [0, 0]
    pst2_free = [0, 0]
    dtok_free = [0, 0]
    x1g_free = 0
    outst_free = [0, 0]
    for grp in range(8):
        for t2 in range(2):
            tt = grp * 2 + t2
            P.wait("sync", "gz0", z_done[0])
            P.wait("sync", "gz1", z_done[1])
            P.wait("sync", "ve", zrow_free)
            sdv = P.ins("sync", "dma_start", out=Zrow[:], in_=z_sc[tt], inc=("sz", 16))
            P.wait("vector", "sz", sdv)
            P.wait("vector", "ve", zb_free)
            cur, oth = Zrow, Zb
            for st in range(13):
                hh = 1 << st
                cv = cur[:].rearrange("p (g two h) -> p g two h", two=2, h=hh)
                ov = oth[:].rearrange("p (g two h) -> p g two h", two=2, h=hh)
                P.ins("vector", "tensor_tensor", out=ov[:, :, 0, :], in0=cv[:, :, 0, :],
                      in1=cv[:, :, 1, :], op=ALU.add)
                P.ins("vector", "tensor_tensor", out=ov[:, :, 1, :], in0=cv[:, :, 0, :],
                      in1=cv[:, :, 1, :], op=ALU.subtract)
                cur, oth = oth, cur
            res = cur
            P.ins("vector", "tensor_reduce", out=amax2[:], in_=res[:], axis=AX.X,
                  op=ALU.max, apply_absolute_value=True)
            P.ins("vector", "drain")
            P.ins("vector", "tensor_scalar", out=clipm2[:], in0=amax2[:], scalar1=FWC,
                  scalar2=EPS, op0=ALU.mult, op1=ALU.max)
            P.ins("vector", "drain")
            P.ins("vector", "reciprocal", out=recip2[:], in_=clipm2[:])
            P.ins("vector", "drain")
            P.ins("vector", "tensor_scalar", out=s2v[:], in0=recip2[:], scalar1=128.0,
                  scalar2=None, op0=ALU.mult)
            P.ins("vector", "drain")
            P.ins("vector", "tensor_scalar", out=gam[:], in0=s2v[:], scalar1=FWC,
                  scalar2=None, op0=ALU.mult)
            P.ins("vector", "tensor_scalar", out=invs2[:], in0=clipm2[:], scalar1=P2M7,
                  scalar2=None, op0=ALU.mult)
            P.ins("vector", "drain")
            vev = P.ins("vector", "tensor_scalar", out=bdt[:, t2:t2 + 1], in0=invs2[:],
                        scalar1=mu_bc[:, 2:3], scalar2=None, op0=ALU.mult, inc=("ve", 1))
            if debug and tt == 0:
                P.wait("gpsimd", "ve", vev)
                P.ins("gpsimd", "dma_start", out=dbg_fw, in_=res[:], inc=("gd", 16))
                P.wait("gpsimd", "gd", P.val("gd"))
                f2g0 = P.bump("gpsimd", "gp", 1)
                P.wait("scalar", "gp", f2g0)
            P.wait("scalar", "ve", vev)
            scv = P.ins("scalar", "activation", out=Zrow[:], in_=res[:], func=AF.Copy,
                        bias=MAGIC, scale=gam[:], inc=("sc", 1))
            P.wait("vector", "sc", scv)
            vv = P.ins("vector", "tensor_scalar", out=Zb[:], in0=Zrow[:], scalar1=-MAGIC,
                       scalar2=-128.0, op0=ALU.add, op1=ALU.max, inc=("ve", 1))
            zrow_free = vv
            P.wait("vector", "tq", kq2_free)
            vev = P.ins("vector", "tensor_scalar", out=kq2[:], in0=Zb[:], scalar1=127.0,
                        scalar2=None, op0=ALU.min, inc=("ve", 1))
            zb_free = vev
            if debug and tt == 0:
                P.wait("gpsimd", "ve", vev)
                P.ins("gpsimd", "dma_start", out=dbg_k2, in_=kq2[:], inc=("gd", 16))
                P.wait("gpsimd", "gd", P.val("gd"))
                f2g = P.bump("gpsimd", "gp", 1)
                P.wait("tensor", "gp", f2g)
            P.wait("tensor", "ve", vev)
            P.wait("tensor", "tq", kq2T_free)
            for g in range(16):
                gs = (tt * 16 + g) % 2
                P.wait("tensor", "sc", pstq2_free[gs])
                for j in range(4):
                    ich = 4 * g + j
                    v = P.ins("tensor", "transpose", ps_tq2[gs][:, j * 128:(j + 1) * 128],
                              kq2[:, ich * 128:(ich + 1) * 128], ident16[:],
                              inc=("tq", 1) if j == 3 else None)
                P.wait("scalar", "tq", v)
                pstq2_free[gs] = P.ins("scalar", "activation",
                                       out=kq2T[:, 4 * g:4 * g + 4, t2, :],
                                       in_=ps_tq2[gs][:].rearrange("p (a b) -> p a b", a=4),
                                       func=AF.Copy, bias=0.0, scale=1.0, inc=("sc", 1))
            kq2_free = P.val("tq")
        kq2T_ready = P.val("sc")
        ve_q = P.val("ve")
        P.wait("tensor", "sc", kq2T_ready)
        P.wait("scalar", "ve", ve_q)
        for ft in range(16):
            fs = ft % 2
            P.wait("sync", "tq", wdch_free[fs])
            sdv = P.ins("sync", "dma_start", out=wdch[fs][:],
                        in_=wdt_sc[ft].rearrange("a p t -> p a t"), inc=(f"sg{fs}", 16))
            P.wait("tensor", f"sg{fs}", sdv)
            P.wait("tensor", "sc", psf_free[fs])
            for ich in range(64):
                v = P.ins("tensor", "matmul", ps_f[fs][:],
                          wdch[fs][:, ich, :], kq2T[:, ich, :, :].rearrange("p a t -> p (a t)"),
                          start=(ich == 0), stop=(ich == 63),
                          inc=("tq", 1) if ich == 63 else None)
            wdch_free[fs] = v
            P.wait("scalar", "tq", v)
            P.wait("scalar", "tq", dstage_free[fs])
            scv = P.ins("scalar", "activation", out=dstage[fs][:], in_=ps_f[fs][:],
                        func=AF.Copy, bias=0.0, scale=1.0, inc=("sc", 1))
            psf_free[fs] = scv
            P.wait("tensor", "sc", scv)
            P.wait("tensor", "sc", pst2_free[fs])
            for t2 in range(2):
                v = P.ins("tensor", "transpose", ps_t2[fs][:, t2 * 128:(t2 + 1) * 128],
                          dstage[fs][:, t2 * 128:(t2 + 1) * 128], ident32[:],
                          inc=("tq", 1) if t2 == 1 else None)
            dstage_free[fs] = v
            P.wait("scalar", "tq", v)
            for t2 in range(2):
                if ft == 0:
                    P.wait("scalar", "ve", dtok_free[t2])
                scv = P.ins("scalar", "activation",
                            out=dtok[t2][:, ft * 128:(ft + 1) * 128],
                            in_=ps_t2[fs][:, t2 * 128:(t2 + 1) * 128],
                            func=AF.Copy, bias=0.0, scale=bdt[:, t2:t2 + 1],
                            inc=("sc", 1))
            pst2_free[fs] = P.val("sc")
        down_done = P.val("sc")
        kq2T_free = P.val("tq")
        if debug and grp == 0:
            P.wait("gpsimd", "sc", down_done)
            P.ins("gpsimd", "dma_start", out=dbg_dt, in_=dtok[0][:], inc=("gd", 16))
            P.wait("gpsimd", "gd", P.val("gd"))
            f2g2 = P.bump("gpsimd", "gp", 1)
            P.wait("vector", "gp", f2g2)
        for t2 in range(2):
            tt = grp * 2 + t2
            P.wait("sync", "ve", x1g_free)
            sdv = P.ins("sync", "dma_start", out=x1g[:], in_=x1_sc[tt], inc=("sxt", 16))
            P.wait("vector", "sc", down_done)
            P.wait("vector", "sxt", sdv)
            vv = P.ins("vector", "tensor_tensor", out=t_b2[:], in0=dtok[t2][:],
                       in1=x1g[:], op=ALU.add, inc=("ve", 1))
            dtok_free[t2] = vv
            x1g_free = vv
            P.ins("vector", "tensor_reduce", out=s1b[:], in_=t_b2[:], axis=AX.X, op=ALU.add)
            P.ins("vector", "drain")
            P.ins("vector", "tensor_scalar", out=meanb[:], in0=s1b[:], scalar1=INV2048,
                  scalar2=None, op0=ALU.mult)
            P.ins("vector", "drain")
            vv = P.ins("vector", "tensor_scalar", out=t_a2[:], in0=t_b2[:], scalar1=meanb[:],
                       scalar2=None, op0=ALU.subtract, inc=("ve", 1))
            P.wait("scalar", "ve", vv)
            P.ins("scalar", "activation", out=t_b2[:], in_=t_a2[:], func=AF.Square,
                  bias=0.0, scale=1.0, accum_out=ssqb[:])
            P.ins("scalar", "drain")
            scv = P.ins("scalar", "activation", out=rstdb[:], in_=ssqb[:], func=AF.Sqrt,
                        bias=eps_col[:], scale=INV2048, inc=("sc", 1))
            P.wait("vector", "sc", scv)
            P.ins("vector", "reciprocal", out=meanb[:], in_=rstdb[:])
            P.ins("vector", "drain")
            P.wait("vector", f"so{t2}", outst_free[t2])
            vv = P.ins("vector", "tensor_scalar", out=outst[t2][:], in0=t_a2[:],
                       scalar1=meanb[:], scalar2=None, op0=ALU.mult, inc=("ve", 1))
            P.wait("gpsimd", "ve", vv)
            outst_free[t2] = P.ins("gpsimd", "dma_start",
                                   out=y_d[tt * 128:(tt + 1) * 128, :], in_=outst[t2][:],
                                   inc=(f"so{t2}", 16))

    P.wait("gpsimd", "so0", P.val("so0"))
    P.wait("gpsimd", "so1", P.val("so1"))

    with nc.Block() as block:
        for eng in ENGS:
            getattr(block, eng)(lambda e, ops=P.ops[eng]: [f(e) for f in ops])
    ph.close()
    es.close()
    return nc


def kernel(L0_attn_x, L0_attn_w_qkv, L0_attn_w_out, L0_attn_b_out,
           L0_ffn_w_gate, L0_ffn_w_up, L0_ffn_w_down):
    nc = build()
    x = np.ascontiguousarray(np.asarray(L0_attn_x, dtype=np.float32))
    wqkv = np.ascontiguousarray(np.asarray(L0_attn_w_qkv, dtype=np.float32))
    wout = np.ascontiguousarray(np.asarray(L0_attn_w_out, dtype=np.float32))
    bout = np.ascontiguousarray(np.asarray(L0_attn_b_out, dtype=np.float32))
    wg = np.ascontiguousarray(np.asarray(L0_ffn_w_gate, dtype=np.float32))
    wu = np.ascontiguousarray(np.asarray(L0_ffn_w_up, dtype=np.float32))
    wd = np.ascontiguousarray(np.asarray(L0_ffn_w_down, dtype=np.float32))
    in_maps = []
    for c in range(8):
        b, qr = c // 4, c % 4
        in_maps.append({
            "x": np.ascontiguousarray(x[b, qr * T:(qr + 1) * T, :]),
            "wqkv": wqkv, "wout": wout, "bout": bout,
            "wg": wg, "wu": wu, "wd": wd,
        })
    res = run_bass_kernel_spmd(nc, in_maps, core_ids=list(range(8)))
    out = np.empty((2, SEQ, E), np.float32)
    for c in range(8):
        b, qr = c // 4, c % 4
        out[b, qr * T:(qr + 1) * T, :] = res.results[c]["y"]
    return out


# version 06
# speedup vs baseline: 1.1020x; 1.1020x over previous
import collections
import contextlib

import numpy as np

import concourse.bass as bass
import concourse.mybir as mybir
from concourse.bass_utils import run_bass_kernel_spmd

F32 = mybir.dt.float32
BF16 = mybir.dt.bfloat16
AF = mybir.ActivationFunctionType
ALU = mybir.AluOpType
AX = mybir.AxisListType

E = 2048
H = 16
DH = 128
QKVF = 6144
NI = 8192
T = 2048
NTT = 16
NTC = 4
SEQ = 8192
NKB = 64
NCH = 16
NRANK = 4

MAGIC = float(np.float32(1.5 * 2 ** 23))
_SC = 1.0 / float(np.float32(np.sqrt(128.0)))
SC_HI = float(np.float32(_SC))
SC_LO = float(np.float32(_SC - SC_HI))
FWC = float(np.float32(1.0 / np.sqrt(8192.0)))
EPS = 1e-5
INV2048 = 1.0 / 2048.0
P2M7 = 2.0 ** -7
P2M24 = 2.0 ** -24

ENGS = ("sync", "gpsimd", "vector", "scalar", "tensor")


class Prog:
    def __init__(self, nc, es):
        self.nc = nc
        self.es = es
        self.ops = {e: [] for e in ENGS}
        self.cnt = collections.Counter()
        self.sems = {}
        self.bar_k = 0

    def sem(self, name):
        if name not in self.sems:
            self.sems[name] = self.es.enter_context(self.nc.semaphore(name))
        return self.sems[name]

    def val(self, name):
        return self.cnt[name]

    def emit(self, eng, fn):
        self.ops[eng].append(fn)

    def wait(self, eng, name, v):
        if v is None or v <= 0:
            return
        s = self.sem(name)
        self.emit(eng, lambda e, s=s, v=v: e.wait_ge(s, v))

    def ins(self, eng, meth, *args, inc=None, **kw):
        if inc is None:
            self.emit(eng, lambda e, m=meth, a=args, k=kw: getattr(e, m)(*a, **k))
            return None
        name, n = inc
        s = self.sem(name)
        self.cnt[name] += n
        v = self.cnt[name]
        self.emit(eng, lambda e, m=meth, a=args, k=kw, s=s, n=n: getattr(e, m)(*a, **k).then_inc(s, n))
        return v

    def bump(self, eng, name, n=1):
        s = self.sem(name)
        self.cnt[name] += n
        v = self.cnt[name]
        self.emit(eng, lambda e, s=s, n=n: e.sem_inc(s, n))
        return v

    def barrier(self):
        self.bar_k += 1
        for name, v in list(self.cnt.items()):
            if name.startswith("s") and name not in ("sc", "so0", "so1"):
                self.wait("sync", name, v)
            elif name.startswith("g") and name != "gp":
                self.wait("gpsimd", name, v)
            elif name in ("so0", "so1"):
                self.wait("gpsimd", name, v)
        for eng in ENGS:
            self.bump(eng, "bar", 1)
        tgt = 5 * self.bar_k
        for eng in ENGS:
            self.wait(eng, "bar", tgt)


def build(debug=False):
    nc = bass.Bass("TRN2", target_bir_lowering=False)
    sck = "ExternalOutput" if debug else "Internal"

    x_d = nc.dram_tensor("x", [T, E], F32, kind="ExternalInput").ap()
    wqkv_d = nc.dram_tensor("wqkv", [QKVF, E], F32, kind="ExternalInput").ap()
    wout_d = nc.dram_tensor("wout", [E, E], F32, kind="ExternalInput").ap()
    bout_d = nc.dram_tensor("bout", [E], F32, kind="ExternalInput").ap()
    wg_d = nc.dram_tensor("wg", [NI, E], F32, kind="ExternalInput").ap()
    wu_d = nc.dram_tensor("wu", [NI, E], F32, kind="ExternalInput").ap()
    wd_d = nc.dram_tensor("wd", [E, NI], F32, kind="ExternalInput").ap()
    y_d = nc.dram_tensor("y", [T, E], F32, kind="ExternalOutput").ap()

    q_sc = nc.dram_tensor("q_sc", [H, DH, T], F32, kind=sck).ap()
    agin = nc.dram_tensor("agin", [2, H, DH, T], F32).ap()
    agout = nc.dram_tensor("agout", [NRANK, 2, H, DH, T], F32).ap()
    ao_sc = nc.dram_tensor("ao_sc", [NTT, H, DH, 128], F32, kind=sck).ap()
    x1_sc = nc.dram_tensor("x1_sc", [NTT, 128, E], F32, kind=sck).ap()
    z_sc = nc.dram_tensor("z_sc", [NTT, 128, NI], F32, kind=sck).ap()
    wgt_sc = nc.dram_tensor("wgt_sc", [16, 4, 16, 128, 128], BF16, kind=sck).ap()
    wut_sc = nc.dram_tensor("wut_sc", [16, 4, 16, 128, 128], BF16, kind=sck).ap()
    wdt_sc = nc.dram_tensor("wdt_sc", [16, 64, 128, 128], BF16, kind=sck).ap()
    if debug:
        dbg_p = nc.dram_tensor("dbg_p", [128, SEQ], F32, kind="ExternalOutput").ap()
        dbg_tab = nc.dram_tensor("dbg_tab", [128, 4 * 64 + 2 * 16 + 4], F32, kind="ExternalOutput").ap()
        dbg_kq = nc.dram_tensor("dbg_kq", [128, E], BF16, kind="ExternalOutput").ap()
        dbg_ffn = nc.dram_tensor("dbg_ffn", [128, 64], F32, kind="ExternalOutput").ap()
        dbg_acc = nc.dram_tensor("dbg_acc", [128, 64], F32, kind="ExternalOutput").ap()
        dbg_g = nc.dram_tensor("dbg_g", [128, 1536], F32, kind="ExternalOutput").ap()
        dbg_fw = nc.dram_tensor("dbg_fw", [128, NI], F32, kind="ExternalOutput").ap()
        dbg_k2 = nc.dram_tensor("dbg_k2", [128, NI], BF16, kind="ExternalOutput").ap()
        dbg_dt = nc.dram_tensor("dbg_dt", [128, E], F32, kind="ExternalOutput").ap()
    dbg_gate = [0]

    es = contextlib.ExitStack()
    P = Prog(nc, es)
    for s in ("sd", "gd", "so", "cc", "bar", "tq", "tp", "tv", "sc", "ve", "gp"):
        P.sem(s)

    ident32 = es.enter_context(nc.sbuf_tensor("ident32", [128, 128], F32))
    ident16 = es.enter_context(nc.sbuf_tensor("ident16", [128, 128], BF16))
    ones_col = es.enter_context(nc.sbuf_tensor("ones_col", [128, 1], F32))
    ones_row = es.enter_context(nc.sbuf_tensor("ones_row", [1, 128], F32))
    mu_bc = es.enter_context(nc.sbuf_tensor("mu_bc", [128, 4], F32))
    agtab = es.enter_context(nc.sbuf_tensor("agtab", [128, NTT], F32))
    eps_col = es.enter_context(nc.sbuf_tensor("eps_col", [128, 1], F32))
    autab = es.enter_context(nc.sbuf_tensor("autab", [128, NTT], F32))

    P.ins("gpsimd", "memset", ident32[:], 0.0)
    P.ins("gpsimd", "affine_select", out=ident32[:], in_=ident32[:],
          compare_op=ALU.not_equal, fill=1.0, base=0, pattern=[[-1, 128]],
          channel_multiplier=1)
    P.ins("gpsimd", "memset", ident16[:], 0.0)
    P.ins("gpsimd", "affine_select", out=ident16[:], in_=ident16[:],
          compare_op=ALU.not_equal, fill=1.0, base=0, pattern=[[-1, 128]],
          channel_multiplier=1)
    P.ins("gpsimd", "memset", ones_col[:], 1.0)
    P.ins("gpsimd", "memset", eps_col[:], EPS)
    gp_const = P.ins("gpsimd", "memset", ones_row[:], 1.0, inc=("gp", 1))

    ph0 = contextlib.ExitStack()
    xraw = [ph0.enter_context(nc.sbuf_tensor(f"xraw{i}", [128, E], F32)) for i in range(2)]
    wqraw = [ph0.enter_context(nc.sbuf_tensor(f"wqraw{i}", [128, E], F32)) for i in range(2)]
    wqT = [ph0.enter_context(nc.sbuf_tensor(f"wqT{i}", [128, 16, 128], F32)) for i in range(2)]
    qstage = [ph0.enter_context(nc.sbuf_tensor(f"qstage{i}", [128, 512], F32)) for i in range(2)]
    qs1 = ph0.enter_context(nc.sbuf_tensor("qs1", [128, 512], F32))
    qs2 = [ph0.enter_context(nc.sbuf_tensor(f"qs2{i}", [128, 512], F32)) for i in range(2)]
    ps_tp = [ph0.enter_context(nc.psum_tensor(f"ps_tp{i}", [128, 512], F32)) for i in range(2)]
    ps_mm = [ph0.enter_context(nc.psum_tensor(f"ps_mm{i}", [128, 512], F32)) for i in range(2)]
    ps_sm = ph0.enter_context(nc.psum_tensor("ps_sm", [128, 2], F32))
    ps_tpb = [ph0.enter_context(nc.psum_tensor(f"ps_tpb{i}", [128, 512], BF16)) for i in range(2)]
    wqvT = ph0.enter_context(nc.sbuf_tensor("wqvT", [128, 16, 256], F32))

    phx = contextlib.ExitStack()
    xT = phx.enter_context(nc.sbuf_tensor("xT", [128, 16, T], F32))

    xraw_free = [0, 0]
    pstp_free = [0, 0]
    P.wait("tensor", "gp", gp_const)
    for tt in range(NTT):
        sl = tt % 2
        P.wait("sync", "tq", xraw_free[sl])
        sdv = P.ins("sync", "dma_start", out=xraw[sl][:], in_=x_d[tt * 128:(tt + 1) * 128, :],
                    inc=(f"sx{sl}", 16))
        P.wait("tensor", f"sx{sl}", sdv)
        for g in range(4):
            gs = (tt * 4 + g) % 2
            P.wait("tensor", "sc", pstp_free[gs])
            for j in range(4):
                ech = 4 * g + j
                v = P.ins("tensor", "transpose", ps_tp[gs][:, j * 128:(j + 1) * 128],
                          xraw[sl][:, ech * 128:(ech + 1) * 128], ident32[:],
                          inc=("tq", 1) if j == 3 else None)
            P.wait("scalar", "tq", v)
            pstp_free[gs] = P.ins("scalar", "activation",
                                  out=xT[:, 4 * g:4 * g + 4, tt * 128:(tt + 1) * 128],
                                  in_=ps_tp[gs][:].rearrange("p (a b) -> p a b", a=4),
                                  func=AF.Copy, bias=0.0, scale=1.0, inc=("sc", 1))
        xraw_free[sl] = P.val("tq")
    sc_xT_done = P.val("sc")

    wqraw_free = [0, 0]
    qstage_free = [0, 0]
    qstage_free_sem = ["gd", "gd"]
    qs2_free = [0, 0]
    psmm_free = [0, 0]
    idx_c = [0]
    ms_c = [0]

    P.wait("tensor", "sc", sc_xT_done)

    def emit_w_transpose(ft, dst_ap_fn):
        sl = idx_c[0] % 2
        idx_c[0] += 1
        P.wait("sync", "tq", wqraw_free[sl])
        sdv = P.ins("sync", "dma_start", out=wqraw[sl][:],
                    in_=wqkv_d[ft * 128:(ft + 1) * 128, :], inc=(f"sw{sl}", 16))
        P.wait("tensor", f"sw{sl}", sdv)
        for g in range(4):
            gs = (idx_c[0] * 4 + g) % 2
            P.wait("tensor", "sc", pstp_free[gs])
            for j in range(4):
                ech = 4 * g + j
                v = P.ins("tensor", "transpose", ps_tp[gs][:, j * 128:(j + 1) * 128],
                          wqraw[sl][:, ech * 128:(ech + 1) * 128], ident32[:],
                          inc=("tq", 1) if j == 3 else None)
            P.wait("scalar", "tq", v)
            pstp_free[gs] = P.ins("scalar", "activation",
                                  out=dst_ap_fn(g),
                                  in_=ps_tp[gs][:].rearrange("p (a b) -> p a b", a=4),
                                  func=AF.Copy, bias=0.0, scale=1.0, inc=("sc", 1))
        wqraw_free[sl] = P.val("tq")
        return P.val("sc")

    def emit_kq_tile(ft):
        sl2 = idx_c[0] % 2
        wqT_ready = emit_w_transpose(ft, lambda g, sl2=sl2: wqT[sl2][:, 4 * g:4 * g + 4, :])
        P.wait("tensor", "sc", wqT_ready)
        for tc in range(NTC):
            ms = ms_c[0] % 2
            ms_c[0] += 1
            P.wait("tensor", "sc", psmm_free[ms])
            for ech in range(16):
                v = P.ins("tensor", "matmul", ps_mm[ms][:],
                          wqT[sl2][:, ech, :], xT[:, ech, tc * 512:(tc + 1) * 512],
                          start=(ech == 0), stop=(ech == 15),
                          inc=("tq", 1) if ech == 15 else None)
            P.wait("scalar", "tq", v)
            P.wait("scalar", qstage_free_sem[ms], qstage_free[ms])
            scv = P.ins("scalar", "activation", out=qstage[ms][:], in_=ps_mm[ms][:],
                        func=AF.Copy, bias=0.0, scale=1.0, inc=("sc", 1))
            psmm_free[ms] = scv
            if ft < 16:
                P.wait("vector", "sc", scv)
                P.ins("vector", "tensor_scalar", out=qs1[:], in0=qstage[ms][:],
                      scalar1=SC_HI, scalar2=None, op0=ALU.mult)
                P.wait("vector", f"gq{ms}", qs2_free[ms])
                vev = P.ins("vector", "scalar_tensor_tensor", out=qs2[ms][:],
                            in0=qstage[ms][:], scalar=SC_LO, in1=qs1[:],
                            op0=ALU.mult, op1=ALU.add, inc=("ve", 1))
                qstage_free[ms] = vev
                qstage_free_sem[ms] = "ve"
                P.wait("gpsimd", "ve", vev)
                qs2_free[ms] = P.ins("gpsimd", "dma_start",
                                     out=q_sc[ft, :, tc * 512:(tc + 1) * 512],
                                     in_=qs2[ms][:], inc=(f"gq{ms}", 16))
            else:
                P.wait("gpsimd", "sc", scv)
                dst = agin[0, ft - 16, :, tc * 512:(tc + 1) * 512]
                qstage_free[ms] = P.ins("gpsimd", "dma_start", out=dst, in_=qstage[ms][:],
                                        inc=(f"gk{ms}", 16))
                qstage_free_sem[ms] = f"gk{ms}"

    for ft in range(16, 32):
        emit_kq_tile(ft)
    agin_v = agin[1].rearrange("h d t -> (h d) t")
    for vg in range(8):
        wqv_ready = 0
        for kk in range(2):
            ft = 32 + 2 * vg + kk
            wqv_ready = emit_w_transpose(
                ft, lambda g, kk=kk: wqvT[:, 4 * g:4 * g + 4, kk * 128:(kk + 1) * 128])
        P.wait("tensor", "sc", wqv_ready)
        for tt in range(NTT):
            ms = ms_c[0] % 2
            ms_c[0] += 1
            P.wait("tensor", "sc", psmm_free[ms])
            for ech in range(16):
                v = P.ins("tensor", "matmul", ps_mm[ms][:, 0:256],
                          xT[:, ech, tt * 128:(tt + 1) * 128],
                          wqvT[:, ech, :].rearrange("p a -> p a"),
                          start=(ech == 0), stop=(ech == 15),
                          inc=("tq", 1) if ech == 15 else None)
            P.wait("scalar", "tq", v)
            P.wait("scalar", qstage_free_sem[ms], qstage_free[ms])
            scv = P.ins("scalar", "activation", out=qstage[ms][:, 0:256],
                        in_=ps_mm[ms][:, 0:256],
                        func=AF.Copy, bias=0.0, scale=1.0, inc=("sc", 1))
            psmm_free[ms] = scv
            P.wait("gpsimd", "sc", scv)
            qstage_free[ms] = P.ins("gpsimd", "dma_start",
                                    out=agin_v[tt * 128:(tt + 1) * 128,
                                               vg * 256:(vg + 1) * 256],
                                    in_=qstage[ms][:, 0:256], inc=(f"gk{ms}", 16))
            qstage_free_sem[ms] = f"gk{ms}"
    P.wait("gpsimd", "gk0", P.val("gk0"))
    P.wait("gpsimd", "gk1", P.val("gk1"))
    P.ins("gpsimd", "collective_compute",
          "AllGather", ALU.bypass,
          replica_groups=[[0, 1, 2, 3], [4, 5, 6, 7]],
          ins=[agin.opt()], outs=[agout.opt()], inc=("cc", 1))
    for ft in range(16):
        emit_kq_tile(ft)
    q_sc_done = (P.val("gq0"), P.val("gq1"))
    tq_c_done = P.val("tq")
    phx.close()
    P.barrier()

    wtile = [ph0.enter_context(nc.sbuf_tensor(f"wtile{i}", [128, E], F32)) for i in range(2)]
    acc = [ph0.enter_context(nc.sbuf_tensor(f"acc{i}", [128, 1], F32)) for i in range(2)]
    tmp1 = ph0.enter_context(nc.sbuf_tensor("tmp1", [128, 1], F32))
    tot = ph0.enter_context(nc.sbuf_tensor("tot", [1, 2], F32))
    pairq = ph0.enter_context(nc.sbuf_tensor("pairq", [1, 2], F32))
    uq2 = [ph0.enter_context(nc.sbuf_tensor(f"uq{i}", [128, E], F32)) for i in range(2)]
    v1q = ph0.enter_context(nc.sbuf_tensor("v1q", [128, E], F32))
    tq_ = [ph0.enter_context(nc.sbuf_tensor(f"tq{i}", [128, E], BF16)) for i in range(2)]
    tstage = [ph0.enter_context(nc.sbuf_tensor(f"tstage{i}", [128, 512], BF16)) for i in range(2)]
    swmu = ph0.enter_context(nc.sbuf_tensor("swmu", [128, 2], F32))

    first_c2 = [True]
    pstpb_free = [0, 0]
    wtile_free = [0, 0]
    wtile_free_sem = ["ve", "ve"]
    tstage_free = [0, 0]
    tqbuf_free = [0, 0]

    def quant_matrix(mi, src_d, dst_sc, kind):
        ntile = 64

        def tile_src(t):
            if kind == "gu":
                return src_d[t * 128:(t + 1) * 128, :]
            ftl, ip = t // 4, t % 4
            return src_d[ftl * 128:(ftl + 1) * 128, ip * E:(ip + 1) * E]

        for t in range(ntile):
            sl = t % 2
            if first_c2[0]:
                P.wait("sync", "tq", tq_c_done)
                first_c2[0] = False
            P.wait("sync", wtile_free_sem[sl], wtile_free[sl])
            sdv = P.ins("sync", "dma_start", out=wtile[sl][:], in_=tile_src(t),
                        inc=(f"sw{sl}", 16))
            P.wait("vector", f"sw{sl}", sdv)
            vv = P.ins("vector", "tensor_reduce", out=tmp1[:], in_=wtile[sl][:],
                       axis=AX.X, op=ALU.add, apply_absolute_value=True, inc=("ve", 1))
            wtile_free[sl] = vv
            wtile_free_sem[sl] = "ve"
            P.ins("vector", "drain")
            if debug and mi == 0:
                P.wait("gpsimd", "ve", vv)
                gdbg = P.ins("gpsimd", "dma_start", out=dbg_acc[:, t:t + 1], in_=tmp1[:],
                             inc=("gd", 16), allow_slow_non_contiguous=True)
                P.wait("vector", "gd", gdbg)
            if t == 0:
                P.ins("vector", "tensor_copy", acc[0][:], tmp1[:])
            else:
                P.ins("vector", "tensor_tensor", out=acc[t % 2][:], in0=acc[(t - 1) % 2][:],
                      in1=tmp1[:], op=ALU.add)
            P.ins("vector", "drain")
        vv = P.bump("vector", "ve", 1)
        if debug and mi == 0:
            P.wait("gpsimd", "ve", vv)
            P.ins("gpsimd", "dma_start", out=dbg_ffn[:, 36:37], in_=acc[(ntile - 1) % 2][:],
                  inc=("gd", 16), allow_slow_non_contiguous=True)
        P.wait("tensor", "ve", vv)
        txv = P.ins("tensor", "matmul", ps_sm[0:1, 0:1], acc[(ntile - 1) % 2][:], ones_col[:],
                    start=True, stop=True, inc=("tq", 1))
        P.wait("scalar", "tq", txv)
        scv = P.ins("scalar", "activation", out=tot[:, 0:1], in_=ps_sm[0:1, 0:1],
                    func=AF.Copy, bias=0.0, scale=1.0, inc=("sc", 1))
        if debug and mi == 0:
            P.wait("gpsimd", "sc", scv)
            P.ins("gpsimd", "dma_start", out=dbg_ffn[0:1, 38:39], in_=tot[:, 0:1],
                  inc=("gd", 16), allow_slow_non_contiguous=True)
        P.wait("vector", "sc", scv)
        P.ins("vector", "tensor_scalar", out=tot[:, 1:2], in0=tot[:, 0:1],
              scalar1=P2M24, scalar2=EPS, op0=ALU.mult, op1=ALU.max)
        P.ins("vector", "drain")
        P.ins("vector", "reciprocal", out=pairq[:, 0:1], in_=tot[:, 1:2])
        P.ins("vector", "drain")
        vv = P.ins("vector", "tensor_copy", pairq[:, 1:2], tot[:, 1:2], inc=("ve", 1))
        P.wait("tensor", "ve", vv)
        txv = P.ins("tensor", "matmul", ps_sm[:, 0:2], ones_row[:], pairq[:],
                    start=True, stop=True, inc=("tq", 1))
        P.wait("scalar", "tq", txv)
        swmu_ready = P.ins("scalar", "activation", out=swmu[:], in_=ps_sm[:, 0:2],
                           func=AF.Copy, bias=0.0, scale=1.0, inc=("sc", 1))
        P.ins("scalar", "drain")
        P.wait("vector", "sc", swmu_ready)
        P.ins("vector", "tensor_copy", mu_bc[:, mi:mi + 1], swmu[:, 1:2])

        for t in range(ntile):
            sl = t % 2
            P.wait("sync", wtile_free_sem[sl], wtile_free[sl])
            sdv = P.ins("sync", "dma_start", out=wtile[sl][:], in_=tile_src(t),
                        inc=(f"sw{sl}", 16))
            P.wait("scalar", f"sw{sl}", sdv)
            scv = P.ins("scalar", "activation", out=uq2[sl][:], in_=wtile[sl][:],
                        func=AF.Copy, bias=MAGIC, scale=swmu[:, 0:1], inc=("sc", 1))
            wtile_free[sl] = scv
            wtile_free_sem[sl] = "sc"
            P.wait("vector", "sc", scv)
            P.ins("vector", "tensor_scalar", out=v1q[:], in0=uq2[sl][:],
                  scalar1=-MAGIC, scalar2=-1.0, op0=ALU.add, op1=ALU.max)
            P.wait("vector", "tq", tqbuf_free[sl])
            vv = P.ins("vector", "tensor_scalar", out=tq_[sl][:], in0=v1q[:],
                       scalar1=1.0, scalar2=None, op0=ALU.min, inc=("ve", 1))
            P.wait("tensor", "ve", vv)
            for g in range(4):
                gs = (t * 4 + g) % 2
                P.wait("tensor", "sc", pstpb_free[gs])
                for j in range(4):
                    ech = 4 * g + j
                    v = P.ins("tensor", "transpose", ps_tpb[gs][:, j * 128:(j + 1) * 128],
                              tq_[sl][:, ech * 128:(ech + 1) * 128], ident16[:],
                              inc=("tq", 1) if j == 3 else None)
                P.wait("scalar", "tq", v)
                P.wait("scalar", f"gt{gs}", tstage_free[gs])
                scv = P.ins("scalar", "activation", out=tstage[gs][:],
                            in_=ps_tpb[gs][:], func=AF.Copy, bias=0.0, scale=1.0,
                            inc=("sc", 1))
                pstpb_free[gs] = scv
                P.wait("gpsimd", "sc", scv)
                if kind == "gu":
                    dst = dst_sc[t // 4, t % 4, 4 * g:4 * g + 4].rearrange("a p t -> p a t")
                else:
                    ftl, ip = t // 4, t % 4
                    dst = dst_sc[ftl, ip * 16 + 4 * g: ip * 16 + 4 * g + 4].rearrange("a p t -> p a t")
                tstage_free[gs] = P.ins("gpsimd", "dma_start",
                                        out=dst,
                                        in_=tstage[gs][:].rearrange("p (a b) -> p a b", a=4),
                                        inc=(f"gt{gs}", 16))
            tqbuf_free[sl] = P.val("tq")

    quant_matrix(0, wg_d, wgt_sc, "gu")
    quant_matrix(1, wu_d, wut_sc, "gu")
    quant_matrix(2, wd_d, wdt_sc, "d")

    ph0.close()
    P.barrier()

    ph = contextlib.ExitStack()
    Kh = [ph.enter_context(nc.sbuf_tensor(f"Kh{i}", [128, SEQ], F32)) for i in range(2)]
    Vtok = [ph.enter_context(nc.sbuf_tensor(f"Vtok{i}", [128, NKB, 128], F32)) for i in range(2)]
    Qh = ph.enter_context(nc.sbuf_tensor("Qh", [128, T], F32))
    Pbuf = ph.enter_context(nc.sbuf_tensor("Pbuf", [128, SEQ], F32))
    P2 = [ph.enter_context(nc.sbuf_tensor(f"P2{i}", [128, T], F32)) for i in range(4)]
    PT = [ph.enter_context(nc.sbuf_tensor(f"PT{i}", [128, 512], F32)) for i in range(2)]
    attst0 = ph.enter_context(nc.sbuf_tensor("attst0", [128, 128], F32))
    attst = [attst0, attst0]
    mtab = ph.enter_context(nc.sbuf_tensor("mtab", [128, 64], F32))
    btab = ph.enter_context(nc.sbuf_tensor("btab", [128, 16], F32))
    ttab = ph.enter_context(nc.sbuf_tensor("ttab", [128, 64], F32))
    arg1 = ph.enter_context(nc.sbuf_tensor("arg1", [128, 16], F32))
    e1t = ph.enter_context(nc.sbuf_tensor("e1t", [128, 16], F32))
    wtab = ph.enter_context(nc.sbuf_tensor("wtab", [128, 64], F32))
    d2t = ph.enter_context(nc.sbuf_tensor("d2t", [128, 64], F32))
    e2t = ph.enter_context(nc.sbuf_tensor("e2t", [128, 64], F32))
    d1t = wtab
    g1t = d2t
    g2t = wtab
    Mv = ph.enter_context(nc.sbuf_tensor("Mv", [128, 1], F32))
    Dv = ph.enter_context(nc.sbuf_tensor("Dv", [128, 1], F32))
    Dinv = ph.enter_context(nc.sbuf_tensor("Dinv", [128, 1], F32))
    ps_s = [ph.enter_context(nc.psum_tensor(f"ps_s{i}", [128, 512], F32)) for i in range(4)]
    ps_pt = [ph.enter_context(nc.psum_tensor(f"ps_pt{i}", [128, 512], F32)) for i in range(2)]
    ps_pv = [ph.enter_context(nc.psum_tensor(f"ps_pv{i}", [128, 128], F32)) for i in range(2)]

    kh_free = [0, 0]
    kh_ready = [0, 0]
    vt_free = [0, 0]
    vt_ready = [0, 0]
    qh_ready = [0]
    pss_free = [0, 0, 0, 0]
    pspt_free = [0, 0]
    ptbuf_free = [0, 0]
    pspv_free = [0, 0]
    attst_free = [0, 0]

    agout_v = [agout[r, 1].rearrange("h d t -> (h d) t") for r in range(NRANK)]

    def kv_loads(h):
        sl = h % 2
        P.wait("sync", "cc", 1)
        P.wait("sync", "tq", kh_free[sl])
        for r in range(NRANK):
            v = P.ins("sync", "dma_start", out=Kh[sl][:, r * T:(r + 1) * T],
                      in_=agout[r, 0, h], inc=(f"sk{sl}", 16))
        kh_ready[sl] = v
        P.wait("sync", "tv", vt_free[sl])
        for r in range(NRANK):
            for b4 in range(4):
                v = P.ins("sync", "dma_start",
                          out=Vtok[sl][:, r * 16 + b4 * 4: r * 16 + b4 * 4 + 4, :],
                          in_=agout_v[r][b4 * 512:(b4 + 1) * 512,
                                         h * 128:(h + 1) * 128]
                          .rearrange("(kb p) d -> p kb d", p=128),
                          inc=(f"sv{sl}", 16))
        vt_ready[sl] = v

    def q_load(h, after_tq):
        P.wait("sync", "gq0", q_sc_done[0])
        P.wait("sync", "gq1", q_sc_done[1])
        P.wait("sync", "tq", after_tq)
        qh_ready[0] = P.ins("sync", "dma_start", out=Qh[:], in_=q_sc[h], inc=("sq0", 16))

    def emit_qkt(h, qt, gq):
        sl = h % 2
        P.wait("tensor", f"sk{sl}", kh_ready[sl])
        P.wait("tensor", "sq0", qh_ready[0])
        for ch in range(NCH):
            cs = (gq * 16 + ch) % 4
            P.wait("tensor", "sc", pss_free[cs])
            tqv = P.ins("tensor", "matmul", ps_s[cs][:],
                        Qh[:, qt * 128:(qt + 1) * 128],
                        Kh[sl][:, ch * 512:(ch + 1) * 512],
                        start=True, stop=True, inc=("tq", 1))
            P.wait("vector", "tq", tqv)
            vev = P.ins("vector", "tensor_reduce", out=mtab[:, ch * 4:ch * 4 + 4],
                        in_=ps_s[cs][:].rearrange("p (a b) -> p a b", a=4),
                        axis=AX.X, op=ALU.max, negate=True, inc=("ve", 1))
            P.wait("scalar", "ve", vev)
            if gq > 0 and ch % 4 == 0:
                P.wait("scalar", "vsl", 4 * (gq - 1) + ch // 4 + 1)
            for j in range(4):
                scv = P.ins("scalar", "activation",
                            out=Pbuf[:, ch * 512 + j * 128: ch * 512 + (j + 1) * 128],
                            in_=ps_s[cs][:, j * 128:(j + 1) * 128],
                            func=AF.Exp, bias=mtab[:, ch * 4 + j: ch * 4 + j + 1],
                            scale=1.0,
                            accum_out=ttab[:, ch * 4 + j: ch * 4 + j + 1],
                            inc=("sc", 1) if j == 3 else None)
            pss_free[cs] = scv
        return P.val("sc")

    def emit_tables(h, qt):
        P.ins("vector", "tensor_reduce", out=Mv[:], in_=mtab[:], axis=AX.X,
              op=ALU.min, negate=True)
        P.ins("vector", "drain")
        vev = P.ins("vector", "tensor_scalar", out=e2t[:], in0=mtab[:],
                    scalar1=Mv[:], scalar2=-1.0, op0=ALU.add, op1=ALU.mult,
                    inc=("ve", 1))
        P.ins("vector", "drain")
        P.wait("scalar", "ve", vev)
        scv = P.ins("scalar", "activation", out=d2t[:], in_=e2t[:], func=AF.Exp,
                    bias=0.0, scale=1.0, inc=("sc", 1))
        P.wait("vector", "sc", scv)
        P.ins("vector", "tensor_tensor", out=wtab[:], in0=ttab[:], in1=d2t[:],
              op=ALU.mult)
        P.ins("vector", "drain")
        P.ins("vector", "tensor_reduce", out=Dv[:], in_=wtab[:], axis=AX.X, op=ALU.add)
        P.ins("vector", "drain")
        P.ins("vector", "reciprocal", out=Dinv[:], in_=Dv[:])
        P.ins("vector", "drain")
        P.ins("vector", "tensor_scalar", out=e2t[:], in0=wtab[:], scalar1=Dinv[:],
              scalar2=None, op0=ALU.mult)
        P.ins("vector", "drain")

    def emit_scales(h, qt, gq, exps_done):
        for r in range(4):
            kq = 4 * gq + r
            slot = kq % 4
            P.wait("vector", "sc", exps_done)
            if kq >= 4:
                P.wait("vector", "tpq", kq - 3)
            P.ins("vector", "tensor_tensor",
                  out=P2[slot][:].rearrange("p (c j) -> p c j", c=16),
                  in0=Pbuf[:, r * T:(r + 1) * T].rearrange("p (c j) -> p c j", c=16),
                  in1=e2t[:, r * 16:(r + 1) * 16].unsqueeze(2).broadcast_to([128, 16, 128]),
                  op=ALU.mult, inc=("vsl", 1))

    def emit_ptpv_te(h, qt, gq, exps_done):
        sl = h % 2
        for r in range(4):
            kq = 4 * gq + r
            slot = kq % 4

            def emit_tp(g, kq=kq, slot=slot, gq=gq, r=r):
                gg = (gq * 16 + r * 4 + g) % 2
                P.wait("tensor", "vsl", kq + 1)
                P.wait("tensor", "sc", pspt_free[gg])
                for j in range(4):
                    col = 4 * g + j
                    v = P.ins("tensor", "transpose", ps_pt[gg][:, j * 128:(j + 1) * 128],
                              P2[slot][:, col * 128:(col + 1) * 128], ident32[:],
                              inc=("tp", 1) if j == 3 else None)
                if g == 3:
                    P.bump("tensor", "tpq", 1)
                P.wait("scalar", "tp", v)
                P.wait("scalar", "tv", ptbuf_free[gg])
                pspt_free[gg] = P.ins("scalar", "activation", out=PT[gg][:],
                                      in_=ps_pt[gg][:], func=AF.Copy, bias=0.0,
                                      scale=1.0, inc=("sc", 1))

            def emit_pv(g, gq=gq, r=r, sl=sl):
                gg = (gq * 16 + r * 4 + g) % 2
                jbase = r * 16 + 4 * g
                P.wait("tensor", "sc", pspt_free[gg])
                if jbase == 0:
                    P.wait("tensor", "sc", pspv_free[gq % 2])
                    P.wait("tensor", f"sv{sl}", vt_ready[sl])
                for j in range(4):
                    kb = jbase + j
                    v = P.ins("tensor", "matmul", ps_pv[gq % 2][:],
                              Vtok[sl][:, kb, :], PT[gg][:, j * 128:(j + 1) * 128],
                              start=(kb == 0), stop=(kb == 63),
                              inc=("tv", 1) if j == 3 else None)
                ptbuf_free[gg] = v

            emit_tp(0)
            emit_tp(1)
            emit_pv(0)
            emit_tp(2)
            emit_pv(1)
            emit_tp(3)
            emit_pv(2)
            emit_pv(3)
        pv_done = P.val("tv")
        if qt == NTT - 1:
            vt_free[sl] = pv_done
        P.wait("scalar", "tv", pv_done)
        P.wait("scalar", "ga0", attst_free[0])
        scv = P.ins("scalar", "activation", out=attst[0][:], in_=ps_pv[gq % 2][:],
                    func=AF.Copy, bias=0.0, scale=1.0, inc=("sc", 1))
        pspv_free[gq % 2] = scv
        P.wait("gpsimd", "sc", scv)
        attst_free[0] = P.ins("gpsimd", "dma_start", out=ao_sc[qt, h],
                              in_=attst[0][:], inc=("ga0", 16))

    kv_loads(0)
    kv_loads(1)
    q_load(0, 0)
    prev = None
    gq = 0
    for h in range(H):
        for qt in range(NTT):
            if prev is not None:
                emit_scales(*prev)
            ed = emit_qkt(h, qt, gq)
            if prev is not None:
                emit_ptpv_te(*prev)
            if qt == 0 and h >= 1 and h + 1 < H:
                kv_loads(h + 1)
            if qt == NTT - 1:
                kh_free[h % 2] = P.val("tq")
                if h + 1 < H:
                    q_load(h + 1, P.val("tq"))
            emit_tables(h, qt)
            prev = (h, qt, gq, ed)
            gq += 1
    emit_scales(*prev)
    emit_ptpv_te(*prev)
    ao_done = (P.val("ga0"), P.val("ga1"))

    ph.close()
    P.barrier()

    ph = contextlib.ExitStack()
    woutT = ph.enter_context(nc.sbuf_tensor("woutT", [128, 16, E], F32))
    woraw = [ph.enter_context(nc.sbuf_tensor(f"woraw{i}", [128, E], F32)) for i in range(2)]
    aoblk = ph.enter_context(nc.sbuf_tensor("aoblk", [128, 16, 128], F32))
    xtt = ph.enter_context(nc.sbuf_tensor("xtt", [128, E], F32))
    borow = ph.enter_context(nc.sbuf_tensor("borow", [1, E], F32))
    Btile = ph.enter_context(nc.sbuf_tensor("Btile", [128, E], F32))
    t_a = ph.enter_context(nc.sbuf_tensor("t_a", [128, E], F32))
    s1 = ph.enter_context(nc.sbuf_tensor("s1", [128, 1], F32))
    meanv = ph.enter_context(nc.sbuf_tensor("meanv", [128, 1], F32))
    ssq = ph.enter_context(nc.sbuf_tensor("ssq", [128, 1], F32))
    rstd = ph.enter_context(nc.sbuf_tensor("rstd", [128, 1], F32))
    x1st = [ph.enter_context(nc.sbuf_tensor(f"x1st{i}", [128, E], F32)) for i in range(2)]

    php = contextlib.ExitStack()
    ps_tp4 = [php.enter_context(nc.psum_tensor(f"ps_tp4{i}", [128, 512], F32)) for i in range(2)]

    sdv = P.ins("sync", "dma_start", out=borow[:], in_=bout_d.rearrange("(a b) -> a b", a=1),
                inc=("sb", 16))
    P.wait("tensor", "sb", sdv)
    pstp_free4 = [0, 0]
    for fc in range(4):
        gs = fc % 2
        P.wait("tensor", "sc", pstp_free4[gs])
        txv = P.ins("tensor", "matmul", ps_tp4[gs][:], ones_row[:],
                    borow[:, fc * 512:(fc + 1) * 512], start=True, stop=True, inc=("tq", 1))
        P.wait("scalar", "tq", txv)
        pstp_free4[gs] = P.ins("scalar", "activation", out=Btile[:, fc * 512:(fc + 1) * 512],
                               in_=ps_tp4[gs][:], func=AF.Copy, bias=0.0, scale=1.0,
                               inc=("sc", 1))
    woraw_free = [0, 0]
    for ft in range(16):
        sl = ft % 2
        P.wait("sync", "tq", woraw_free[sl])
        sdv = P.ins("sync", "dma_start", out=woraw[sl][:],
                    in_=wout_d[ft * 128:(ft + 1) * 128, :], inc=(f"sw{sl}", 16))
        P.wait("tensor", f"sw{sl}", sdv)
        for g in range(4):
            gs = g % 2
            P.wait("tensor", "sc", pstp_free4[gs])
            for j in range(4):
                ech = 4 * g + j
                v = P.ins("tensor", "transpose", ps_tp4[gs][:, j * 128:(j + 1) * 128],
                          woraw[sl][:, ech * 128:(ech + 1) * 128], ident32[:],
                          inc=("tq", 1) if j == 3 else None)
            P.wait("scalar", "tq", v)
            pstp_free4[gs] = P.ins("scalar", "activation",
                                   out=woutT[:, 4 * g:4 * g + 4, ft * 128:(ft + 1) * 128],
                                   in_=ps_tp4[gs][:].rearrange("p (a b) -> p a b", a=4),
                                   func=AF.Copy, bias=0.0, scale=1.0, inc=("sc", 1))
        woraw_free[sl] = P.val("tq")
    woutT_ready = P.val("sc")
    tq_prep_done = P.val("tq")
    php.close()

    ps_o = [ph.enter_context(nc.psum_tensor(f"ps_o{i}", [128, E], F32)) for i in range(2)]
    aoblk2 = [aoblk[:], woraw[0][:].rearrange("p (h t) -> p h t", h=16)]
    xtt2 = [xtt, woraw[1]]
    aoblk_free = [0, 0]
    xtt_free = [0, 0]
    pso_free = [0, 0]
    x1st_free = [0, 0]
    P.wait("tensor", "sc", woutT_ready)
    for tt in range(NTT):
        sl = tt % 2
        P.wait("sync", "ga0", ao_done[0])
        P.wait("sync", "ga1", ao_done[1])
        if tt < 2:
            P.wait("sync", "tq", tq_prep_done)
        P.wait("sync", "tq", aoblk_free[sl])
        aoblk_ready = P.ins("sync", "dma_start", out=aoblk2[sl],
                            in_=ao_sc[tt].rearrange("h p t -> p h t"), inc=(f"sa{sl}", 16))
        P.wait("sync", "ve", xtt_free[sl])
        xtt_ready = P.ins("sync", "dma_start", out=xtt2[sl][:],
                          in_=x_d[tt * 128:(tt + 1) * 128, :], inc=(f"sxt{sl}", 16))
        P.wait("tensor", f"sa{sl}", aoblk_ready)
        P.wait("tensor", "sc", pso_free[sl])
        if tt < 2:
            P.wait("tensor", "tq", tq_prep_done)
        for ech in range(16):
            for fc in range(4):
                v = P.ins("tensor", "matmul", ps_o[sl][:, fc * 512:(fc + 1) * 512],
                          aoblk2[sl][:, ech, :], woutT[:, ech, fc * 512:(fc + 1) * 512],
                          start=(ech == 0), stop=(ech == 15),
                          inc=("tq", 1) if (ech == 15 and fc == 3) else None)
        aoblk_free[sl] = v
        P.wait("vector", "tq", v)
        P.wait("vector", f"sxt{sl}", xtt_ready)
        P.wait("vector", "sc", woutT_ready)
        P.wait("vector", f"gx{sl}", x1st_free[sl])
        P.ins("vector", "tensor_tensor", out=t_a[:], in0=ps_o[sl][:], in1=Btile[:], op=ALU.add)
        vv = P.ins("vector", "tensor_tensor", out=x1st[sl][:], in0=t_a[:], in1=xtt2[sl][:],
                   op=ALU.add, inc=("ve", 1))
        xtt_free[sl] = vv
        P.ins("vector", "tensor_reduce", out=s1[:], in_=x1st[sl][:], axis=AX.X, op=ALU.add)
        P.ins("vector", "drain")
        P.ins("vector", "tensor_scalar", out=meanv[:], in0=s1[:], scalar1=INV2048,
              scalar2=None, op0=ALU.mult)
        P.ins("vector", "drain")
        vv = P.ins("vector", "tensor_scalar", out=t_a[:], in0=x1st[sl][:], scalar1=meanv[:],
                   scalar2=None, op0=ALU.subtract, inc=("ve", 1))
        P.wait("scalar", "ve", vv)
        pso_free[sl] = P.ins("scalar", "activation", out=ps_o[sl][:], in_=t_a[:],
                             func=AF.Square, bias=0.0, scale=1.0, accum_out=ssq[:],
                             inc=("sc", 1))
        P.ins("scalar", "drain")
        scv = P.ins("scalar", "activation", out=rstd[:], in_=ssq[:], func=AF.Sqrt,
                    bias=eps_col[:], scale=INV2048, inc=("sc", 1))
        P.wait("vector", "sc", scv)
        P.ins("vector", "reciprocal", out=meanv[:], in_=rstd[:])
        P.ins("vector", "drain")
        vv = P.ins("vector", "tensor_scalar", out=x1st[sl][:], in0=t_a[:], scalar1=meanv[:],
                   scalar2=None, op0=ALU.mult, inc=("ve", 1))
        P.wait("gpsimd", "ve", vv)
        x1st_free[sl] = P.ins("gpsimd", "dma_start", out=x1_sc[tt], in_=x1st[sl][:],
                              inc=(f"gx{sl}", 16))
    x1_done = (P.val("gx0"), P.val("gx1"))

    ph.close()
    P.barrier()

    ph = contextlib.ExitStack()
    kqT = ph.enter_context(nc.sbuf_tensor("kqT", [128, 16, NTT, 128], BF16))
    x1row = [ph.enter_context(nc.sbuf_tensor(f"x1row{i}", [128, E], F32)) for i in range(2)]
    uF = ph.enter_context(nc.sbuf_tensor("uF", [128, E], F32))
    v1F = ph.enter_context(nc.sbuf_tensor("v1F", [128, E], F32))
    kqF = ph.enter_context(nc.sbuf_tensor("kqF", [128, E], BF16))
    amax = ph.enter_context(nc.sbuf_tensor("amax", [128, 1], F32))
    clipm = ph.enter_context(nc.sbuf_tensor("clipm", [128, 1], F32))
    recipm = ph.enter_context(nc.sbuf_tensor("recipm", [128, 1], F32))
    sfull = ph.enter_context(nc.sbuf_tensor("sfull", [128, 1], F32))
    invs = ph.enter_context(nc.sbuf_tensor("invs", [128, 1], F32))
    wgch = [ph.enter_context(nc.sbuf_tensor(f"wgch{i}", [128, 4, 16, 128], BF16)) for i in range(2)]
    wuch = [ph.enter_context(nc.sbuf_tensor(f"wuch{i}", [128, 4, 16, 128], BF16)) for i in range(2)]
    zg = [ph.enter_context(nc.sbuf_tensor(f"zg{i}", [128, 512], F32)) for i in range(2)]
    zst = [ph.enter_context(nc.sbuf_tensor(f"zst{i}", [128, 512], F32)) for i in range(2)]
    ps_g = [ph.enter_context(nc.psum_tensor(f"ps_g{i}", [128, 512], F32)) for i in range(2)]
    ps_u = [ph.enter_context(nc.psum_tensor(f"ps_u{i}", [128, 512], F32)) for i in range(2)]
    ps_tpf = [ph.enter_context(nc.psum_tensor(f"ps_tpf{i}", [128, 512], BF16)) for i in range(2)]

    x1row_free = [0, 0]
    pstpf_free = [0, 0]
    for tt in range(NTT):
        sl = tt % 2
        P.wait("sync", "gx0", x1_done[0])
        P.wait("sync", "gx1", x1_done[1])
        P.wait("sync", "sc", x1row_free[sl])
        sdv = P.ins("sync", "dma_start", out=x1row[sl][:], in_=x1_sc[tt],
                    inc=(f"sx{sl}", 16))
        P.wait("vector", f"sx{sl}", sdv)
        P.ins("vector", "tensor_reduce", out=amax[:], in_=x1row[sl][:], axis=AX.X,
              op=ALU.max, apply_absolute_value=True)
        P.ins("vector", "drain")
        P.ins("vector", "tensor_scalar", out=clipm[:], in0=amax[:], scalar1=EPS,
              scalar2=None, op0=ALU.max)
        P.ins("vector", "drain")
        P.ins("vector", "reciprocal", out=recipm[:], in_=clipm[:])
        P.ins("vector", "drain")
        P.ins("vector", "tensor_scalar", out=sfull[:], in0=recipm[:], scalar1=128.0,
              scalar2=None, op0=ALU.mult)
        P.ins("vector", "tensor_scalar", out=invs[:], in0=clipm[:], scalar1=P2M7,
              scalar2=None, op0=ALU.mult)
        P.ins("vector", "drain")
        P.ins("vector", "tensor_scalar", out=agtab[:, tt:tt + 1], in0=invs[:],
              scalar1=mu_bc[:, 0:1], scalar2=None, op0=ALU.mult)
        vev = P.ins("vector", "tensor_scalar", out=autab[:, tt:tt + 1], in0=invs[:],
                    scalar1=mu_bc[:, 1:2], scalar2=None, op0=ALU.mult, inc=("ve", 1))
        P.wait("scalar", "ve", vev)
        scv = P.ins("scalar", "activation", out=uF[:], in_=x1row[sl][:], func=AF.Copy,
                    bias=MAGIC, scale=sfull[:], inc=("sc", 1))
        x1row_free[sl] = scv
        P.wait("vector", "sc", scv)
        P.ins("vector", "tensor_scalar", out=v1F[:], in0=uF[:], scalar1=-MAGIC,
              scalar2=-128.0, op0=ALU.add, op1=ALU.max)
        vev = P.ins("vector", "tensor_scalar", out=kqF[:], in0=v1F[:], scalar1=127.0,
                    scalar2=None, op0=ALU.min, inc=("ve", 1))
        if debug and tt == 0:
            P.wait("gpsimd", "ve", vev)
            P.ins("gpsimd", "dma_start", out=dbg_kq, in_=kqF[:], inc=("gd", 16))
            P.wait("gpsimd", "gd", P.val("gd"))
            dbg_gate[0] = P.bump("gpsimd", "gp", 1)
            P.wait("scalar", "gp", dbg_gate[0])
        P.wait("tensor", "ve", vev)
        for g in range(4):
            gs = (tt * 4 + g) % 2
            P.wait("tensor", "sc", pstpf_free[gs])
            for j in range(4):
                ech = 4 * g + j
                v = P.ins("tensor", "transpose", ps_tpf[gs][:, j * 128:(j + 1) * 128],
                          kqF[:, ech * 128:(ech + 1) * 128], ident16[:],
                          inc=("tq", 1) if j == 3 else None)
            P.wait("scalar", "tq", v)
            pstpf_free[gs] = P.ins("scalar", "activation",
                                   out=kqT[:, 4 * g:4 * g + 4, tt, :],
                                   in_=ps_tpf[gs][:].rearrange("p (a b) -> p a b", a=4),
                                   func=AF.Copy, bias=0.0, scale=1.0, inc=("sc", 1))
    kqT_ready = P.val("sc")
    prep_ve = P.val("ve")
    if debug:
        P.wait("gpsimd", "ve", prep_ve)
        for off, tsr, w in ((0, agtab, 16), (16, autab, 16), (32, mu_bc, 4)):
            P.ins("gpsimd", "dma_start", out=dbg_ffn[:, off:off + w], in_=tsr[:],
                  inc=("gd", 16), allow_slow_non_contiguous=True)

    wgch_free = [0, 0]
    psg_free = [0, 0]
    psu_free = [0, 0]
    zg_free = [0, 0]
    zst_free = [0, 0]
    P.wait("tensor", "sc", kqT_ready)
    P.wait("scalar", "ve", prep_ve)
    for ic in range(16):
        cs = ic % 2
        P.wait("sync", "tq", wgch_free[cs])
        P.ins("sync", "dma_start", out=wgch[cs][:],
              in_=wgt_sc[ic].rearrange("a e p t -> p a e t"), inc=(f"sg{cs}", 16))
        sdv = P.ins("sync", "dma_start", out=wuch[cs][:],
                    in_=wut_sc[ic].rearrange("a e p t -> p a e t"), inc=(f"sg{cs}", 16))
        P.wait("tensor", f"sg{cs}", sdv)
        for tt in range(NTT):
            ts_ = tt % 2
            P.wait("tensor", "sc", psg_free[ts_])
            P.wait("tensor", "ve", psu_free[ts_])
            for ech in range(16):
                P.ins("tensor", "matmul", ps_g[ts_][:], kqT[:, ech, tt, :],
                      wgch[cs][:, :, ech, :], start=(ech == 0), stop=(ech == 15))
                v = P.ins("tensor", "matmul", ps_u[ts_][:], kqT[:, ech, tt, :],
                          wuch[cs][:, :, ech, :], start=(ech == 0), stop=(ech == 15),
                          inc=("tq", 1) if ech == 15 else None)
            P.wait("scalar", "tq", v)
            if debug and ic == 0 and tt == 0:
                P.wait("scalar", "tq", v)
                P.ins("scalar", "activation", out=uF[:, 0:512], in_=ps_g[ts_][:],
                      func=AF.Copy, bias=0.0, scale=1.0)
                dbv = P.ins("scalar", "activation", out=uF[:, 512:1024], in_=ps_u[ts_][:],
                            func=AF.Copy, bias=0.0, scale=1.0, inc=("sc", 1))
                P.wait("gpsimd", "sc", dbv)
                P.ins("gpsimd", "dma_start", out=dbg_g[:, 0:1024], in_=uF[:, 0:1024],
                      inc=("gd", 16), allow_slow_non_contiguous=True)
                P.wait("gpsimd", "gd", P.val("gd"))
                zgate = P.bump("gpsimd", "gp", 1)
                P.wait("scalar", "gp", zgate)
            P.wait("scalar", "ve", zg_free[ts_])
            scv = P.ins("scalar", "activation", out=zg[ts_][:], in_=ps_g[ts_][:],
                        func=AF.Silu, bias=0.0, scale=agtab[:, tt:tt + 1], inc=("sc", 1))
            psg_free[ts_] = scv
            if debug and ic == 0 and tt == 0:
                P.wait("gpsimd", "sc", scv)
                P.ins("gpsimd", "dma_start", out=dbg_g[:, 1024:1536], in_=zg[ts_][:],
                      inc=("gd", 16), allow_slow_non_contiguous=True)
                P.wait("gpsimd", "gd", P.val("gd"))
                zgate = P.bump("gpsimd", "gp", 1)
                P.wait("vector", "gp", zgate)
            P.wait("vector", "sc", scv)
            P.wait("vector", f"gz{ts_}", zst_free[ts_])
            vev = P.ins("vector", "scalar_tensor_tensor", out=zst[ts_][:], in0=ps_u[ts_][:],
                        scalar=autab[:, tt:tt + 1], in1=zg[ts_][:],
                        op0=ALU.mult, op1=ALU.mult, inc=("ve", 1))
            psu_free[ts_] = vev
            zg_free[ts_] = vev
            P.wait("gpsimd", "ve", vev)
            zst_free[ts_] = P.ins("gpsimd", "dma_start",
                                  out=z_sc[tt, :, ic * 512:(ic + 1) * 512],
                                  in_=zst[ts_][:], inc=(f"gz{ts_}", 16))
        wgch_free[cs] = P.val("tq")
    z_done = (P.val("gz0"), P.val("gz1"))

    ph.close()
    P.barrier()

    ph = contextlib.ExitStack()
    Zrow = ph.enter_context(nc.sbuf_tensor("Zrow", [128, NI], F32))
    Zb = ph.enter_context(nc.sbuf_tensor("Zb", [128, NI], F32))
    kq2 = ph.enter_context(nc.sbuf_tensor("kq2", [128, NI], BF16))
    kq2T = ph.enter_context(nc.sbuf_tensor("kq2T", [128, 64, 2, 128], BF16))
    wdch = [ph.enter_context(nc.sbuf_tensor(f"wdch{i}", [128, 64, 128], BF16)) for i in range(2)]
    dstage = [ph.enter_context(nc.sbuf_tensor(f"dst{i}", [128, 256], F32)) for i in range(2)]
    dtok = [ph.enter_context(nc.sbuf_tensor(f"dtok{i}", [128, E], F32)) for i in range(2)]
    x1g = ph.enter_context(nc.sbuf_tensor("x1g", [128, E], F32))
    outst = [ph.enter_context(nc.sbuf_tensor(f"outst{i}", [128, E], F32)) for i in range(2)]
    amax2 = ph.enter_context(nc.sbuf_tensor("amax2", [128, 1], F32))
    clipm2 = ph.enter_context(nc.sbuf_tensor("clipm2", [128, 1], F32))
    recip2 = ph.enter_context(nc.sbuf_tensor("recip2", [128, 1], F32))
    s2v = ph.enter_context(nc.sbuf_tensor("s2v", [128, 1], F32))
    gam = ph.enter_context(nc.sbuf_tensor("gam", [128, 1], F32))
    invs2 = ph.enter_context(nc.sbuf_tensor("invs2", [128, 1], F32))
    bdt = ph.enter_context(nc.sbuf_tensor("bdt", [128, 2], F32))
    s1b = ph.enter_context(nc.sbuf_tensor("s1b", [128, 1], F32))
    meanb = ph.enter_context(nc.sbuf_tensor("meanb", [128, 1], F32))
    ssqb = ph.enter_context(nc.sbuf_tensor("ssqb", [128, 1], F32))
    rstdb = ph.enter_context(nc.sbuf_tensor("rstdb", [128, 1], F32))
    t_a2 = ph.enter_context(nc.sbuf_tensor("t_a2", [128, E], F32))
    t_b2 = ph.enter_context(nc.sbuf_tensor("t_b2", [128, E], F32))
    ps_f = [ph.enter_context(nc.psum_tensor(f"ps_f{i}", [128, 256], F32)) for i in range(2)]
    ps_t2 = [ph.enter_context(nc.psum_tensor(f"ps_t2{i}", [128, 512], F32)) for i in range(2)]
    ps_tq2 = [ph.enter_context(nc.psum_tensor(f"ps_tq2{i}", [128, 512], BF16)) for i in range(2)]

    zrow_free = 0
    zb_free = 0
    kq2_free = 0
    kq2T_free = 0
    pstq2_free = [0, 0]
    wdch_free = [0, 0]
    psf_free = [0, 0]
    dstage_free = [0, 0]
    pst2_free = [0, 0]
    dtok_free = [0, 0]
    x1g_free = 0
    outst_free = [0, 0]
    for grp in range(8):
        for t2 in range(2):
            tt = grp * 2 + t2
            P.wait("sync", "gz0", z_done[0])
            P.wait("sync", "gz1", z_done[1])
            P.wait("sync", "ve", zrow_free)
            sdv = P.ins("sync", "dma_start", out=Zrow[:], in_=z_sc[tt], inc=("sz", 16))
            P.wait("vector", "sz", sdv)
            P.wait("vector", "ve", zb_free)
            cur, oth = Zrow, Zb
            for st in range(13):
                hh = 1 << st
                cv = cur[:].rearrange("p (g two h) -> p g two h", two=2, h=hh)
                ov = oth[:].rearrange("p (g two h) -> p g two h", two=2, h=hh)
                P.ins("vector", "tensor_tensor", out=ov[:, :, 0, :], in0=cv[:, :, 0, :],
                      in1=cv[:, :, 1, :], op=ALU.add)
                P.ins("vector", "tensor_tensor", out=ov[:, :, 1, :], in0=cv[:, :, 0, :],
                      in1=cv[:, :, 1, :], op=ALU.subtract)
                cur, oth = oth, cur
            res = cur
            P.ins("vector", "tensor_reduce", out=amax2[:], in_=res[:], axis=AX.X,
                  op=ALU.max, apply_absolute_value=True)
            P.ins("vector", "drain")
            P.ins("vector", "tensor_scalar", out=clipm2[:], in0=amax2[:], scalar1=FWC,
                  scalar2=EPS, op0=ALU.mult, op1=ALU.max)
            P.ins("vector", "drain")
            P.ins("vector", "reciprocal", out=recip2[:], in_=clipm2[:])
            P.ins("vector", "drain")
            P.ins("vector", "tensor_scalar", out=s2v[:], in0=recip2[:], scalar1=128.0,
                  scalar2=None, op0=ALU.mult)
            P.ins("vector", "drain")
            P.ins("vector", "tensor_scalar", out=gam[:], in0=s2v[:], scalar1=FWC,
                  scalar2=None, op0=ALU.mult)
            P.ins("vector", "tensor_scalar", out=invs2[:], in0=clipm2[:], scalar1=P2M7,
                  scalar2=None, op0=ALU.mult)
            P.ins("vector", "drain")
            vev = P.ins("vector", "tensor_scalar", out=bdt[:, t2:t2 + 1], in0=invs2[:],
                        scalar1=mu_bc[:, 2:3], scalar2=None, op0=ALU.mult, inc=("ve", 1))
            if debug and tt == 0:
                P.wait("gpsimd", "ve", vev)
                P.ins("gpsimd", "dma_start", out=dbg_fw, in_=res[:], inc=("gd", 16))
                P.wait("gpsimd", "gd", P.val("gd"))
                f2g0 = P.bump("gpsimd", "gp", 1)
                P.wait("scalar", "gp", f2g0)
            P.wait("scalar", "ve", vev)
            scv = P.ins("scalar", "activation", out=Zrow[:], in_=res[:], func=AF.Copy,
                        bias=MAGIC, scale=gam[:], inc=("sc", 1))
            P.wait("vector", "sc", scv)
            vv = P.ins("vector", "tensor_scalar", out=Zb[:], in0=Zrow[:], scalar1=-MAGIC,
                       scalar2=-128.0, op0=ALU.add, op1=ALU.max, inc=("ve", 1))
            zrow_free = vv
            P.wait("vector", "tq", kq2_free)
            vev = P.ins("vector", "tensor_scalar", out=kq2[:], in0=Zb[:], scalar1=127.0,
                        scalar2=None, op0=ALU.min, inc=("ve", 1))
            zb_free = vev
            if debug and tt == 0:
                P.wait("gpsimd", "ve", vev)
                P.ins("gpsimd", "dma_start", out=dbg_k2, in_=kq2[:], inc=("gd", 16))
                P.wait("gpsimd", "gd", P.val("gd"))
                f2g = P.bump("gpsimd", "gp", 1)
                P.wait("tensor", "gp", f2g)
            P.wait("tensor", "ve", vev)
            P.wait("tensor", "tq", kq2T_free)
            for g in range(16):
                gs = (tt * 16 + g) % 2
                P.wait("tensor", "sc", pstq2_free[gs])
                for j in range(4):
                    ich = 4 * g + j
                    v = P.ins("tensor", "transpose", ps_tq2[gs][:, j * 128:(j + 1) * 128],
                              kq2[:, ich * 128:(ich + 1) * 128], ident16[:],
                              inc=("tq", 1) if j == 3 else None)
                P.wait("scalar", "tq", v)
                pstq2_free[gs] = P.ins("scalar", "activation",
                                       out=kq2T[:, 4 * g:4 * g + 4, t2, :],
                                       in_=ps_tq2[gs][:].rearrange("p (a b) -> p a b", a=4),
                                       func=AF.Copy, bias=0.0, scale=1.0, inc=("sc", 1))
            kq2_free = P.val("tq")
        kq2T_ready = P.val("sc")
        ve_q = P.val("ve")
        P.wait("tensor", "sc", kq2T_ready)
        P.wait("scalar", "ve", ve_q)
        for ft in range(16):
            fs = ft % 2
            P.wait("sync", "tq", wdch_free[fs])
            sdv = P.ins("sync", "dma_start", out=wdch[fs][:],
                        in_=wdt_sc[ft].rearrange("a p t -> p a t"), inc=(f"sg{fs}", 16))
            P.wait("tensor", f"sg{fs}", sdv)
            P.wait("tensor", "sc", psf_free[fs])
            for ich in range(64):
                v = P.ins("tensor", "matmul", ps_f[fs][:],
                          wdch[fs][:, ich, :], kq2T[:, ich, :, :].rearrange("p a t -> p (a t)"),
                          start=(ich == 0), stop=(ich == 63),
                          inc=("tq", 1) if ich == 63 else None)
            wdch_free[fs] = v
            P.wait("scalar", "tq", v)
            P.wait("scalar", "tq", dstage_free[fs])
            scv = P.ins("scalar", "activation", out=dstage[fs][:], in_=ps_f[fs][:],
                        func=AF.Copy, bias=0.0, scale=1.0, inc=("sc", 1))
            psf_free[fs] = scv
            P.wait("tensor", "sc", scv)
            P.wait("tensor", "sc", pst2_free[fs])
            for t2 in range(2):
                v = P.ins("tensor", "transpose", ps_t2[fs][:, t2 * 128:(t2 + 1) * 128],
                          dstage[fs][:, t2 * 128:(t2 + 1) * 128], ident32[:],
                          inc=("tq", 1) if t2 == 1 else None)
            dstage_free[fs] = v
            P.wait("scalar", "tq", v)
            for t2 in range(2):
                if ft == 0:
                    P.wait("scalar", "ve", dtok_free[t2])
                scv = P.ins("scalar", "activation",
                            out=dtok[t2][:, ft * 128:(ft + 1) * 128],
                            in_=ps_t2[fs][:, t2 * 128:(t2 + 1) * 128],
                            func=AF.Copy, bias=0.0, scale=bdt[:, t2:t2 + 1],
                            inc=("sc", 1))
            pst2_free[fs] = P.val("sc")
        down_done = P.val("sc")
        kq2T_free = P.val("tq")
        if debug and grp == 0:
            P.wait("gpsimd", "sc", down_done)
            P.ins("gpsimd", "dma_start", out=dbg_dt, in_=dtok[0][:], inc=("gd", 16))
            P.wait("gpsimd", "gd", P.val("gd"))
            f2g2 = P.bump("gpsimd", "gp", 1)
            P.wait("vector", "gp", f2g2)
        for t2 in range(2):
            tt = grp * 2 + t2
            P.wait("sync", "ve", x1g_free)
            sdv = P.ins("sync", "dma_start", out=x1g[:], in_=x1_sc[tt], inc=("sxt", 16))
            P.wait("vector", "sc", down_done)
            P.wait("vector", "sxt", sdv)
            vv = P.ins("vector", "tensor_tensor", out=t_b2[:], in0=dtok[t2][:],
                       in1=x1g[:], op=ALU.add, inc=("ve", 1))
            dtok_free[t2] = vv
            x1g_free = vv
            P.ins("vector", "tensor_reduce", out=s1b[:], in_=t_b2[:], axis=AX.X, op=ALU.add)
            P.ins("vector", "drain")
            P.ins("vector", "tensor_scalar", out=meanb[:], in0=s1b[:], scalar1=INV2048,
                  scalar2=None, op0=ALU.mult)
            P.ins("vector", "drain")
            vv = P.ins("vector", "tensor_scalar", out=t_a2[:], in0=t_b2[:], scalar1=meanb[:],
                       scalar2=None, op0=ALU.subtract, inc=("ve", 1))
            P.wait("scalar", "ve", vv)
            P.ins("scalar", "activation", out=t_b2[:], in_=t_a2[:], func=AF.Square,
                  bias=0.0, scale=1.0, accum_out=ssqb[:])
            P.ins("scalar", "drain")
            scv = P.ins("scalar", "activation", out=rstdb[:], in_=ssqb[:], func=AF.Sqrt,
                        bias=eps_col[:], scale=INV2048, inc=("sc", 1))
            P.wait("vector", "sc", scv)
            P.ins("vector", "reciprocal", out=meanb[:], in_=rstdb[:])
            P.ins("vector", "drain")
            P.wait("vector", f"so{t2}", outst_free[t2])
            vv = P.ins("vector", "tensor_scalar", out=outst[t2][:], in0=t_a2[:],
                       scalar1=meanb[:], scalar2=None, op0=ALU.mult, inc=("ve", 1))
            P.wait("gpsimd", "ve", vv)
            outst_free[t2] = P.ins("gpsimd", "dma_start",
                                   out=y_d[tt * 128:(tt + 1) * 128, :], in_=outst[t2][:],
                                   inc=(f"so{t2}", 16))

    P.wait("gpsimd", "so0", P.val("so0"))
    P.wait("gpsimd", "so1", P.val("so1"))

    with nc.Block() as block:
        for eng in ENGS:
            getattr(block, eng)(lambda e, ops=P.ops[eng]: [f(e) for f in ops])
    ph.close()
    es.close()
    return nc


def kernel(L0_attn_x, L0_attn_w_qkv, L0_attn_w_out, L0_attn_b_out,
           L0_ffn_w_gate, L0_ffn_w_up, L0_ffn_w_down):
    nc = build()
    x = np.ascontiguousarray(np.asarray(L0_attn_x, dtype=np.float32))
    wqkv = np.ascontiguousarray(np.asarray(L0_attn_w_qkv, dtype=np.float32))
    wout = np.ascontiguousarray(np.asarray(L0_attn_w_out, dtype=np.float32))
    bout = np.ascontiguousarray(np.asarray(L0_attn_b_out, dtype=np.float32))
    wg = np.ascontiguousarray(np.asarray(L0_ffn_w_gate, dtype=np.float32))
    wu = np.ascontiguousarray(np.asarray(L0_ffn_w_up, dtype=np.float32))
    wd = np.ascontiguousarray(np.asarray(L0_ffn_w_down, dtype=np.float32))
    in_maps = []
    for c in range(8):
        b, qr = c // 4, c % 4
        in_maps.append({
            "x": np.ascontiguousarray(x[b, qr * T:(qr + 1) * T, :]),
            "wqkv": wqkv, "wout": wout, "bout": bout,
            "wg": wg, "wu": wu, "wd": wd,
        })
    res = run_bass_kernel_spmd(nc, in_maps, core_ids=list(range(8)))
    out = np.empty((2, SEQ, E), np.float32)
    for c in range(8):
        b, qr = c // 4, c % 4
        out[b, qr * T:(qr + 1) * T, :] = res.results[c]["y"]
    return out


# version 07 (final)
# speedup vs baseline: 1.1100x; 1.0072x over previous
import collections
import contextlib

import numpy as np

import concourse.bass as bass
import concourse.mybir as mybir
from concourse.bass_utils import run_bass_kernel_spmd

F32 = mybir.dt.float32
BF16 = mybir.dt.bfloat16
AF = mybir.ActivationFunctionType
ALU = mybir.AluOpType
AX = mybir.AxisListType

E = 2048
H = 16
DH = 128
QKVF = 6144
NI = 8192
T = 2048
NTT = 16
NTC = 4
SEQ = 8192
NKB = 64
NCH = 16
NRANK = 4

MAGIC = float(np.float32(1.5 * 2 ** 23))
_SC = 1.0 / float(np.float32(np.sqrt(128.0)))
SC_HI = float(np.float32(_SC))
SC_LO = float(np.float32(_SC - SC_HI))
FWC = float(np.float32(1.0 / np.sqrt(8192.0)))
EPS = 1e-5
INV2048 = 1.0 / 2048.0
P2M7 = 2.0 ** -7
P2M24 = 2.0 ** -24

ENGS = ("sync", "gpsimd", "vector", "scalar", "tensor")


class Prog:
    def __init__(self, nc, es):
        self.nc = nc
        self.es = es
        self.ops = {e: [] for e in ENGS}
        self.cnt = collections.Counter()
        self.sems = {}
        self.bar_k = 0

    def sem(self, name):
        if name not in self.sems:
            self.sems[name] = self.es.enter_context(self.nc.semaphore(name))
        return self.sems[name]

    def val(self, name):
        return self.cnt[name]

    def emit(self, eng, fn):
        self.ops[eng].append(fn)

    def wait(self, eng, name, v):
        if v is None or v <= 0:
            return
        s = self.sem(name)
        self.emit(eng, lambda e, s=s, v=v: e.wait_ge(s, v))

    def ins(self, eng, meth, *args, inc=None, **kw):
        if inc is None:
            self.emit(eng, lambda e, m=meth, a=args, k=kw: getattr(e, m)(*a, **k))
            return None
        name, n = inc
        s = self.sem(name)
        self.cnt[name] += n
        v = self.cnt[name]
        self.emit(eng, lambda e, m=meth, a=args, k=kw, s=s, n=n: getattr(e, m)(*a, **k).then_inc(s, n))
        return v

    def bump(self, eng, name, n=1):
        s = self.sem(name)
        self.cnt[name] += n
        v = self.cnt[name]
        self.emit(eng, lambda e, s=s, n=n: e.sem_inc(s, n))
        return v

    def barrier(self):
        self.bar_k += 1
        for name, v in list(self.cnt.items()):
            if name.startswith("s") and name not in ("sc", "so0", "so1"):
                self.wait("sync", name, v)
            elif name.startswith("g") and name != "gp":
                self.wait("gpsimd", name, v)
            elif name in ("so0", "so1"):
                self.wait("gpsimd", name, v)
        for eng in ENGS:
            self.bump(eng, "bar", 1)
        tgt = 5 * self.bar_k
        for eng in ENGS:
            self.wait(eng, "bar", tgt)


def build(debug=False):
    nc = bass.Bass("TRN2", target_bir_lowering=False)
    sck = "ExternalOutput" if debug else "Internal"

    x_d = nc.dram_tensor("x", [T, E], F32, kind="ExternalInput").ap()
    wqkv_d = nc.dram_tensor("wqkv", [QKVF, E], F32, kind="ExternalInput").ap()
    wout_d = nc.dram_tensor("wout", [E, E], F32, kind="ExternalInput").ap()
    bout_d = nc.dram_tensor("bout", [E], F32, kind="ExternalInput").ap()
    wg_d = nc.dram_tensor("wg", [NI, E], F32, kind="ExternalInput").ap()
    wu_d = nc.dram_tensor("wu", [NI, E], F32, kind="ExternalInput").ap()
    wd_d = nc.dram_tensor("wd", [E, NI], F32, kind="ExternalInput").ap()
    y_d = nc.dram_tensor("y", [T, E], F32, kind="ExternalOutput").ap()

    q_sc = nc.dram_tensor("q_sc", [H, DH, T], F32, kind=sck).ap()
    agin = nc.dram_tensor("agin", [2, H, DH, T], F32).ap()
    agout = nc.dram_tensor("agout", [NRANK, 2, H, DH, T], F32).ap()
    ao_sc = nc.dram_tensor("ao_sc", [NTT, H, DH, 128], F32, kind=sck).ap()
    x1_sc = nc.dram_tensor("x1_sc", [NTT, 128, E], F32, kind=sck).ap()
    z_sc = nc.dram_tensor("z_sc", [NTT, 128, NI], F32, kind=sck).ap()
    wgt_sc = nc.dram_tensor("wgt_sc", [16, 4, 16, 128, 128], BF16, kind=sck).ap()
    wut_sc = nc.dram_tensor("wut_sc", [16, 4, 16, 128, 128], BF16, kind=sck).ap()
    wdt_sc = nc.dram_tensor("wdt_sc", [16, 64, 128, 128], BF16, kind=sck).ap()
    if debug:
        dbg_p = nc.dram_tensor("dbg_p", [128, SEQ], F32, kind="ExternalOutput").ap()
        dbg_tab = nc.dram_tensor("dbg_tab", [128, 4 * 64 + 2 * 16 + 4], F32, kind="ExternalOutput").ap()
        dbg_kq = nc.dram_tensor("dbg_kq", [128, E], BF16, kind="ExternalOutput").ap()
        dbg_ffn = nc.dram_tensor("dbg_ffn", [128, 64], F32, kind="ExternalOutput").ap()
        dbg_acc = nc.dram_tensor("dbg_acc", [128, 64], F32, kind="ExternalOutput").ap()
        dbg_g = nc.dram_tensor("dbg_g", [128, 1536], F32, kind="ExternalOutput").ap()
        dbg_fw = nc.dram_tensor("dbg_fw", [128, NI], F32, kind="ExternalOutput").ap()
        dbg_k2 = nc.dram_tensor("dbg_k2", [128, NI], BF16, kind="ExternalOutput").ap()
        dbg_dt = nc.dram_tensor("dbg_dt", [128, E], F32, kind="ExternalOutput").ap()
    dbg_gate = [0]

    es = contextlib.ExitStack()
    P = Prog(nc, es)
    for s in ("sd", "gd", "so", "cc", "bar", "tq", "tp", "tv", "sc", "ve", "gp"):
        P.sem(s)

    ident32 = es.enter_context(nc.sbuf_tensor("ident32", [128, 128], F32))
    ones_col = es.enter_context(nc.sbuf_tensor("ones_col", [128, 1], F32))
    ones_row = es.enter_context(nc.sbuf_tensor("ones_row", [1, 128], F32))
    mu_bc = es.enter_context(nc.sbuf_tensor("mu_bc", [128, 4], F32))
    eps_col = es.enter_context(nc.sbuf_tensor("eps_col", [128, 1], F32))

    ph0 = contextlib.ExitStack()
    ident16 = ph0.enter_context(nc.sbuf_tensor("ident16", [128, 128], BF16))
    P.ins("gpsimd", "memset", ident32[:], 0.0)
    P.ins("gpsimd", "affine_select", out=ident32[:], in_=ident32[:],
          compare_op=ALU.not_equal, fill=1.0, base=0, pattern=[[-1, 128]],
          channel_multiplier=1)
    P.ins("gpsimd", "memset", ident16[:], 0.0)
    P.ins("gpsimd", "affine_select", out=ident16[:], in_=ident16[:],
          compare_op=ALU.not_equal, fill=1.0, base=0, pattern=[[-1, 128]],
          channel_multiplier=1)
    P.ins("gpsimd", "memset", ones_col[:], 1.0)
    P.ins("gpsimd", "memset", eps_col[:], EPS)
    gp_const = P.ins("gpsimd", "memset", ones_row[:], 1.0, inc=("gp", 1))

    xraw = [ph0.enter_context(nc.sbuf_tensor(f"xraw{i}", [128, E], F32)) for i in range(2)]
    wqraw = [ph0.enter_context(nc.sbuf_tensor(f"wqraw{i}", [128, E], F32)) for i in range(2)]
    wqT = [ph0.enter_context(nc.sbuf_tensor(f"wqT{i}", [128, 16, 128], F32)) for i in range(2)]
    qstage = [ph0.enter_context(nc.sbuf_tensor(f"qstage{i}", [128, 512], F32)) for i in range(2)]
    qs1 = ph0.enter_context(nc.sbuf_tensor("qs1", [128, 512], F32))
    qs2 = [ph0.enter_context(nc.sbuf_tensor(f"qs2{i}", [128, 512], F32)) for i in range(2)]
    ps_tp = [ph0.enter_context(nc.psum_tensor(f"ps_tp{i}", [128, 512], F32)) for i in range(2)]
    ps_mm = [ph0.enter_context(nc.psum_tensor(f"ps_mm{i}", [128, 512], F32)) for i in range(2)]
    ps_sm = ph0.enter_context(nc.psum_tensor("ps_sm", [128, 2], F32))
    ps_tpb = [ph0.enter_context(nc.psum_tensor(f"ps_tpb{i}", [128, 512], BF16)) for i in range(2)]
    wqvT = ph0.enter_context(nc.sbuf_tensor("wqvT", [128, 16, 256], F32))

    phx = contextlib.ExitStack()
    xT = phx.enter_context(nc.sbuf_tensor("xT", [128, 16, T], F32))

    xraw_free = [0, 0]
    pstp_free = [0, 0]
    P.wait("tensor", "gp", gp_const)
    for tt in range(NTT):
        sl = tt % 2
        P.wait("sync", "tq", xraw_free[sl])
        sdv = P.ins("sync", "dma_start", out=xraw[sl][:], in_=x_d[tt * 128:(tt + 1) * 128, :],
                    inc=(f"sx{sl}", 16))
        P.wait("tensor", f"sx{sl}", sdv)
        for g in range(4):
            gs = (tt * 4 + g) % 2
            P.wait("tensor", "sc", pstp_free[gs])
            for j in range(4):
                ech = 4 * g + j
                v = P.ins("tensor", "transpose", ps_tp[gs][:, j * 128:(j + 1) * 128],
                          xraw[sl][:, ech * 128:(ech + 1) * 128], ident32[:],
                          inc=("tq", 1) if j == 3 else None)
            P.wait("scalar", "tq", v)
            pstp_free[gs] = P.ins("scalar", "activation",
                                  out=xT[:, 4 * g:4 * g + 4, tt * 128:(tt + 1) * 128],
                                  in_=ps_tp[gs][:].rearrange("p (a b) -> p a b", a=4),
                                  func=AF.Copy, bias=0.0, scale=1.0, inc=("sc", 1))
        xraw_free[sl] = P.val("tq")
    sc_xT_done = P.val("sc")

    wqraw_free = [0, 0]
    qstage_free = [0, 0]
    qstage_free_sem = ["gd", "gd"]
    qs2_free = [0, 0]
    psmm_free = [0, 0]
    idx_c = [0]
    ms_c = [0]

    P.wait("tensor", "sc", sc_xT_done)

    def emit_w_transpose(ft, dst_ap_fn):
        sl = idx_c[0] % 2
        idx_c[0] += 1
        P.wait("sync", "tq", wqraw_free[sl])
        sdv = P.ins("sync", "dma_start", out=wqraw[sl][:],
                    in_=wqkv_d[ft * 128:(ft + 1) * 128, :], inc=(f"sw{sl}", 16))
        P.wait("tensor", f"sw{sl}", sdv)
        for g in range(4):
            gs = (idx_c[0] * 4 + g) % 2
            P.wait("tensor", "sc", pstp_free[gs])
            for j in range(4):
                ech = 4 * g + j
                v = P.ins("tensor", "transpose", ps_tp[gs][:, j * 128:(j + 1) * 128],
                          wqraw[sl][:, ech * 128:(ech + 1) * 128], ident32[:],
                          inc=("tq", 1) if j == 3 else None)
            P.wait("scalar", "tq", v)
            pstp_free[gs] = P.ins("scalar", "activation",
                                  out=dst_ap_fn(g),
                                  in_=ps_tp[gs][:].rearrange("p (a b) -> p a b", a=4),
                                  func=AF.Copy, bias=0.0, scale=1.0, inc=("sc", 1))
        wqraw_free[sl] = P.val("tq")
        return P.val("sc")

    def emit_kq_tile(ft):
        sl2 = idx_c[0] % 2
        wqT_ready = emit_w_transpose(ft, lambda g, sl2=sl2: wqT[sl2][:, 4 * g:4 * g + 4, :])
        P.wait("tensor", "sc", wqT_ready)
        for tc in range(NTC):
            ms = ms_c[0] % 2
            ms_c[0] += 1
            P.wait("tensor", "sc", psmm_free[ms])
            for ech in range(16):
                v = P.ins("tensor", "matmul", ps_mm[ms][:],
                          wqT[sl2][:, ech, :], xT[:, ech, tc * 512:(tc + 1) * 512],
                          start=(ech == 0), stop=(ech == 15),
                          inc=("tq", 1) if ech == 15 else None)
            P.wait("scalar", "tq", v)
            P.wait("scalar", qstage_free_sem[ms], qstage_free[ms])
            scv = P.ins("scalar", "activation", out=qstage[ms][:], in_=ps_mm[ms][:],
                        func=AF.Copy, bias=0.0, scale=1.0, inc=("sc", 1))
            psmm_free[ms] = scv
            if ft < 16:
                P.wait("vector", "sc", scv)
                P.ins("vector", "tensor_scalar", out=qs1[:], in0=qstage[ms][:],
                      scalar1=SC_HI, scalar2=None, op0=ALU.mult)
                P.wait("vector", f"gq{ms}", qs2_free[ms])
                vev = P.ins("vector", "scalar_tensor_tensor", out=qs2[ms][:],
                            in0=qstage[ms][:], scalar=SC_LO, in1=qs1[:],
                            op0=ALU.mult, op1=ALU.add, inc=("ve", 1))
                qstage_free[ms] = vev
                qstage_free_sem[ms] = "ve"
                P.wait("gpsimd", "ve", vev)
                qs2_free[ms] = P.ins("gpsimd", "dma_start",
                                     out=q_sc[ft, :, tc * 512:(tc + 1) * 512],
                                     in_=qs2[ms][:], inc=(f"gq{ms}", 16))
            else:
                P.wait("gpsimd", "sc", scv)
                dst = agin[0, ft - 16, :, tc * 512:(tc + 1) * 512]
                qstage_free[ms] = P.ins("gpsimd", "dma_start", out=dst, in_=qstage[ms][:],
                                        inc=(f"gk{ms}", 16))
                qstage_free_sem[ms] = f"gk{ms}"

    for ft in range(16, 32):
        emit_kq_tile(ft)
    agin_v = agin[1].rearrange("h d t -> (h d) t")
    for vg in range(8):
        wqv_ready = 0
        for kk in range(2):
            ft = 32 + 2 * vg + kk
            wqv_ready = emit_w_transpose(
                ft, lambda g, kk=kk: wqvT[:, 4 * g:4 * g + 4, kk * 128:(kk + 1) * 128])
        P.wait("tensor", "sc", wqv_ready)
        for tt in range(NTT):
            ms = ms_c[0] % 2
            ms_c[0] += 1
            P.wait("tensor", "sc", psmm_free[ms])
            for ech in range(16):
                v = P.ins("tensor", "matmul", ps_mm[ms][:, 0:256],
                          xT[:, ech, tt * 128:(tt + 1) * 128],
                          wqvT[:, ech, :].rearrange("p a -> p a"),
                          start=(ech == 0), stop=(ech == 15),
                          inc=("tq", 1) if ech == 15 else None)
            P.wait("scalar", "tq", v)
            P.wait("scalar", qstage_free_sem[ms], qstage_free[ms])
            scv = P.ins("scalar", "activation", out=qstage[ms][:, 0:256],
                        in_=ps_mm[ms][:, 0:256],
                        func=AF.Copy, bias=0.0, scale=1.0, inc=("sc", 1))
            psmm_free[ms] = scv
            P.wait("gpsimd", "sc", scv)
            qstage_free[ms] = P.ins("gpsimd", "dma_start",
                                    out=agin_v[tt * 128:(tt + 1) * 128,
                                               vg * 256:(vg + 1) * 256],
                                    in_=qstage[ms][:, 0:256], inc=(f"gk{ms}", 16))
            qstage_free_sem[ms] = f"gk{ms}"
    P.wait("gpsimd", "gk0", P.val("gk0"))
    P.wait("gpsimd", "gk1", P.val("gk1"))
    P.ins("gpsimd", "collective_compute",
          "AllGather", ALU.bypass,
          replica_groups=[[0, 1, 2, 3], [4, 5, 6, 7]],
          ins=[agin.opt()], outs=[agout.opt()], inc=("cc", 1))
    for ft in range(16):
        emit_kq_tile(ft)
    q_sc_done = (P.val("gq0"), P.val("gq1"))
    tq_c_done = P.val("tq")
    phx.close()
    P.barrier()

    wtile = [ph0.enter_context(nc.sbuf_tensor(f"wtile{i}", [128, E], F32)) for i in range(2)]
    acc = [ph0.enter_context(nc.sbuf_tensor(f"acc{i}", [128, 1], F32)) for i in range(2)]
    tmp1 = ph0.enter_context(nc.sbuf_tensor("tmp1", [128, 1], F32))
    tot = ph0.enter_context(nc.sbuf_tensor("tot", [1, 2], F32))
    pairq = ph0.enter_context(nc.sbuf_tensor("pairq", [1, 2], F32))
    uq2 = [ph0.enter_context(nc.sbuf_tensor(f"uq{i}", [128, E], F32)) for i in range(2)]
    v1q = ph0.enter_context(nc.sbuf_tensor("v1q", [128, E], F32))
    tq_ = [ph0.enter_context(nc.sbuf_tensor(f"tq{i}", [128, E], BF16)) for i in range(2)]
    tstage = [ph0.enter_context(nc.sbuf_tensor(f"tstage{i}", [128, 512], BF16)) for i in range(2)]
    swmu = ph0.enter_context(nc.sbuf_tensor("swmu", [128, 2], F32))

    first_c2 = [True]
    pstpb_free = [0, 0]
    wtile_free = [0, 0]
    wtile_free_sem = ["ve", "ve"]
    tstage_free = [0, 0]
    tqbuf_free = [0, 0]

    def quant_matrix(mi, src_d, dst_sc, kind):
        ntile = 64

        def tile_src(t):
            if kind == "gu":
                return src_d[t * 128:(t + 1) * 128, :]
            ftl, ip = t // 4, t % 4
            return src_d[ftl * 128:(ftl + 1) * 128, ip * E:(ip + 1) * E]

        for t in range(ntile):
            sl = t % 2
            if first_c2[0]:
                P.wait("sync", "tq", tq_c_done)
                first_c2[0] = False
            P.wait("sync", wtile_free_sem[sl], wtile_free[sl])
            sdv = P.ins("sync", "dma_start", out=wtile[sl][:], in_=tile_src(t),
                        inc=(f"sw{sl}", 16))
            P.wait("vector", f"sw{sl}", sdv)
            vv = P.ins("vector", "tensor_reduce", out=tmp1[:], in_=wtile[sl][:],
                       axis=AX.X, op=ALU.add, apply_absolute_value=True, inc=("ve", 1))
            wtile_free[sl] = vv
            wtile_free_sem[sl] = "ve"
            P.ins("vector", "drain")
            if debug and mi == 0:
                P.wait("gpsimd", "ve", vv)
                gdbg = P.ins("gpsimd", "dma_start", out=dbg_acc[:, t:t + 1], in_=tmp1[:],
                             inc=("gd", 16), allow_slow_non_contiguous=True)
                P.wait("vector", "gd", gdbg)
            if t == 0:
                P.ins("vector", "tensor_copy", acc[0][:], tmp1[:])
            else:
                P.ins("vector", "tensor_tensor", out=acc[t % 2][:], in0=acc[(t - 1) % 2][:],
                      in1=tmp1[:], op=ALU.add)
            P.ins("vector", "drain")
        vv = P.bump("vector", "ve", 1)
        if debug and mi == 0:
            P.wait("gpsimd", "ve", vv)
            P.ins("gpsimd", "dma_start", out=dbg_ffn[:, 36:37], in_=acc[(ntile - 1) % 2][:],
                  inc=("gd", 16), allow_slow_non_contiguous=True)
        P.wait("tensor", "ve", vv)
        txv = P.ins("tensor", "matmul", ps_sm[0:1, 0:1], acc[(ntile - 1) % 2][:], ones_col[:],
                    start=True, stop=True, inc=("tq", 1))
        P.wait("scalar", "tq", txv)
        scv = P.ins("scalar", "activation", out=tot[:, 0:1], in_=ps_sm[0:1, 0:1],
                    func=AF.Copy, bias=0.0, scale=1.0, inc=("sc", 1))
        if debug and mi == 0:
            P.wait("gpsimd", "sc", scv)
            P.ins("gpsimd", "dma_start", out=dbg_ffn[0:1, 38:39], in_=tot[:, 0:1],
                  inc=("gd", 16), allow_slow_non_contiguous=True)
        P.wait("vector", "sc", scv)
        P.ins("vector", "tensor_scalar", out=tot[:, 1:2], in0=tot[:, 0:1],
              scalar1=P2M24, scalar2=EPS, op0=ALU.mult, op1=ALU.max)
        P.ins("vector", "drain")
        P.ins("vector", "reciprocal", out=pairq[:, 0:1], in_=tot[:, 1:2])
        P.ins("vector", "drain")
        vv = P.ins("vector", "tensor_copy", pairq[:, 1:2], tot[:, 1:2], inc=("ve", 1))
        P.wait("tensor", "ve", vv)
        txv = P.ins("tensor", "matmul", ps_sm[:, 0:2], ones_row[:], pairq[:],
                    start=True, stop=True, inc=("tq", 1))
        P.wait("scalar", "tq", txv)
        swmu_ready = P.ins("scalar", "activation", out=swmu[:], in_=ps_sm[:, 0:2],
                           func=AF.Copy, bias=0.0, scale=1.0, inc=("sc", 1))
        P.ins("scalar", "drain")
        P.wait("vector", "sc", swmu_ready)
        P.ins("vector", "tensor_copy", mu_bc[:, mi:mi + 1], swmu[:, 1:2])

        for t in range(ntile):
            sl = t % 2
            P.wait("sync", wtile_free_sem[sl], wtile_free[sl])
            sdv = P.ins("sync", "dma_start", out=wtile[sl][:], in_=tile_src(t),
                        inc=(f"sw{sl}", 16))
            P.wait("scalar", f"sw{sl}", sdv)
            scv = P.ins("scalar", "activation", out=uq2[sl][:], in_=wtile[sl][:],
                        func=AF.Copy, bias=MAGIC, scale=swmu[:, 0:1], inc=("sc", 1))
            wtile_free[sl] = scv
            wtile_free_sem[sl] = "sc"
            P.wait("vector", "sc", scv)
            P.ins("vector", "tensor_scalar", out=v1q[:], in0=uq2[sl][:],
                  scalar1=-MAGIC, scalar2=-1.0, op0=ALU.add, op1=ALU.max)
            P.wait("vector", "tq", tqbuf_free[sl])
            vv = P.ins("vector", "tensor_scalar", out=tq_[sl][:], in0=v1q[:],
                       scalar1=1.0, scalar2=None, op0=ALU.min, inc=("ve", 1))
            P.wait("tensor", "ve", vv)
            for g in range(4):
                gs = (t * 4 + g) % 2
                P.wait("tensor", "sc", pstpb_free[gs])
                for j in range(4):
                    ech = 4 * g + j
                    v = P.ins("tensor", "transpose", ps_tpb[gs][:, j * 128:(j + 1) * 128],
                              tq_[sl][:, ech * 128:(ech + 1) * 128], ident16[:],
                              inc=("tq", 1) if j == 3 else None)
                P.wait("scalar", "tq", v)
                P.wait("scalar", f"gt{gs}", tstage_free[gs])
                scv = P.ins("scalar", "activation", out=tstage[gs][:],
                            in_=ps_tpb[gs][:], func=AF.Copy, bias=0.0, scale=1.0,
                            inc=("sc", 1))
                pstpb_free[gs] = scv
                P.wait("gpsimd", "sc", scv)
                if kind == "gu":
                    dst = dst_sc[t // 4, t % 4, 4 * g:4 * g + 4].rearrange("a p t -> p a t")
                else:
                    ftl, ip = t // 4, t % 4
                    dst = dst_sc[ftl, ip * 16 + 4 * g: ip * 16 + 4 * g + 4].rearrange("a p t -> p a t")
                tstage_free[gs] = P.ins("gpsimd", "dma_start",
                                        out=dst,
                                        in_=tstage[gs][:].rearrange("p (a b) -> p a b", a=4),
                                        inc=(f"gt{gs}", 16))
            tqbuf_free[sl] = P.val("tq")

    quant_matrix(0, wg_d, wgt_sc, "gu")
    quant_matrix(1, wu_d, wut_sc, "gu")
    quant_matrix(2, wd_d, wdt_sc, "d")

    ph0.close()
    P.barrier()

    ph = contextlib.ExitStack()
    Kh = [ph.enter_context(nc.sbuf_tensor(f"Kh{i}", [128, SEQ], F32)) for i in range(2)]
    Vtok = [ph.enter_context(nc.sbuf_tensor(f"Vtok{i}", [128, NKB, 128], F32)) for i in range(2)]
    Qt = [ph.enter_context(nc.sbuf_tensor(f"Qt{i}", [128, 128], F32)) for i in range(2)]
    Pbuf0 = ph.enter_context(nc.sbuf_tensor("Pbuf0", [128, SEQ], F32))
    Pbuf1 = ph.enter_context(nc.sbuf_tensor("Pbuf1", [128, 3 * T], F32))
    P2 = [ph.enter_context(nc.sbuf_tensor(f"P2{i}", [128, T], F32)) for i in range(2)]

    def pq_ap(gq2, q):
        if q == 3:
            return Pbuf0[:, 3 * T:4 * T]
        return (Pbuf0 if gq2 % 2 == 0 else Pbuf1)[:, q * T:(q + 1) * T]
    PT = [ph.enter_context(nc.sbuf_tensor(f"PT{i}", [128, 512], F32)) for i in range(2)]
    attst0 = ph.enter_context(nc.sbuf_tensor("attst0", [128, 128], F32))
    attst = [attst0, attst0]
    mtab = ph.enter_context(nc.sbuf_tensor("mtab", [128, 64], F32))
    ttab = ph.enter_context(nc.sbuf_tensor("ttab", [128, 64], F32))
    wtab = ph.enter_context(nc.sbuf_tensor("wtab", [128, 64], F32))
    e2t = ph.enter_context(nc.sbuf_tensor("e2t", [128, 64], F32))
    d2t = mtab
    Mv = ph.enter_context(nc.sbuf_tensor("Mv", [128, 1], F32))
    Dv = ph.enter_context(nc.sbuf_tensor("Dv", [128, 1], F32))
    Dinv = ph.enter_context(nc.sbuf_tensor("Dinv", [128, 1], F32))
    ps_s = [ph.enter_context(nc.psum_tensor(f"ps_s{i}", [128, 512], F32)) for i in range(4)]
    ps_pt = [ph.enter_context(nc.psum_tensor(f"ps_pt{i}", [128, 512], F32)) for i in range(2)]
    ps_pv = [ph.enter_context(nc.psum_tensor(f"ps_pv{i}", [128, 128], F32)) for i in range(2)]

    kh_free = [0, 0]
    kh_ready = [0, 0]
    vt_free = [0, 0]
    vt_ready = [0, 0]
    qt_ready = [0, 0]
    qt_freed = [0, 0]
    pss_free = [0, 0, 0, 0]
    pspt_free = [("sc", 0), ("sc", 0)]
    ptbuf_free = [0, 0]
    pspv_free = [0, 0]
    attst_free = [0, 0]

    agout_v = [agout[r, 1].rearrange("h d t -> (h d) t") for r in range(NRANK)]

    def kv_loads(h):
        sl = h % 2
        P.wait("sync", "cc", 1)
        P.wait("sync", "tq", kh_free[sl])
        for r in range(NRANK):
            v = P.ins("sync", "dma_start", out=Kh[sl][:, r * T:(r + 1) * T],
                      in_=agout[r, 0, h], inc=(f"sk{sl}", 16))
        kh_ready[sl] = v
        P.wait("sync", "tv", vt_free[sl])
        for r in range(NRANK):
            for b4 in range(4):
                v = P.ins("sync", "dma_start",
                          out=Vtok[sl][:, r * 16 + b4 * 4: r * 16 + b4 * 4 + 4, :],
                          in_=agout_v[r][b4 * 512:(b4 + 1) * 512,
                                         h * 128:(h + 1) * 128]
                          .rearrange("(kb p) d -> p kb d", p=128),
                          inc=(f"sv{sl}", 16))
        vt_ready[sl] = v

    def q_load(gq2):
        h2, qt2 = gq2 // NTT, gq2 % NTT
        sl2 = gq2 % 2
        P.wait("sync", "gq0", q_sc_done[0])
        P.wait("sync", "gq1", q_sc_done[1])
        P.wait("sync", "tq", qt_freed[sl2])
        qt_ready[sl2] = P.ins("sync", "dma_start", out=Qt[sl2][:],
                              in_=q_sc[h2, :, qt2 * 128:(qt2 + 1) * 128],
                              inc=(f"sq{sl2}", 16))

    def emit_qkt(h, qt, gq):
        sl = h % 2
        P.wait("tensor", f"sk{sl}", kh_ready[sl])
        P.wait("tensor", f"sq{gq % 2}", qt_ready[gq % 2])
        for ch in range(NCH):
            cs = (gq * 16 + ch) % 4
            P.wait("tensor", "sc", pss_free[cs])
            tqv = P.ins("tensor", "matmul", ps_s[cs][:],
                        Qt[gq % 2][:, :],
                        Kh[sl][:, ch * 512:(ch + 1) * 512],
                        start=True, stop=True, inc=("tq", 1))
            P.wait("vector", "tq", tqv)
            vev = P.ins("vector", "tensor_reduce", out=mtab[:, ch * 4:ch * 4 + 4],
                        in_=ps_s[cs][:].rearrange("p (a b) -> p a b", a=4),
                        axis=AX.X, op=ALU.max, negate=True, inc=("ve", 1))
            P.wait("scalar", "ve", vev)
            if ch % 4 == 0:
                q4 = ch // 4
                tgt = 4 * gq
                if tgt >= 1 and gq >= 1:
                    P.wait("scalar", "vsl", tgt)
            pqv = pq_ap(gq, ch // 4)
            for j in range(4):
                scv = P.ins("scalar", "activation",
                            out=pqv[:, (ch % 4) * 512 + j * 128: (ch % 4) * 512 + (j + 1) * 128],
                            in_=ps_s[cs][:, j * 128:(j + 1) * 128],
                            func=AF.Exp, bias=mtab[:, ch * 4 + j: ch * 4 + j + 1],
                            scale=1.0,
                            accum_out=ttab[:, ch * 4 + j: ch * 4 + j + 1],
                            inc=("sc", 1) if j == 3 else None)
            pss_free[cs] = scv
        return P.val("sc")

    def emit_tables(h, qt):
        P.ins("vector", "drain")
        P.ins("vector", "tensor_reduce", out=Mv[:], in_=mtab[:], axis=AX.X,
              op=ALU.min, negate=True)
        P.ins("vector", "drain")
        vev = P.ins("vector", "tensor_scalar", out=e2t[:], in0=mtab[:],
                    scalar1=Mv[:], scalar2=-1.0, op0=ALU.add, op1=ALU.mult,
                    inc=("ve", 1))
        P.ins("vector", "drain")
        P.wait("scalar", "ve", vev)
        scv = P.ins("scalar", "activation", out=d2t[:], in_=e2t[:], func=AF.Exp,
                    bias=0.0, scale=1.0, inc=("sc", 1))
        P.wait("vector", "sc", scv)
        P.ins("vector", "tensor_tensor", out=wtab[:], in0=ttab[:], in1=d2t[:],
              op=ALU.mult)
        P.ins("vector", "drain")
        P.ins("vector", "tensor_reduce", out=Dv[:], in_=wtab[:], axis=AX.X, op=ALU.add)
        P.ins("vector", "drain")
        P.ins("vector", "reciprocal", out=Dinv[:], in_=Dv[:])
        P.ins("vector", "drain")
        P.ins("vector", "tensor_scalar", out=e2t[:], in0=wtab[:], scalar1=Dinv[:],
              scalar2=None, op0=ALU.mult)
        P.ins("vector", "drain")
        if debug and h == 0 and qt == 9:
            vv = P.bump("vector", "ve", 1)
            P.wait("gpsimd", "ve", vv)
            for src, off, w in ((e2t, 0, 64), (d2t, 64, 64), (ttab, 128, 64),
                               (wtab, 192, 64), (Mv, 256, 1), (Dv, 257, 1),
                               (Dinv, 258, 1)):
                gdbg = P.ins("gpsimd", "dma_start",
                             out=dbg_tab[:, off:off + w], in_=src[:],
                             inc=("gd", 16), allow_slow_non_contiguous=True)
            P.wait("vector", "gd", gdbg)

    def emit_ptpv_te(h, qt, gq, exps_done):
        sl = h % 2
        for r in range(4):
            kq = 4 * gq + r
            slot = kq % 2
            P.wait("vector", "sc", exps_done)
            if kq >= 2:
                P.wait("vector", "tpq", kq - 1)
            vslv = P.ins("vector", "tensor_tensor",
                  out=P2[slot][:].rearrange("p (c j) -> p c j", c=16),
                  in0=pq_ap(gq, r).rearrange("p (c j) -> p c j", c=16),
                  in1=e2t[:, r * 16:(r + 1) * 16].unsqueeze(2).broadcast_to([128, 16, 128]),
                  op=ALU.mult, inc=("vsl", 1))
            if debug and h == 0 and qt == 9:
                P.wait("gpsimd", "vsl", vslv)
                P.ins("gpsimd", "dma_start", out=dbg_p[:, r * T:(r + 1) * T],
                      in_=P2[slot][:], inc=("gd", 16))

            def emit_tp(g, kq=kq, slot=slot, gq=gq, r=r):
                gg = (gq * 16 + r * 4 + g) % 2
                P.wait("tensor", "vsl", kq + 1)
                P.wait("tensor", pspt_free[gg][0], pspt_free[gg][1])
                for j in range(4):
                    col = 4 * g + j
                    v = P.ins("tensor", "transpose", ps_pt[gg][:, j * 128:(j + 1) * 128],
                              P2[slot][:, col * 128:(col + 1) * 128], ident32[:],
                              inc=("tp", 1) if j == 3 else None)
                if g == 3:
                    P.wait("tensor", "tp", v)
                    P.bump("tensor", "tpq", 1)
                on_scalar = True
                eng = "scalar" if on_scalar else "vector"
                P.wait(eng, "tp", v)
                P.wait(eng, "tv", ptbuf_free[gg])
                if on_scalar:
                    pspt_free[gg] = ("sc", P.ins("scalar", "activation", out=PT[gg][:],
                                                 in_=ps_pt[gg][:], func=AF.Copy, bias=0.0,
                                                 scale=1.0, inc=("sc", 1)))
                else:
                    pspt_free[gg] = ("ve", P.ins("vector", "tensor_copy", PT[gg][:],
                                                 ps_pt[gg][:], inc=("ve", 1)))

            def emit_pv(g, gq=gq, r=r, sl=sl):
                gg = (gq * 16 + r * 4 + g) % 2
                jbase = r * 16 + 4 * g
                P.wait("tensor", pspt_free[gg][0], pspt_free[gg][1])
                if jbase == 0:
                    P.wait("tensor", "sc", pspv_free[gq % 2])
                    P.wait("tensor", f"sv{sl}", vt_ready[sl])
                for j in range(4):
                    kb = jbase + j
                    v = P.ins("tensor", "matmul", ps_pv[gq % 2][:],
                              Vtok[sl][:, kb, :], PT[gg][:, j * 128:(j + 1) * 128],
                              start=(kb == 0), stop=(kb == 63),
                              inc=("tv", 1) if j == 3 else None)
                ptbuf_free[gg] = v

            emit_tp(0)
            emit_tp(1)
            emit_pv(0)
            emit_tp(2)
            emit_pv(1)
            emit_tp(3)
            emit_pv(2)
            emit_pv(3)
        pv_done = P.val("tv")
        if qt == NTT - 1:
            vt_free[sl] = pv_done
        P.wait("scalar", "tv", pv_done)
        P.wait("scalar", "ga0", attst_free[0])
        scv = P.ins("scalar", "activation", out=attst[0][:], in_=ps_pv[gq % 2][:],
                    func=AF.Copy, bias=0.0, scale=1.0, inc=("sc", 1))
        pspv_free[gq % 2] = scv
        P.wait("gpsimd", "sc", scv)
        attst_free[0] = P.ins("gpsimd", "dma_start", out=ao_sc[qt, h],
                              in_=attst[0][:], inc=("ga0", 16))

    kv_loads(0)
    kv_loads(1)
    q_load(0)
    q_load(1)
    prev = None
    gq = 0
    for h in range(H):
        for qt in range(NTT):
            if prev is not None:
                emit_ptpv_te(*prev)
            if qt == 0 and h >= 1 and h + 1 < H:
                kv_loads(h + 1)
            ed = emit_qkt(h, qt, gq)
            qt_freed[gq % 2] = P.val("tq")
            if gq + 2 < H * NTT:
                q_load(gq + 2)
            if qt == NTT - 1:
                kh_free[h % 2] = P.val("tq")
            emit_tables(h, qt)
            prev = (h, qt, gq, ed)
            gq += 1
    emit_ptpv_te(*prev)
    ao_done = (P.val("ga0"), P.val("ga1"))

    ph.close()
    P.barrier()

    ph = contextlib.ExitStack()
    woutT = ph.enter_context(nc.sbuf_tensor("woutT", [128, 16, E], F32))
    woraw = [ph.enter_context(nc.sbuf_tensor(f"woraw{i}", [128, E], F32)) for i in range(2)]
    aoblk = ph.enter_context(nc.sbuf_tensor("aoblk", [128, 16, 128], F32))
    xtt = ph.enter_context(nc.sbuf_tensor("xtt", [128, E], F32))
    borow = ph.enter_context(nc.sbuf_tensor("borow", [1, E], F32))
    Btile = ph.enter_context(nc.sbuf_tensor("Btile", [128, E], F32))
    t_a = ph.enter_context(nc.sbuf_tensor("t_a", [128, E], F32))
    s1 = ph.enter_context(nc.sbuf_tensor("s1", [128, 1], F32))
    meanv = ph.enter_context(nc.sbuf_tensor("meanv", [128, 1], F32))
    ssq = ph.enter_context(nc.sbuf_tensor("ssq", [128, 1], F32))
    rstd = ph.enter_context(nc.sbuf_tensor("rstd", [128, 1], F32))
    x1st = [ph.enter_context(nc.sbuf_tensor(f"x1st{i}", [128, E], F32)) for i in range(2)]

    php = contextlib.ExitStack()
    ps_tp4 = [php.enter_context(nc.psum_tensor(f"ps_tp4{i}", [128, 512], F32)) for i in range(2)]

    sdv = P.ins("sync", "dma_start", out=borow[:], in_=bout_d.rearrange("(a b) -> a b", a=1),
                inc=("sb", 16))
    P.wait("tensor", "sb", sdv)
    pstp_free4 = [0, 0]
    for fc in range(4):
        gs = fc % 2
        P.wait("tensor", "sc", pstp_free4[gs])
        txv = P.ins("tensor", "matmul", ps_tp4[gs][:], ones_row[:],
                    borow[:, fc * 512:(fc + 1) * 512], start=True, stop=True, inc=("tq", 1))
        P.wait("scalar", "tq", txv)
        pstp_free4[gs] = P.ins("scalar", "activation", out=Btile[:, fc * 512:(fc + 1) * 512],
                               in_=ps_tp4[gs][:], func=AF.Copy, bias=0.0, scale=1.0,
                               inc=("sc", 1))
    woraw_free = [0, 0]
    for ft in range(16):
        sl = ft % 2
        P.wait("sync", "tq", woraw_free[sl])
        sdv = P.ins("sync", "dma_start", out=woraw[sl][:],
                    in_=wout_d[ft * 128:(ft + 1) * 128, :], inc=(f"sw{sl}", 16))
        P.wait("tensor", f"sw{sl}", sdv)
        for g in range(4):
            gs = g % 2
            P.wait("tensor", "sc", pstp_free4[gs])
            for j in range(4):
                ech = 4 * g + j
                v = P.ins("tensor", "transpose", ps_tp4[gs][:, j * 128:(j + 1) * 128],
                          woraw[sl][:, ech * 128:(ech + 1) * 128], ident32[:],
                          inc=("tq", 1) if j == 3 else None)
            P.wait("scalar", "tq", v)
            pstp_free4[gs] = P.ins("scalar", "activation",
                                   out=woutT[:, 4 * g:4 * g + 4, ft * 128:(ft + 1) * 128],
                                   in_=ps_tp4[gs][:].rearrange("p (a b) -> p a b", a=4),
                                   func=AF.Copy, bias=0.0, scale=1.0, inc=("sc", 1))
        woraw_free[sl] = P.val("tq")
    woutT_ready = P.val("sc")
    tq_prep_done = P.val("tq")
    php.close()

    ps_o = [ph.enter_context(nc.psum_tensor(f"ps_o{i}", [128, E], F32)) for i in range(2)]
    aoblk2 = [aoblk[:], woraw[0][:].rearrange("p (h t) -> p h t", h=16)]
    xtt2 = [xtt, woraw[1]]
    aoblk_free = [0, 0]
    xtt_free = [0, 0]
    pso_free = [0, 0]
    x1st_free = [0, 0]
    P.wait("tensor", "sc", woutT_ready)
    for tt in range(NTT):
        sl = tt % 2
        P.wait("sync", "ga0", ao_done[0])
        P.wait("sync", "ga1", ao_done[1])
        if tt < 2:
            P.wait("sync", "tq", tq_prep_done)
        P.wait("sync", "tq", aoblk_free[sl])
        aoblk_ready = P.ins("sync", "dma_start", out=aoblk2[sl],
                            in_=ao_sc[tt].rearrange("h p t -> p h t"), inc=(f"sa{sl}", 16))
        P.wait("sync", "ve", xtt_free[sl])
        xtt_ready = P.ins("sync", "dma_start", out=xtt2[sl][:],
                          in_=x_d[tt * 128:(tt + 1) * 128, :], inc=(f"sxt{sl}", 16))
        P.wait("tensor", f"sa{sl}", aoblk_ready)
        P.wait("tensor", "sc", pso_free[sl])
        if tt < 2:
            P.wait("tensor", "tq", tq_prep_done)
        for ech in range(16):
            for fc in range(4):
                v = P.ins("tensor", "matmul", ps_o[sl][:, fc * 512:(fc + 1) * 512],
                          aoblk2[sl][:, ech, :], woutT[:, ech, fc * 512:(fc + 1) * 512],
                          start=(ech == 0), stop=(ech == 15),
                          inc=("tq", 1) if (ech == 15 and fc == 3) else None)
        aoblk_free[sl] = v
        P.wait("vector", "tq", v)
        P.wait("vector", f"sxt{sl}", xtt_ready)
        P.wait("vector", "sc", woutT_ready)
        P.wait("vector", f"gx{sl}", x1st_free[sl])
        P.ins("vector", "tensor_tensor", out=t_a[:], in0=ps_o[sl][:], in1=Btile[:], op=ALU.add)
        vv = P.ins("vector", "tensor_tensor", out=x1st[sl][:], in0=t_a[:], in1=xtt2[sl][:],
                   op=ALU.add, inc=("ve", 1))
        xtt_free[sl] = vv
        P.ins("vector", "tensor_reduce", out=s1[:], in_=x1st[sl][:], axis=AX.X, op=ALU.add)
        P.ins("vector", "drain")
        P.ins("vector", "tensor_scalar", out=meanv[:], in0=s1[:], scalar1=INV2048,
              scalar2=None, op0=ALU.mult)
        P.ins("vector", "drain")
        vv = P.ins("vector", "tensor_scalar", out=t_a[:], in0=x1st[sl][:], scalar1=meanv[:],
                   scalar2=None, op0=ALU.subtract, inc=("ve", 1))
        P.wait("scalar", "ve", vv)
        pso_free[sl] = P.ins("scalar", "activation", out=ps_o[sl][:], in_=t_a[:],
                             func=AF.Square, bias=0.0, scale=1.0, accum_out=ssq[:],
                             inc=("sc", 1))
        P.ins("scalar", "drain")
        scv = P.ins("scalar", "activation", out=rstd[:], in_=ssq[:], func=AF.Sqrt,
                    bias=eps_col[:], scale=INV2048, inc=("sc", 1))
        P.wait("vector", "sc", scv)
        P.ins("vector", "reciprocal", out=meanv[:], in_=rstd[:])
        P.ins("vector", "drain")
        vv = P.ins("vector", "tensor_scalar", out=x1st[sl][:], in0=t_a[:], scalar1=meanv[:],
                   scalar2=None, op0=ALU.mult, inc=("ve", 1))
        P.wait("gpsimd", "ve", vv)
        x1st_free[sl] = P.ins("gpsimd", "dma_start", out=x1_sc[tt], in_=x1st[sl][:],
                              inc=(f"gx{sl}", 16))
    x1_done = (P.val("gx0"), P.val("gx1"))

    ph.close()
    P.barrier()

    ph = contextlib.ExitStack()
    kqT = ph.enter_context(nc.sbuf_tensor("kqT", [128, 16, NTT, 128], BF16))
    agtab = ph.enter_context(nc.sbuf_tensor("agtab", [128, NTT], F32))
    autab = ph.enter_context(nc.sbuf_tensor("autab", [128, NTT], F32))
    ident16 = ph.enter_context(nc.sbuf_tensor("ident16f1", [128, 128], BF16))
    P.ins("gpsimd", "memset", ident16[:], 0.0)
    idg = P.ins("gpsimd", "affine_select", out=ident16[:], in_=ident16[:],
                compare_op=ALU.not_equal, fill=1.0, base=0, pattern=[[-1, 128]],
                channel_multiplier=1, inc=("gp", 1))
    P.wait("tensor", "gp", idg)
    x1row = [ph.enter_context(nc.sbuf_tensor(f"x1row{i}", [128, E], F32)) for i in range(2)]
    uF = ph.enter_context(nc.sbuf_tensor("uF", [128, E], F32))
    v1F = ph.enter_context(nc.sbuf_tensor("v1F", [128, E], F32))
    kqF = ph.enter_context(nc.sbuf_tensor("kqF", [128, E], BF16))
    amax = ph.enter_context(nc.sbuf_tensor("amax", [128, 1], F32))
    clipm = ph.enter_context(nc.sbuf_tensor("clipm", [128, 1], F32))
    recipm = ph.enter_context(nc.sbuf_tensor("recipm", [128, 1], F32))
    sfull = ph.enter_context(nc.sbuf_tensor("sfull", [128, 1], F32))
    invs = ph.enter_context(nc.sbuf_tensor("invs", [128, 1], F32))
    wgch = [ph.enter_context(nc.sbuf_tensor(f"wgch{i}", [128, 4, 16, 128], BF16)) for i in range(2)]
    wuch = [ph.enter_context(nc.sbuf_tensor(f"wuch{i}", [128, 4, 16, 128], BF16)) for i in range(2)]
    zg = [ph.enter_context(nc.sbuf_tensor(f"zg{i}", [128, 512], F32)) for i in range(2)]
    zst = [ph.enter_context(nc.sbuf_tensor(f"zst{i}", [128, 512], F32)) for i in range(2)]
    ps_g = [ph.enter_context(nc.psum_tensor(f"ps_g{i}", [128, 512], F32)) for i in range(2)]
    ps_u = [ph.enter_context(nc.psum_tensor(f"ps_u{i}", [128, 512], F32)) for i in range(2)]
    ps_tpf = [ph.enter_context(nc.psum_tensor(f"ps_tpf{i}", [128, 512], BF16)) for i in range(2)]

    x1row_free = [0, 0]
    pstpf_free = [0, 0]
    for tt in range(NTT):
        sl = tt % 2
        P.wait("sync", "gx0", x1_done[0])
        P.wait("sync", "gx1", x1_done[1])
        P.wait("sync", "sc", x1row_free[sl])
        sdv = P.ins("sync", "dma_start", out=x1row[sl][:], in_=x1_sc[tt],
                    inc=(f"sx{sl}", 16))
        P.wait("vector", f"sx{sl}", sdv)
        P.ins("vector", "tensor_reduce", out=amax[:], in_=x1row[sl][:], axis=AX.X,
              op=ALU.max, apply_absolute_value=True)
        P.ins("vector", "drain")
        P.ins("vector", "tensor_scalar", out=clipm[:], in0=amax[:], scalar1=EPS,
              scalar2=None, op0=ALU.max)
        P.ins("vector", "drain")
        P.ins("vector", "reciprocal", out=recipm[:], in_=clipm[:])
        P.ins("vector", "drain")
        P.ins("vector", "tensor_scalar", out=sfull[:], in0=recipm[:], scalar1=128.0,
              scalar2=None, op0=ALU.mult)
        P.ins("vector", "tensor_scalar", out=invs[:], in0=clipm[:], scalar1=P2M7,
              scalar2=None, op0=ALU.mult)
        P.ins("vector", "drain")
        P.ins("vector", "tensor_scalar", out=agtab[:, tt:tt + 1], in0=invs[:],
              scalar1=mu_bc[:, 0:1], scalar2=None, op0=ALU.mult)
        vev = P.ins("vector", "tensor_scalar", out=autab[:, tt:tt + 1], in0=invs[:],
                    scalar1=mu_bc[:, 1:2], scalar2=None, op0=ALU.mult, inc=("ve", 1))
        P.wait("scalar", "ve", vev)
        scv = P.ins("scalar", "activation", out=uF[:], in_=x1row[sl][:], func=AF.Copy,
                    bias=MAGIC, scale=sfull[:], inc=("sc", 1))
        x1row_free[sl] = scv
        P.wait("vector", "sc", scv)
        P.ins("vector", "tensor_scalar", out=v1F[:], in0=uF[:], scalar1=-MAGIC,
              scalar2=-128.0, op0=ALU.add, op1=ALU.max)
        vev = P.ins("vector", "tensor_scalar", out=kqF[:], in0=v1F[:], scalar1=127.0,
                    scalar2=None, op0=ALU.min, inc=("ve", 1))
        if debug and tt == 0:
            P.wait("gpsimd", "ve", vev)
            P.ins("gpsimd", "dma_start", out=dbg_kq, in_=kqF[:], inc=("gd", 16))
            P.wait("gpsimd", "gd", P.val("gd"))
            dbg_gate[0] = P.bump("gpsimd", "gp", 1)
            P.wait("scalar", "gp", dbg_gate[0])
        P.wait("tensor", "ve", vev)
        for g in range(4):
            gs = (tt * 4 + g) % 2
            P.wait("tensor", "sc", pstpf_free[gs])
            for j in range(4):
                ech = 4 * g + j
                v = P.ins("tensor", "transpose", ps_tpf[gs][:, j * 128:(j + 1) * 128],
                          kqF[:, ech * 128:(ech + 1) * 128], ident16[:],
                          inc=("tq", 1) if j == 3 else None)
            P.wait("scalar", "tq", v)
            pstpf_free[gs] = P.ins("scalar", "activation",
                                   out=kqT[:, 4 * g:4 * g + 4, tt, :],
                                   in_=ps_tpf[gs][:].rearrange("p (a b) -> p a b", a=4),
                                   func=AF.Copy, bias=0.0, scale=1.0, inc=("sc", 1))
    kqT_ready = P.val("sc")
    prep_ve = P.val("ve")
    if debug:
        P.wait("gpsimd", "ve", prep_ve)
        for off, tsr, w in ((0, agtab, 16), (16, autab, 16), (32, mu_bc, 4)):
            P.ins("gpsimd", "dma_start", out=dbg_ffn[:, off:off + w], in_=tsr[:],
                  inc=("gd", 16), allow_slow_non_contiguous=True)

    wgch_free = [0, 0]
    psg_free = [0, 0]
    psu_free = [0, 0]
    zg_free = [0, 0]
    zst_free = [0, 0]
    P.wait("tensor", "sc", kqT_ready)
    P.wait("scalar", "ve", prep_ve)
    for ic in range(16):
        cs = ic % 2
        P.wait("sync", "tq", wgch_free[cs])
        P.ins("sync", "dma_start", out=wgch[cs][:],
              in_=wgt_sc[ic].rearrange("a e p t -> p a e t"), inc=(f"sg{cs}", 16))
        sdv = P.ins("sync", "dma_start", out=wuch[cs][:],
                    in_=wut_sc[ic].rearrange("a e p t -> p a e t"), inc=(f"sg{cs}", 16))
        P.wait("tensor", f"sg{cs}", sdv)
        for tt in range(NTT):
            ts_ = tt % 2
            P.wait("tensor", "sc", psg_free[ts_])
            P.wait("tensor", "ve", psu_free[ts_])
            for ech in range(16):
                P.ins("tensor", "matmul", ps_g[ts_][:], kqT[:, ech, tt, :],
                      wgch[cs][:, :, ech, :], start=(ech == 0), stop=(ech == 15))
                v = P.ins("tensor", "matmul", ps_u[ts_][:], kqT[:, ech, tt, :],
                          wuch[cs][:, :, ech, :], start=(ech == 0), stop=(ech == 15),
                          inc=("tq", 1) if ech == 15 else None)
            P.wait("scalar", "tq", v)
            if debug and ic == 0 and tt == 0:
                P.wait("scalar", "tq", v)
                P.ins("scalar", "activation", out=uF[:, 0:512], in_=ps_g[ts_][:],
                      func=AF.Copy, bias=0.0, scale=1.0)
                dbv = P.ins("scalar", "activation", out=uF[:, 512:1024], in_=ps_u[ts_][:],
                            func=AF.Copy, bias=0.0, scale=1.0, inc=("sc", 1))
                P.wait("gpsimd", "sc", dbv)
                P.ins("gpsimd", "dma_start", out=dbg_g[:, 0:1024], in_=uF[:, 0:1024],
                      inc=("gd", 16), allow_slow_non_contiguous=True)
                P.wait("gpsimd", "gd", P.val("gd"))
                zgate = P.bump("gpsimd", "gp", 1)
                P.wait("scalar", "gp", zgate)
            P.wait("scalar", "ve", zg_free[ts_])
            scv = P.ins("scalar", "activation", out=zg[ts_][:], in_=ps_g[ts_][:],
                        func=AF.Silu, bias=0.0, scale=agtab[:, tt:tt + 1], inc=("sc", 1))
            psg_free[ts_] = scv
            if debug and ic == 0 and tt == 0:
                P.wait("gpsimd", "sc", scv)
                P.ins("gpsimd", "dma_start", out=dbg_g[:, 1024:1536], in_=zg[ts_][:],
                      inc=("gd", 16), allow_slow_non_contiguous=True)
                P.wait("gpsimd", "gd", P.val("gd"))
                zgate = P.bump("gpsimd", "gp", 1)
                P.wait("vector", "gp", zgate)
            P.wait("vector", "sc", scv)
            P.wait("vector", f"gz{ts_}", zst_free[ts_])
            vev = P.ins("vector", "scalar_tensor_tensor", out=zst[ts_][:], in0=ps_u[ts_][:],
                        scalar=autab[:, tt:tt + 1], in1=zg[ts_][:],
                        op0=ALU.mult, op1=ALU.mult, inc=("ve", 1))
            psu_free[ts_] = vev
            zg_free[ts_] = vev
            P.wait("gpsimd", "ve", vev)
            zst_free[ts_] = P.ins("gpsimd", "dma_start",
                                  out=z_sc[tt, :, ic * 512:(ic + 1) * 512],
                                  in_=zst[ts_][:], inc=(f"gz{ts_}", 16))
        wgch_free[cs] = P.val("tq")
    z_done = (P.val("gz0"), P.val("gz1"))

    ph.close()
    P.barrier()

    ph = contextlib.ExitStack()
    Zrow = ph.enter_context(nc.sbuf_tensor("Zrow", [128, NI], F32))
    Zb = ph.enter_context(nc.sbuf_tensor("Zb", [128, NI], F32))
    kq2 = ph.enter_context(nc.sbuf_tensor("kq2", [128, NI], BF16))
    kq2T = ph.enter_context(nc.sbuf_tensor("kq2T", [128, 64, 2, 128], BF16))
    wdch = [ph.enter_context(nc.sbuf_tensor(f"wdch{i}", [128, 64, 128], BF16)) for i in range(2)]
    dstage = [ph.enter_context(nc.sbuf_tensor(f"dst{i}", [128, 256], F32)) for i in range(2)]
    dtok = [ph.enter_context(nc.sbuf_tensor(f"dtok{i}", [128, E], F32)) for i in range(2)]
    x1g = ph.enter_context(nc.sbuf_tensor("x1g", [128, E], F32))
    outst = [ph.enter_context(nc.sbuf_tensor(f"outst{i}", [128, E], F32)) for i in range(2)]
    amax2 = ph.enter_context(nc.sbuf_tensor("amax2", [128, 1], F32))
    clipm2 = ph.enter_context(nc.sbuf_tensor("clipm2", [128, 1], F32))
    recip2 = ph.enter_context(nc.sbuf_tensor("recip2", [128, 1], F32))
    s2v = ph.enter_context(nc.sbuf_tensor("s2v", [128, 1], F32))
    gam = ph.enter_context(nc.sbuf_tensor("gam", [128, 1], F32))
    invs2 = ph.enter_context(nc.sbuf_tensor("invs2", [128, 1], F32))
    bdt = ph.enter_context(nc.sbuf_tensor("bdt", [128, 2], F32))
    s1b = ph.enter_context(nc.sbuf_tensor("s1b", [128, 1], F32))
    meanb = ph.enter_context(nc.sbuf_tensor("meanb", [128, 1], F32))
    ssqb = ph.enter_context(nc.sbuf_tensor("ssqb", [128, 1], F32))
    rstdb = ph.enter_context(nc.sbuf_tensor("rstdb", [128, 1], F32))
    t_a2 = ph.enter_context(nc.sbuf_tensor("t_a2", [128, E], F32))
    t_b2 = ph.enter_context(nc.sbuf_tensor("t_b2", [128, E], F32))
    ps_f = [ph.enter_context(nc.psum_tensor(f"ps_f{i}", [128, 256], F32)) for i in range(2)]
    ps_t2 = [ph.enter_context(nc.psum_tensor(f"ps_t2{i}", [128, 512], F32)) for i in range(2)]
    ps_tq2 = [ph.enter_context(nc.psum_tensor(f"ps_tq2{i}", [128, 512], BF16)) for i in range(2)]
    ident16 = ph.enter_context(nc.sbuf_tensor("ident16f2", [128, 128], BF16))
    P.ins("gpsimd", "memset", ident16[:], 0.0)
    idg2 = P.ins("gpsimd", "affine_select", out=ident16[:], in_=ident16[:],
                 compare_op=ALU.not_equal, fill=1.0, base=0, pattern=[[-1, 128]],
                 channel_multiplier=1, inc=("gp", 1))
    P.wait("tensor", "gp", idg2)

    zrow_free = 0
    zb_free = 0
    kq2_free = 0
    kq2T_free = 0
    pstq2_free = [0, 0]
    wdch_free = [0, 0]
    psf_free = [0, 0]
    dstage_free = [0, 0]
    pst2_free = [0, 0]
    dtok_free = [0, 0]
    x1g_free = 0
    outst_free = [0, 0]
    for grp in range(8):
        for t2 in range(2):
            tt = grp * 2 + t2
            P.wait("sync", "gz0", z_done[0])
            P.wait("sync", "gz1", z_done[1])
            P.wait("sync", "ve", zrow_free)
            sdv = P.ins("sync", "dma_start", out=Zrow[:], in_=z_sc[tt], inc=("sz", 16))
            P.wait("vector", "sz", sdv)
            P.wait("vector", "ve", zb_free)
            cur, oth = Zrow, Zb
            for st in range(13):
                hh = 1 << st
                cv = cur[:].rearrange("p (g two h) -> p g two h", two=2, h=hh)
                ov = oth[:].rearrange("p (g two h) -> p g two h", two=2, h=hh)
                P.ins("vector", "tensor_tensor", out=ov[:, :, 0, :], in0=cv[:, :, 0, :],
                      in1=cv[:, :, 1, :], op=ALU.add)
                P.ins("vector", "tensor_tensor", out=ov[:, :, 1, :], in0=cv[:, :, 0, :],
                      in1=cv[:, :, 1, :], op=ALU.subtract)
                cur, oth = oth, cur
            res = cur
            P.ins("vector", "tensor_reduce", out=amax2[:], in_=res[:], axis=AX.X,
                  op=ALU.max, apply_absolute_value=True)
            P.ins("vector", "drain")
            P.ins("vector", "tensor_scalar", out=clipm2[:], in0=amax2[:], scalar1=FWC,
                  scalar2=EPS, op0=ALU.mult, op1=ALU.max)
            P.ins("vector", "drain")
            P.ins("vector", "reciprocal", out=recip2[:], in_=clipm2[:])
            P.ins("vector", "drain")
            P.ins("vector", "tensor_scalar", out=s2v[:], in0=recip2[:], scalar1=128.0,
                  scalar2=None, op0=ALU.mult)
            P.ins("vector", "drain")
            P.ins("vector", "tensor_scalar", out=gam[:], in0=s2v[:], scalar1=FWC,
                  scalar2=None, op0=ALU.mult)
            P.ins("vector", "tensor_scalar", out=invs2[:], in0=clipm2[:], scalar1=P2M7,
                  scalar2=None, op0=ALU.mult)
            P.ins("vector", "drain")
            vev = P.ins("vector", "tensor_scalar", out=bdt[:, t2:t2 + 1], in0=invs2[:],
                        scalar1=mu_bc[:, 2:3], scalar2=None, op0=ALU.mult, inc=("ve", 1))
            if debug and tt == 0:
                P.wait("gpsimd", "ve", vev)
                P.ins("gpsimd", "dma_start", out=dbg_fw, in_=res[:], inc=("gd", 16))
                P.wait("gpsimd", "gd", P.val("gd"))
                f2g0 = P.bump("gpsimd", "gp", 1)
                P.wait("scalar", "gp", f2g0)
            P.wait("scalar", "ve", vev)
            scv = P.ins("scalar", "activation", out=Zrow[:], in_=res[:], func=AF.Copy,
                        bias=MAGIC, scale=gam[:], inc=("sc", 1))
            P.wait("vector", "sc", scv)
            vv = P.ins("vector", "tensor_scalar", out=Zb[:], in0=Zrow[:], scalar1=-MAGIC,
                       scalar2=-128.0, op0=ALU.add, op1=ALU.max, inc=("ve", 1))
            zrow_free = vv
            P.wait("vector", "tq", kq2_free)
            vev = P.ins("vector", "tensor_scalar", out=kq2[:], in0=Zb[:], scalar1=127.0,
                        scalar2=None, op0=ALU.min, inc=("ve", 1))
            zb_free = vev
            if debug and tt == 0:
                P.wait("gpsimd", "ve", vev)
                P.ins("gpsimd", "dma_start", out=dbg_k2, in_=kq2[:], inc=("gd", 16))
                P.wait("gpsimd", "gd", P.val("gd"))
                f2g = P.bump("gpsimd", "gp", 1)
                P.wait("tensor", "gp", f2g)
            P.wait("tensor", "ve", vev)
            P.wait("tensor", "tq", kq2T_free)
            for g in range(16):
                gs = (tt * 16 + g) % 2
                P.wait("tensor", "sc", pstq2_free[gs])
                for j in range(4):
                    ich = 4 * g + j
                    v = P.ins("tensor", "transpose", ps_tq2[gs][:, j * 128:(j + 1) * 128],
                              kq2[:, ich * 128:(ich + 1) * 128], ident16[:],
                              inc=("tq", 1) if j == 3 else None)
                P.wait("scalar", "tq", v)
                pstq2_free[gs] = P.ins("scalar", "activation",
                                       out=kq2T[:, 4 * g:4 * g + 4, t2, :],
                                       in_=ps_tq2[gs][:].rearrange("p (a b) -> p a b", a=4),
                                       func=AF.Copy, bias=0.0, scale=1.0, inc=("sc", 1))
            kq2_free = P.val("tq")
        kq2T_ready = P.val("sc")
        ve_q = P.val("ve")
        P.wait("tensor", "sc", kq2T_ready)
        P.wait("scalar", "ve", ve_q)
        for ft in range(16):
            fs = ft % 2
            P.wait("sync", "tq", wdch_free[fs])
            sdv = P.ins("sync", "dma_start", out=wdch[fs][:],
                        in_=wdt_sc[ft].rearrange("a p t -> p a t"), inc=(f"sg{fs}", 16))
            P.wait("tensor", f"sg{fs}", sdv)
            P.wait("tensor", "sc", psf_free[fs])
            for ich in range(64):
                v = P.ins("tensor", "matmul", ps_f[fs][:],
                          wdch[fs][:, ich, :], kq2T[:, ich, :, :].rearrange("p a t -> p (a t)"),
                          start=(ich == 0), stop=(ich == 63),
                          inc=("tq", 1) if ich == 63 else None)
            wdch_free[fs] = v
            P.wait("scalar", "tq", v)
            P.wait("scalar", "tq", dstage_free[fs])
            scv = P.ins("scalar", "activation", out=dstage[fs][:], in_=ps_f[fs][:],
                        func=AF.Copy, bias=0.0, scale=1.0, inc=("sc", 1))
            psf_free[fs] = scv
            P.wait("tensor", "sc", scv)
            P.wait("tensor", "sc", pst2_free[fs])
            for t2 in range(2):
                v = P.ins("tensor", "transpose", ps_t2[fs][:, t2 * 128:(t2 + 1) * 128],
                          dstage[fs][:, t2 * 128:(t2 + 1) * 128], ident32[:],
                          inc=("tq", 1) if t2 == 1 else None)
            dstage_free[fs] = v
            P.wait("scalar", "tq", v)
            for t2 in range(2):
                if ft == 0:
                    P.wait("scalar", "ve", dtok_free[t2])
                scv = P.ins("scalar", "activation",
                            out=dtok[t2][:, ft * 128:(ft + 1) * 128],
                            in_=ps_t2[fs][:, t2 * 128:(t2 + 1) * 128],
                            func=AF.Copy, bias=0.0, scale=bdt[:, t2:t2 + 1],
                            inc=("sc", 1))
            pst2_free[fs] = P.val("sc")
        down_done = P.val("sc")
        kq2T_free = P.val("tq")
        if debug and grp == 0:
            P.wait("gpsimd", "sc", down_done)
            P.ins("gpsimd", "dma_start", out=dbg_dt, in_=dtok[0][:], inc=("gd", 16))
            P.wait("gpsimd", "gd", P.val("gd"))
            f2g2 = P.bump("gpsimd", "gp", 1)
            P.wait("vector", "gp", f2g2)
        for t2 in range(2):
            tt = grp * 2 + t2
            P.wait("sync", "ve", x1g_free)
            sdv = P.ins("sync", "dma_start", out=x1g[:], in_=x1_sc[tt], inc=("sxt", 16))
            P.wait("vector", "sc", down_done)
            P.wait("vector", "sxt", sdv)
            vv = P.ins("vector", "tensor_tensor", out=t_b2[:], in0=dtok[t2][:],
                       in1=x1g[:], op=ALU.add, inc=("ve", 1))
            dtok_free[t2] = vv
            x1g_free = vv
            P.ins("vector", "tensor_reduce", out=s1b[:], in_=t_b2[:], axis=AX.X, op=ALU.add)
            P.ins("vector", "drain")
            P.ins("vector", "tensor_scalar", out=meanb[:], in0=s1b[:], scalar1=INV2048,
                  scalar2=None, op0=ALU.mult)
            P.ins("vector", "drain")
            vv = P.ins("vector", "tensor_scalar", out=t_a2[:], in0=t_b2[:], scalar1=meanb[:],
                       scalar2=None, op0=ALU.subtract, inc=("ve", 1))
            P.wait("scalar", "ve", vv)
            P.ins("scalar", "activation", out=t_b2[:], in_=t_a2[:], func=AF.Square,
                  bias=0.0, scale=1.0, accum_out=ssqb[:])
            P.ins("scalar", "drain")
            scv = P.ins("scalar", "activation", out=rstdb[:], in_=ssqb[:], func=AF.Sqrt,
                        bias=eps_col[:], scale=INV2048, inc=("sc", 1))
            P.wait("vector", "sc", scv)
            P.ins("vector", "reciprocal", out=meanb[:], in_=rstdb[:])
            P.ins("vector", "drain")
            P.wait("vector", f"so{t2}", outst_free[t2])
            vv = P.ins("vector", "tensor_scalar", out=outst[t2][:], in0=t_a2[:],
                       scalar1=meanb[:], scalar2=None, op0=ALU.mult, inc=("ve", 1))
            P.wait("gpsimd", "ve", vv)
            outst_free[t2] = P.ins("gpsimd", "dma_start",
                                   out=y_d[tt * 128:(tt + 1) * 128, :], in_=outst[t2][:],
                                   inc=(f"so{t2}", 16))

    P.wait("gpsimd", "so0", P.val("so0"))
    P.wait("gpsimd", "so1", P.val("so1"))

    with nc.Block() as block:
        for eng in ENGS:
            getattr(block, eng)(lambda e, ops=P.ops[eng]: [f(e) for f in ops])
    ph.close()
    es.close()
    return nc


def kernel(L0_attn_x, L0_attn_w_qkv, L0_attn_w_out, L0_attn_b_out,
           L0_ffn_w_gate, L0_ffn_w_up, L0_ffn_w_down):
    nc = build()
    x = np.ascontiguousarray(np.asarray(L0_attn_x, dtype=np.float32))
    wqkv = np.ascontiguousarray(np.asarray(L0_attn_w_qkv, dtype=np.float32))
    wout = np.ascontiguousarray(np.asarray(L0_attn_w_out, dtype=np.float32))
    bout = np.ascontiguousarray(np.asarray(L0_attn_b_out, dtype=np.float32))
    wg = np.ascontiguousarray(np.asarray(L0_ffn_w_gate, dtype=np.float32))
    wu = np.ascontiguousarray(np.asarray(L0_ffn_w_up, dtype=np.float32))
    wd = np.ascontiguousarray(np.asarray(L0_ffn_w_down, dtype=np.float32))
    in_maps = []
    for c in range(8):
        b, qr = c // 4, c % 4
        in_maps.append({
            "x": np.ascontiguousarray(x[b, qr * T:(qr + 1) * T, :]),
            "wqkv": wqkv, "wout": wout, "bout": bout,
            "wg": wg, "wu": wu, "wd": wd,
        })
    res = run_bass_kernel_spmd(nc, in_maps, core_ids=list(range(8)))
    out = np.empty((2, SEQ, E), np.float32)
    for c in range(8):
        b, qr = c // 4, c % 4
        out[b, qr * T:(qr + 1) * T, :] = res.results[c]["y"]
    return out
